# version 02
# baseline (speedup 1.0000x reference)
import contextlib
import numpy as np
import ml_dtypes
import concourse.bass as bass
import concourse.mybir as mybir
from concourse.bass_utils import run_bass_kernel_spmd

F32 = mybir.dt.float32
BF16 = mybir.dt.bfloat16
U32 = mybir.dt.uint32
I32 = mybir.dt.int32
U8 = mybir.dt.uint8
AF = mybir.ActivationFunctionType
ALU = mybir.AluOpType
BF = ml_dtypes.bfloat16

D = 4096
S = 2048
B = 4
H = 32
HD = 128
INTER = 11008
NGU = 2 * INTER
KB_GU = D // 128
KB_DN = INTER // 128
MC_GU = NGU // 512
MC_DN = D // 512
TOKQ = 1024
CH = 512
N_CORES = 8
EPS = 1e-5
SCALE = 1.0 / float(np.sqrt(np.float32(HD)))

DMA = 16
SZ = {F32: 4, U32: 4, I32: 4, BF16: 2, U8: 1}


def perm128():
    p = np.arange(128)
    return 8 * (p % 16) + p // 16


def full_perm(n):
    return (np.arange(0, n, 128)[:, None] + perm128()[None, :]).ravel()


def rope_perm():
    base = np.concatenate([np.arange(0, HD, 2), np.arange(1, HD, 2)])
    return (np.arange(0, D, HD)[:, None] + base[None, :]).ravel()


class NS:
    pass


def build_program(phases=("D", "P", "A", "W", "F"), debug=False):
    nc = bass.Bass("TRN2", target_bir_lowering=False)
    ctx = contextlib.ExitStack()

    def inp(name, shape, dtype):
        return nc.dram_tensor(name, list(shape), dtype, kind="ExternalInput").ap()

    t = NS()
    t.xT = inp("xT", [D, S], F32)
    t.x_own = inp("x_own", [D, TOKQ], F32)
    t.wq = inp("wq", [D, D], F32)
    t.wk = inp("wk", [D, D], F32)
    t.wv = inp("wv", [D, D], F32)
    t.wo = inp("wo", [D, D], F32)
    t.qw_gu = inp("qw_gu", [D // 8, NGU], U32)
    t.qz_gu = inp("qz_gu", [KB_GU, NGU // 8], U32)
    t.sc_gu = inp("sc_gu", [KB_GU, NGU], F32)
    t.qw_dn = inp("qw_dn", [INTER // 8, D], U32)
    t.qz_dn = inp("qz_dn", [KB_DN, D // 8], U32)
    t.sc_dn = inp("sc_dn", [KB_DN, D], F32)
    t.gi_bc_gu = inp("gi_bc_gu", [KB_GU, D], I32)
    t.gi_col_gu = inp("gi_col_gu", [128, KB_GU], I32)
    t.gi_bc_dn = inp("gi_bc_dn", [KB_DN, INTER], I32)
    t.gi_col_dn = inp("gi_col_dn", [128, KB_DN], I32)
    t.cos_k = inp("cos_k", [64, S], F32)
    t.sin_k = inp("sin_k", [64, S], F32)
    t.cos_q = inp("cos_q", [64, TOKQ], F32)
    t.sin_q = inp("sin_q", [64, TOKQ], F32)
    t.mask = inp("mask", [128, 16 * TOKQ], U8)
    t.ones_col = inp("ones_col", [128, 1], BF16)
    t.iota_row = inp("iota_row", [128, KB_DN], F32)
    t.iota_col = inp("iota_col", [KB_DN, 1], F32)
    t.mask_col = inp("mask_col", [128, 1], I32)
    t.pscale_col = inp("pscale_col", [128, 1], F32)

    t.outT = nc.dram_tensor("outT", [D, TOKQ], F32, kind="ExternalOutput").ap()

    dbg = "ExternalOutput" if debug else "Internal"
    t.wgu_c = nc.dram_tensor("wgu_c", [MC_GU, KB_GU, 128, 512], BF16, kind=dbg).ap()
    t.wdn_c = nc.dram_tensor("wdn_c", [MC_DN, KB_DN, 128, 512], BF16, kind=dbg).ap()
    t.negbg_gu = nc.dram_tensor("negbg_gu", [KB_GU, NGU], BF16, kind=dbg).ap()
    t.negbg_dn = nc.dram_tensor("negbg_dn", [KB_DN, D], BF16, kind=dbg).ap()
    if debug:
        t.gbf_dump = nc.dram_tensor("gbf_dump", [KB_DN, INTER], BF16, kind="ExternalOutput").ap()
        t.indT_dump = nc.dram_tensor("indT_dump", [KB_DN, 512], BF16, kind="ExternalOutput").ap()
        t.indgu_dump = nc.dram_tensor("indgu_dump", [128, KB_GU * 32], BF16, kind="ExternalOutput").ap()
        t.qt_dump = nc.dram_tensor("qt_dump", [128, 32 * 1024], BF16, kind="ExternalOutput").ap()
        t.attn_dump = nc.dram_tensor("attn_dump", [128, 32 * 1024], BF16, kind="ExternalOutput").ap()
    t.kT_sp = nc.dram_tensor("kT_sp", [D, S], BF16, kind=dbg).ap()
    t.v_sp = nc.dram_tensor("v_sp", [H, 16, 128, 128], BF16, kind=dbg).ap()
    t.xmid = nc.dram_tensor("xmid", [D, TOKQ], F32, kind="Internal").ap()
    t.t2_sp = nc.dram_tensor("t2_sp", [D, CH], F32, kind="Internal").ap()
    t.rows_sp = nc.dram_tensor("rows_sp", [16, TOKQ], F32, kind="Internal").ap()

    SBUF_BASE = 18432

    def sbuf(name, off, shape, dtype):
        nbytes = int(np.prod(shape[1:])) * SZ[dtype]
        assert SBUF_BASE + off + nbytes <= 229376, f"{name}: {off}+{nbytes} over budget"
        return nc.alloc_sbuf_tensor_at(name, list(shape), dtype, offset=SBUF_BASE + off)

    s = NS()
    s.ones = sbuf("s_ones", 0, [128, 1], BF16)
    s.iota_row = sbuf("s_iota_row", 64, [128, KB_DN], F32)
    s.iota_col = sbuf("s_iota_col", 448, [KB_DN, 1], F32)
    s.mask = sbuf("s_mask", 512, [128, 1], I32)
    s.pscale = sbuf("s_pscale", 3648, [128, 1], F32)
    s.gcolf_gu = sbuf("s_gcolf_gu", 576, [128, KB_GU], F32)
    s.gcolf_dn = sbuf("s_gcolf_dn", 704, [128, KB_DN], F32)
    s.gcol_gu_i = sbuf("s_gcol_gu_i", 1088, [128, KB_GU], I32)
    s.gcol_dn_i = sbuf("s_gcol_dn_i", 1216, [128, KB_DN], I32)
    s.ind_gu = sbuf("s_ind_gu", 1600, [128, KB_GU * 32], BF16)
    s.rows = sbuf("s_rows", 3712, [128, 1536], F32)
    CONST2 = 9856
    STAGE = 34432
    R2 = 55936
    R3 = 121472
    TOP = 188032
    LIM = 208 * 1024

    sems = NS()
    for n in ["d_ld", "d_l0", "d_pr", "d_ve", "d_mm", "d_st", "d_fr", "d_ix",
              "d_q0", "d_q1", "d_w0", "d_w1", "d_w2", "d_w3",
              "p_x0", "p_x1", "p_w0", "p_w1", "p_cs", "p_v0", "p_v1", "p_k0", "p_k1", "p_ch", "d_ch",
              "a_k0", "a_k1", "a_r0", "a_r1", "a_b0", "a_b1", "w_s0", "w_s1",
              "p_ld", "p_wl", "p_cv", "p_mm", "p_ve", "p_st", "p_fr", "p_rp",
              "a_ld", "a_mm", "a_ex", "a_ve", "a_sc", "a_st", "a_fr", "a_sm",
              "w_ld", "w_wl", "w_mm", "w_ve", "w_sc", "w_st", "w_rw",
              "f_ld", "f_wl", "f_mm", "f_ve", "f_sc", "f_st", "f_hs", "f_x", "f_s2",
              "f_q0", "f_q1", "f_q2", "f_n0", "f_n1", "f_m0", "f_m1", "f_t0", "f_t1",
              "f_r0", "f_r1", "f_o0", "f_o1", "f_ab",
              "bar"]:
        setattr(sems, n, ctx.enter_context(nc.semaphore(n)))

    pq = [ctx.enter_context(nc.psum_tensor(f"pq{i}", [128, 1024], F32)) for i in range(4)]

    d = NS()
    d.scf = sbuf("d_scf", STAGE, [KB_DN, 2048], F32)
    d.qzb = sbuf("d_qzb", STAGE + 8192, [KB_DN, 256], U32)
    d.zq = sbuf("d_zq", STAGE + 9216, [KB_DN, 2048], U32)
    d.nbg = sbuf("d_nbg", STAGE + 17408, [KB_DN, 2048], BF16)
    d.qrep = [sbuf(f"d_qrep{i}", R2 + i * 8192, [128, 2048], U32) for i in range(2)]
    d.scbf_gu = sbuf("d_scbf_gu", R2 + 16384, [KB_GU, NGU], BF16)
    d.scbf_dn = sbuf("d_scbf_dn", R3, [KB_DN, D], BF16)
    d.indT = sbuf("d_indT", R3 + 8192, [KB_DN, 128 * 4], BF16)
    d.nib = [sbuf(f"d_nib{i}", R3 + 9280 + i * 2048, [128, 512], U32) for i in range(2)]
    d.wt = [sbuf(f"d_wt{i}", R3 + 13376 + i * 1024, [128, 512], BF16) for i in range(4)]
    d.nsc = sbuf("d_nsc", R3 + 29760, [KB_DN, 2048], F32)
    d.nibf = [sbuf(f"d_nibf{i}", R3 + 17472 + i * 2048, [128, 512], F32) for i in range(2)]
    d.gist = [sbuf(f"d_gist{i}", R3 + 21568 + i * 8192, [KB_DN, 2048], I32) for i in range(2)]
    d.gbf_gu = sbuf("d_gbf_gu", R3 + 37952, [KB_GU, D], BF16)
    d.gbf_dn = sbuf("d_gbf_dn", R3 + 46144, [KB_DN, INTER], BF16)

    gu_chunks = [(i * 2048, 2048) for i in range(10)] + [(20480, 1536)]
    dn_chunks = [(0, 2048), (2048, 2048)]
    prep = [("gu", c0, w) for c0, w in gu_chunks] + [("dn", c0, w) for c0, w in dn_chunks]
    n_prep = len(prep)

    d_main = []
    gix = 0
    for G in range(KB_GU):
        for c0, w in gu_chunks:
            d_main.append(("gu", G, c0, w, gix))
        gix += 1
    for G in range(KB_DN):
        for c0, w in dn_chunks:
            d_main.append(("dn", G, c0, w, gix))
        gix += 1
    n_gix = gix
    ts_cum = []
    c = 0
    for (_, _, _, w, _) in d_main:
        c += w // 512
        ts_cum.append(c)
    d_tot_tt = c
    d_tot_ld = 2 * n_prep
    d_tot_st = n_prep
    g_last_tt = {}
    g_first_it = {}
    for i2, (_, _, _, w, g) in enumerate(d_main):
        g_last_tt[g] = ts_cum[i2]
        g_first_it.setdefault(g, i2)

    def d_sync(sync):
        sync.dma_start(out=s.ones[:], in_=t.ones_col[:]).then_inc(sems.d_l0, DMA)
        sync.dma_start(out=s.iota_row[:], in_=t.iota_row[:]).then_inc(sems.d_l0, DMA)
        sync.dma_start(out=s.iota_col[:], in_=t.iota_col[:]).then_inc(sems.d_l0, DMA)
        sync.dma_start(out=s.mask[:], in_=t.mask_col[:]).then_inc(sems.d_l0, DMA)
        sync.dma_start(out=s.pscale[:], in_=t.pscale_col[:]).then_inc(sems.d_l0, DMA)
        sync.dma_start(out=s.gcol_gu_i[:], in_=t.gi_col_gu[:]).then_inc(sems.d_l0, DMA)
        sync.dma_start(out=s.gcol_dn_i[:], in_=t.gi_col_dn[:]).then_inc(sems.d_l0, DMA)
        gchunks = [("gu", c) for c in range(0, D, 2048)] + [("dn", c) for c in range(0, INTER, 2048)]
        d_sync.gchunks = gchunks
        for gi, (kind, c0) in enumerate(gchunks):
            w2 = min(2048, (D if kind == "gu" else INTER) - c0)
            nb2 = KB_GU if kind == "gu" else KB_DN
            gsrc = t.gi_bc_gu if kind == "gu" else t.gi_bc_dn
            if gi >= 2:
                sync.wait_ge(sems.d_ix, gi)
            sync.dma_start(out=d.gist[gi % 2][0:nb2, 0:w2],
                           in_=gsrc[:, c0:c0 + w2]).then_inc([sems.d_q0, sems.d_q1][gi % 2], DMA)
        for i, (kind, c0, w) in enumerate(prep):
            nb = KB_GU if kind == "gu" else KB_DN
            scsrc = t.sc_gu if kind == "gu" else t.sc_dn
            qzsrc = t.qz_gu if kind == "gu" else t.qz_dn
            sync.wait_ge(sems.d_pr, i)
            sync.dma_start(out=d.scf[0:nb, 0:w], in_=scsrc[:, c0:c0 + w]).then_inc(sems.d_ld, DMA)
            sync.dma_start(out=d.qzb[0:nb, 0:w // 8], in_=qzsrc[:, c0 // 8:(c0 + w) // 8]).then_inc(sems.d_ld, DMA)
            sync.wait_ge(sems.d_pr, i + 1)
            nbgdst = t.negbg_gu if kind == "gu" else t.negbg_dn
            sync.dma_start(out=nbgdst[:, c0:c0 + w], in_=d.nbg[0:nb, 0:w]).then_inc(sems.d_st, DMA)
        ttc = 0
        d_w = [sems.d_w0, sems.d_w1, sems.d_w2, sems.d_w3]
        for it, (kind, G, c0, w, g) in enumerate(d_main):
            qrep = d.qrep[it % 2]
            dq = [sems.d_q0, sems.d_q1][it % 2]
            f = it // 2
            qw = t.qw_gu if kind == "gu" else t.qw_dn
            if it >= 2:
                sync.wait_ge(sems.d_ve, ts_cum[it - 2])
            sync.dma_start(out=qrep[0:16, 0:w], in_=qw[16 * G:16 * G + 16, c0:c0 + w]).then_inc(dq, DMA)
            sync.wait_ge(dq, (4 + 8 * f + 1) * DMA)
            for j in range(1, 8):
                sync.dma_start(out=qrep[16 * j:16 * j + 16, 0:w], in_=qrep[0:16, 0:w]).then_inc(dq, DMA)
            cache = t.wgu_c if kind == "gu" else t.wdn_c
            for u in range(w // 512):
                ttc += 1
                sync.wait_ge(sems.d_fr, ttc)
                mc = (c0 + 512 * u) // 512
                sync.dma_start(out=cache[mc, G], in_=d.wt[(ttc - 1) % 4][:]).then_inc(d_w[(ttc - 1) % 4], DMA)

    def d_vector(vector):
        vector.wait_ge(sems.d_l0, 7 * DMA)
        vector.tensor_copy(out=s.gcolf_gu[:], in_=s.gcol_gu_i[:])
        vector.tensor_copy(out=s.gcolf_dn[:], in_=s.gcol_dn_i[:]).then_inc(sems.d_ix, 1)
        for gi, (kind, c0) in enumerate(d_sync.gchunks):
            w2 = min(2048, (D if kind == "gu" else INTER) - c0)
            nb2 = KB_GU if kind == "gu" else KB_DN
            gdst = d.gbf_gu if kind == "gu" else d.gbf_dn
            vector.wait_ge([sems.d_q0, sems.d_q1][gi % 2], (gi // 2 + 1) * DMA)
            vector.tensor_copy(out=gdst[0:nb2, c0:c0 + w2],
                               in_=d.gist[gi % 2][0:nb2, 0:w2]).then_inc(sems.d_ix, 1)
        for i, (kind, c0, w) in enumerate(prep):
            nb = KB_GU if kind == "gu" else KB_DN
            scbf = d.scbf_gu if kind == "gu" else d.scbf_dn
            if i == 0:
                vector.wait_ge(sems.d_ix, 9)
            vector.wait_ge(sems.d_ld, (2 * (i + 1)) * DMA)
            if i >= 1:
                vector.wait_ge(sems.d_st, i * DMA)
                vector.wait_ge(sems.d_pr, i)
            vector.tensor_copy(out=scbf[0:nb, c0:c0 + w], in_=d.scf[0:nb, 0:w])
            vector.tensor_scalar(out=d.nsc[0:nb, 0:w], in0=d.scf[0:nb, 0:w],
                                 scalar1=-1.0, scalar2=None, op0=ALU.mult).then_inc(sems.d_ch, 1)
            nblk = w // 128
            for j in range(8):
                src = d.qzb[0:nb, 0:w // 8].rearrange("p (b c) -> p b c", c=16)
                dst = d.zq[0:nb, 0:w].rearrange("p (b c) -> p b c", c=128)[:, :, 16 * j:16 * j + 16]
                ins = vector.tensor_scalar(out=dst, in0=src, scalar1=4 * j, scalar2=15,
                                           op0=ALU.logical_shift_right, op1=ALU.bitwise_and)
                if j == 7:
                    ins.then_inc(sems.d_ch, 1)
            vector.wait_ge(sems.d_ch, 2 * (i + 1))
            vector.scalar_tensor_tensor(out=d.nbg[0:nb, 0:w], in0=d.zq[0:nb, 0:w],
                                        scalar=1.0, in1=d.nsc[0:nb, 0:w],
                                        op0=ALU.add, op1=ALU.mult).then_inc(sems.d_pr, 1)
        vector.wait_ge(sems.d_ix, 10)
        tsc = 0
        for it, (kind, G, c0, w, g) in enumerate(d_main):
            qrep = d.qrep[it % 2]
            nb = KB_GU if kind == "gu" else KB_DN
            if c0 == 0:
                if g >= 4:
                    vector.wait_ge(sems.d_mm, g_last_tt[g - 4])
                gbf = d.gbf_gu if kind == "gu" else d.gbf_dn
                vector.tensor_scalar(out=d.indT[0:nb, 128 * (g % 4):128 * (g % 4) + 128],
                                     in0=gbf[0:nb, 128 * G:128 * G + 128],
                                     scalar1=s.iota_col[0:nb, :], scalar2=None,
                                     op0=ALU.is_equal).then_inc(sems.d_ix, 1)
                if kind == "gu":
                    vector.tensor_scalar(out=s.ind_gu[:, 32 * G:32 * G + 32],
                                         in0=s.iota_row[:, 0:KB_GU],
                                         scalar1=s.gcolf_gu[:, G:G + 1], scalar2=None,
                                         op0=ALU.is_equal)
            vector.wait_ge([sems.d_q0, sems.d_q1][it % 2], (4 + 8 * (it // 2 + 1)) * DMA)
            d_w = [sems.d_w0, sems.d_w1, sems.d_w2, sems.d_w3]
            for u in range(w // 512):
                tsc += 1
                nib = d.nib[tsc % 2]
                nibf = d.nibf[tsc % 2]
                if tsc > 2:
                    vector.wait_ge(sems.d_ch, 2 * n_prep + tsc - 2)
                vector.tensor_scalar(out=nib[:], in0=qrep[:, 512 * u:512 * u + 512],
                                     scalar1=s.mask[:], scalar2=None,
                                     op0=ALU.bitwise_and).then_inc(sems.d_ve, 1)
                vector.wait_ge(sems.d_ve, tsc)
                if tsc > 2:
                    vector.wait_ge(sems.d_fr, tsc - 2)
                vector.tensor_scalar(out=nibf[:], in0=nib[:],
                                     scalar1=s.pscale[:], scalar2=None,
                                     op0=ALU.mult).then_inc(sems.d_ch, 1)
                vector.wait_ge(sems.d_mm, tsc)
                vector.wait_ge(sems.d_ch, 2 * n_prep + tsc)
                if tsc > 4:
                    vector.wait_ge(d_w[(tsc - 1) % 4], ((tsc - 1) // 4) * DMA)
                vector.tensor_tensor(out=d.wt[(tsc - 1) % 4][:], in0=nibf[:],
                                     in1=pq[tsc % 2][:, 0:512], op=ALU.mult).then_inc(sems.d_fr, 1)

    def d_tensor(tensor):
        tensor.wait_ge(sems.d_pr, n_prep)
        ttc = 0
        ixc = 0
        for it, (kind, G, c0, w, g) in enumerate(d_main):
            nb = KB_GU if kind == "gu" else KB_DN
            scbf = d.scbf_gu if kind == "gu" else d.scbf_dn
            if c0 == 0:
                tensor.wait_ge(sems.d_ix, 10 + (g + 1))
            for u in range(w // 512):
                ttc += 1
                if ttc > 2:
                    tensor.wait_ge(sems.d_fr, ttc - 2)
                tensor.matmul(out=pq[ttc % 2][:, 0:512],
                              lhsT=d.indT[0:nb, 128 * (g % 4):128 * (g % 4) + 128],
                              rhs=scbf[0:nb, c0 + 512 * u:c0 + 512 * (u + 1)],
                              start=True, stop=True).then_inc(sems.d_mm, 1)

    def d_scalar(scalar):
        scalar.wait_ge(sems.d_ix, 9)
        scalar.nop()
        scalar.sem_inc(sems.d_ix, 1)

    p = NS()
    p.cosk = sbuf("p_cosk", CONST2, [64, 1024], F32)
    p.sink = sbuf("p_sink", CONST2 + 4096, [64, 1024], F32)
    p.cosq = sbuf("p_cosq", CONST2 + 8192, [64, 1024], F32)
    p.sinq = sbuf("p_sinq", CONST2 + 12288, [64, 1024], F32)
    p.rt0 = sbuf("p_rt0", CONST2 + 16384, [64, 1024], F32)
    p.rt1 = sbuf("p_rt1", CONST2 + 20480, [64, 1024], F32)
    p.rt2 = sbuf("p_rt2", STAGE + 17408, [64, 1024], F32)
    p.rt3 = sbuf("p_rt3", R3 + 66560, [64, 1024], F32)
    p.xstage = [sbuf(f"p_xst{i}", STAGE + i * 4096, [128, 1024], F32) for i in range(2)]
    p.wstage = [sbuf(f"p_wst{i}", STAGE + 8192 + i * 1024, [128, 256], F32) for i in range(2)]
    p.wbf = [sbuf(f"p_wbf{i}", STAGE + 10240 + i * 512, [128, 256], BF16) for i in range(2)]
    p.kbf = [sbuf(f"p_kbf{i}", STAGE + 11264 + i * 2048, [128, 1024], BF16) for i in range(2)]
    p.vbf = [sbuf(f"p_vbf{i}", STAGE + 15360 + i * 1024, [128, 512], BF16) for i in range(2)]
    p.xh = sbuf("p_xh", R3, [128, 32 * 1024], BF16)
    p.wvbf = sbuf("p_wvbf", R2, [128, 32 * 512], BF16)
    p.qt = sbuf("p_qt", R2, [128, 32 * 1024], BF16)

    p_x = [sems.p_x0, sems.p_x1]
    p_w = [sems.p_w0, sems.p_w1]
    p_v = [sems.p_v0, sems.p_v1]
    p_k = [sems.p_k0, sems.p_k1]
    PLD_H = 802
    PCV_H = 32
    PWL_H = 768
    PMM_H = 4096
    PVE_H = 96
    PST_H = 288
    P_TOT_LD = 2 * PLD_H + 546
    P_TOT_CV = 96
    P_TOT_WL = 2 * PWL_H + 512
    P_TOT_MM = 2 * PMM_H + 2048
    P_TOT_VE = 2 * PVE_H + 32
    P_TOT_ST = 2 * PST_H

    def p_xl_sync(sync, c, xsrc, cos_src, sin_src, cdst, sdst, cv_base):
        sync.dma_start(out=cdst[:], in_=cos_src).then_inc(sems.p_cs, DMA)
        sync.dma_start(out=sdst[:], in_=sin_src).then_inc(sems.p_cs, DMA)
        for k in range(32):
            if k >= 2:
                sync.wait_ge(sems.p_cv, cv_base + k - 1)
            sync.dma_start(out=p.xstage[k % 2][:],
                           in_=xsrc[128 * k:128 * k + 128, :]).then_inc(p_x[k % 2], DMA)

    def p_xl_vector(vector, c, xfill_base):
        for k in range(32):
            vector.wait_ge(p_x[k % 2], (xfill_base + k // 2 + 1) * DMA)
            vector.tensor_copy(out=p.xh[:, 1024 * k:1024 * (k + 1)],
                               in_=p.xstage[k % 2][:]).then_inc(sems.p_cv, 1)
            c["cv"] += 1

    def p_proj_sync(sync, c, w_src, wl_base, ve_base, spill, st_base):
        for mc2 in range(16):
            for k in range(32):
                i = mc2 * 32 + k
                if i >= 2:
                    sync.wait_ge(sems.p_wl, wl_base + i - 1)
                sync.dma_start(out=p.wstage[i % 2][:],
                               in_=w_src[128 * k:128 * k + 128, 256 * mc2:256 * mc2 + 256]).then_inc(p_w[i % 2], DMA)
            if spill is not None and mc2 >= 1:
                for m2 in range(2):
                    j = (mc2 - 1) * 2 + m2
                    sync.wait_ge(sems.p_ve, ve_base + j + 1)
                    sync.dma_start(out=spill(j), in_=p.kbf[j % 2][:]).then_inc(p_k[j % 2], DMA)
                    c["st"] += 1
        if spill is not None:
            for m2 in range(2):
                j = 30 + m2
                sync.wait_ge(sems.p_ve, ve_base + j + 1)
                sync.dma_start(out=spill(j), in_=p.kbf[j % 2][:]).then_inc(p_k[j % 2], DMA)
                c["st"] += 1

    def p_proj_scalar(scalar, c, wfill_base, mm_base):
        for i in range(512):
            scalar.wait_ge(p_w[i % 2], (wfill_base + i // 2 + 1) * DMA)
            if i >= 2:
                scalar.wait_ge(sems.p_mm, mm_base + (i - 1) * 4)
            scalar.copy(out=p.wbf[i % 2][:], in_=p.wstage[i % 2][:]).then_inc(sems.p_wl, 1)
            c["wl"] += 1

    def p_proj_tensor(tensor, c, wl_base, cv_base, ve_base):
        for mc2 in range(16):
            for k in range(32):
                i = mc2 * 32 + k
                tensor.wait_ge(sems.p_wl, wl_base + i + 1)
                if k == 0:
                    tensor.wait_ge(sems.p_cv, cv_base + 32)
                    if mc2 >= 2:
                        tensor.wait_ge(sems.p_ve, ve_base + 2 * (mc2 - 2) + 2)
                    else:
                        tensor.wait_ge(sems.p_ve, ve_base)
                for m2 in range(2):
                    for c5 in range(2):
                        tensor.matmul(out=pq[2 * (mc2 % 2) + m2][:, 512 * c5:512 * (c5 + 1)],
                                      lhsT=p.wbf[i % 2][:, 128 * m2:128 * (m2 + 1)],
                                      rhs=p.xh[:, 1024 * k + 512 * c5:1024 * k + 512 * (c5 + 1)],
                                      start=(k == 0), stop=(k == 31)).then_inc(sems.p_mm, 1)
                        c["mm"] += 1

    def p_proj_vector(vector, c, mm_base, st_base, ve_base, cosg, sing, out_of_j, cs_val):
        for mc2 in range(16):
            for m2 in range(2):
                j = mc2 * 2 + m2
                if j == 0:
                    vector.wait_ge(sems.p_cs, cs_val * DMA)
                vector.wait_ge(sems.p_mm, mm_base + 128 * (mc2 + 1))
                if j > 0:
                    vector.wait_ge(sems.p_ve, ve_base + j)
                dst = out_of_j(j)
                if dst is None:
                    dstap = p.kbf[j % 2]
                    if j >= 2 or st_base > 0:
                        vector.wait_ge(p_k[j % 2], (st_base + j // 2) * DMA)
                else:
                    dstap = dst
                pp = pq[2 * (mc2 % 2) + m2]
                ch = c["ch"]
                vector.tensor_mul(out=p.rt0[:], in0=pp[0:64, :], in1=cosg[:]).then_inc(sems.p_ch, 1)
                vector.tensor_mul(out=p.rt1[:], in0=pp[64:128, :], in1=sing[:]).then_inc(sems.p_ch, 1)
                vector.wait_ge(sems.p_ch, ch + 2)
                vector.tensor_sub(out=dstap[0:64, :], in0=p.rt0[:], in1=p.rt1[:])
                vector.tensor_mul(out=p.rt2[:], in0=pp[0:64, :], in1=sing[:]).then_inc(sems.p_ch, 1)
                vector.tensor_mul(out=p.rt3[:], in0=pp[64:128, :], in1=cosg[:]).then_inc(sems.p_ch, 1)
                vector.wait_ge(sems.p_ch, ch + 4)
                c["ch"] += 4
                vector.tensor_add(out=dstap[64:128, :], in0=p.rt2[:],
                                  in1=p.rt3[:]).then_inc(sems.p_ve, 1)
                c["ve"] += 1

    def p_v_sync(sync, c, th, wl_base, ve_base, st_base):
        for nc8 in range(8):
            if nc8 >= 1:
                for tokm in range(8):
                    e = (nc8 - 1) * 8 + tokm
                    sync.wait_ge(sems.p_ve, ve_base + e + 1)
                    for h4 in range(4):
                        sync.dma_start(out=t.v_sp[4 * (nc8 - 1) + h4, 8 * th + tokm],
                                       in_=p.vbf[e % 2][:, 128 * h4:128 * (h4 + 1)]).then_inc(p_v[e % 2], DMA)
                        c["st"] += 1
            for k in range(32):
                i = nc8 * 32 + k
                if i >= 2:
                    sync.wait_ge(sems.p_wl, wl_base + i - 1)
                sync.dma_start(out=p.xstage[i % 2][:, 0:512],
                               in_=t.wv[128 * k:128 * k + 128, 512 * nc8:512 * nc8 + 512]).then_inc(p_x[i % 2], DMA)
        for tokm in range(8):
            e = 7 * 8 + tokm
            sync.wait_ge(sems.p_ve, ve_base + e + 1)
            for h4 in range(4):
                sync.dma_start(out=t.v_sp[4 * 7 + h4, 8 * th + tokm],
                               in_=p.vbf[e % 2][:, 128 * h4:128 * (h4 + 1)]).then_inc(p_v[e % 2], DMA)
                c["st"] += 1

    def p_v_scalar(scalar, c, xfill_base, mm_base):
        for nc8 in range(8):
            for k in range(32):
                i = nc8 * 32 + k
                scalar.wait_ge(p_x[i % 2], (xfill_base + i // 2 + 1) * DMA)
                if nc8 >= 1:
                    scalar.wait_ge(sems.p_mm, mm_base + 256 * nc8)
                scalar.copy(out=p.wvbf[:, 512 * k:512 * (k + 1)],
                            in_=p.xstage[i % 2][:, 0:512]).then_inc(sems.p_wl, 1)
                c["wl"] += 1

    def p_v_tensor(tensor, c, wl_base, ve_base):
        for nc8 in range(8):
            for tokm in range(8):
                g = nc8 * 8 + tokm
                if g >= 2:
                    tensor.wait_ge(sems.p_ve, ve_base + g - 1)
                else:
                    tensor.wait_ge(sems.p_ve, ve_base)
                for k in range(32):
                    tensor.wait_ge(sems.p_wl, wl_base + nc8 * 32 + k + 1)
                    tensor.matmul(out=pq[0][:, 512 * (g % 2):512 * (g % 2) + 512],
                                  lhsT=p.xh[:, 1024 * k + 128 * tokm:1024 * k + 128 * (tokm + 1)],
                                  rhs=p.wvbf[:, 512 * k:512 * (k + 1)],
                                  start=(k == 0), stop=(k == 31)).then_inc(sems.p_mm, 1)
                    c["mm"] += 1

    def p_v_vector(vector, c, mm_base, th):
        for nc8 in range(8):
            for tokm in range(8):
                e = nc8 * 8 + tokm
                vector.wait_ge(sems.p_mm, mm_base + (e + 1) * 32)
                if e >= 2 or th == 1:
                    vector.wait_ge(p_v[e % 2], 4 * (32 * th + e // 2) * DMA)
                vector.tensor_copy(out=p.vbf[e % 2][:],
                                   in_=pq[0][:, 512 * (e % 2):512 * (e % 2) + 512]).then_inc(sems.p_ve, 1)
                c["ve"] += 1

    def p_sync(sync):
        c = {"ld": 0, "st": 0}
        for th in range(2):
            sl = slice(1024 * th, 1024 * (th + 1))
            cv_base = 32 * th
            wl_base = PWL_H * th
            ve_base = PVE_H * th
            st_base_k = PST_H * th * DMA
            p_xl_sync(sync, c, t.xT[:, sl], t.cos_k[:, sl], t.sin_k[:, sl], p.cosk, p.sink, cv_base)
            def kspill(j, th=th):
                return t.kT_sp.rearrange("(a b) c -> a b c", b=128)[j, :, sl]
            p_proj_sync(sync, c, t.wk, wl_base, ve_base, kspill, PST_H * th)
            p_v_sync(sync, c, th, wl_base + 512, ve_base + 32, PST_H * th + 32)
            if th == 0:
                sync.wait_ge(sems.p_mm, PMM_H)
                sync.wait_ge(sems.p_ve, PVE_H)
                sync.wait_ge(sems.p_wl, PWL_H)
        sync.wait_ge(sems.p_mm, 2 * PMM_H)
        sync.wait_ge(sems.p_wl, 2 * PWL_H)
        sync.dma_start(out=p.cosq[:], in_=t.cos_q[:]).then_inc(sems.p_cs, DMA)
        sync.dma_start(out=p.sinq[:], in_=t.sin_q[:]).then_inc(sems.p_cs, DMA)
        for k in range(32):
            if k >= 2:
                sync.wait_ge(sems.p_cv, 64 + k - 1)
            sync.dma_start(out=p.xstage[k % 2][:],
                           in_=t.x_own[128 * k:128 * k + 128, :]).then_inc(p_x[k % 2], DMA)
        for mc2 in range(16):
            for k in range(32):
                i = mc2 * 32 + k
                if i >= 2:
                    sync.wait_ge(sems.p_wl, 2 * PWL_H + i - 1)
                sync.dma_start(out=p.wstage[i % 2][:],
                               in_=t.wq[128 * k:128 * k + 128, 256 * mc2:256 * mc2 + 256]).then_inc(p_w[i % 2], DMA)
        sync.wait_ge(p_k[0], 32 * DMA)
        sync.wait_ge(p_k[1], 32 * DMA)
        sync.wait_ge(p_v[0], 2 * 32 * 4 * DMA)
        sync.wait_ge(p_v[1], 2 * 32 * 4 * DMA)
        sync.wait_ge(sems.p_ve, P_TOT_VE)
        sync.wait_ge(p_x[0], 304 * DMA)
        sync.wait_ge(p_x[1], 304 * DMA)
        sync.wait_ge(p_w[0], 768 * DMA)
        sync.wait_ge(p_w[1], 768 * DMA)
        sync.wait_ge(sems.p_cs, 6 * DMA)

    def p_xl_vector2(vector, xfill_base):
        for k in range(32):
            vector.wait_ge(p_x[k % 2], (xfill_base + k // 2 + 1) * DMA)
            vector.tensor_copy(out=p.xh[:, 1024 * k:1024 * (k + 1)],
                               in_=p.xstage[k % 2][:]).then_inc(sems.p_cv, 1)

    def p_vector(vector):
        c = {"cv": 0, "ve": 0, "ch": 0}
        for th in range(2):
            p_xl_vector(vector, c, 144 * th)
            p_proj_vector(vector, c, PMM_H * th, 16 * th, PVE_H * th,
                          p.cosk, p.sink, lambda j: None, 2 * (th + 1))
            p_v_vector(vector, c, PMM_H * th + 2048, th)
        p_xl_vector2(vector, 288)
        p_proj_vector(vector, c, 2 * PMM_H, 0, 2 * PVE_H,
                      p.cosq, p.sinq,
                      lambda j: p.qt[:, 1024 * j:1024 * (j + 1)], 6)

    def p_scalar(scalar):
        c = {"wl": 0}
        for th in range(2):
            p_proj_scalar(scalar, c, 256 * th, PMM_H * th)
            p_v_scalar(scalar, c, 144 * th + 16, PMM_H * th + 2048)
        for i in range(512):
            scalar.wait_ge(p_w[i % 2], (512 + i // 2 + 1) * DMA)
            if i >= 2:
                scalar.wait_ge(sems.p_mm, 2 * PMM_H + (i - 1) * 4)
            scalar.copy(out=p.wbf[i % 2][:], in_=p.wstage[i % 2][:]).then_inc(sems.p_wl, 1)

    def p_tensor(tensor):
        c = {"mm": 0}
        for th in range(2):
            p_proj_tensor(tensor, c, PWL_H * th, 32 * th, PVE_H * th)
            p_v_tensor(tensor, c, PWL_H * th + 512, PVE_H * th + 32)
        tensor.wait_ge(sems.p_cv, 96)
        p_proj_tensor(tensor, c, 2 * PWL_H, 64, 2 * PVE_H)

    a = NS()
    a.kh = [sbuf(f"a_kh{i}", R3 + i * 4096, [128, 2048], BF16) for i in range(2)]
    a.vh = [sbuf(f"a_vh{i}", R3 + 8192 + i * 4096, [128, 2048], BF16) for i in range(2)]
    a.e = [sbuf(f"a_e{i}", R3 + 16384 + i * 1024, [128, 512], BF16) for i in range(3)]
    a.attn = sbuf("a_attn", R3 + 19456, [128, 32 * 1024], BF16)
    a.mask = sbuf("a_mask", STAGE, [128, 16 * 1024], U8)
    a.recb = [sbuf(f"a_recb{i}", STAGE + 16384 + i * 2048, [128, 512], F32) for i in range(2)]

    A_SEQ = ["S0", "S1"]
    for k in range(14):
        A_SEQ += [f"sum{k}", f"AV{k}", f"S{k + 2}"]
    A_SEQ += ["sum14", "AV14", "sum15", "AV15"]
    A_POS = {tag: i + 1 for i, tag in enumerate(A_SEQ)}
    A_MM_G = 48
    A_VE_G = 18
    A_EX_G = 16

    a_kv = [sems.a_k0, sems.a_k1]
    a_r = [sems.a_r0, sems.a_r1]
    a_b = [sems.a_b0, sems.a_b1]

    def a_sync(sync):
        sync.dma_start(out=a.mask[:], in_=t.mask[:]).then_inc(sems.a_sm, DMA)
        for h in range(H):
            if h >= 2:
                sync.wait_ge(sems.a_mm, 96 * (h - 1))
            sync.dma_start(out=a.kh[h % 2][:], in_=t.kT_sp[128 * h:128 * h + 128, :]).then_inc(a_kv[h % 2], DMA)
            sync.dma_start(out=a.vh[h % 2][:],
                           in_=t.v_sp[h].rearrange("t p d -> p t d")).then_inc(a_kv[h % 2], DMA)
            for qc in range(2):
                idx = 2 * h + qc
                sync.wait_ge(sems.a_ve, A_VE_G * idx + 17)
                if idx >= 2:
                    sync.wait_ge(sems.a_ve, A_VE_G * (idx - 1))
                sync.dma_start(out=t.rows_sp[idx % 4, 0:512],
                               in_=s.rows[0:1, 512 * (idx % 3):512 * (idx % 3) + 512]).then_inc(a_r[idx % 2], DMA)
                sync.wait_ge(a_r[idx % 2], (idx // 2 + 1) * DMA)
                sync.dma_start(out=a.recb[idx % 2][:],
                               in_=t.rows_sp[idx % 4, 0:512].unsqueeze(0).partition_broadcast(128)).then_inc(a_b[idx % 2], DMA)

    def a_tensor(tensor):
        for h in range(H):
            tensor.wait_ge(a_kv[h % 2], 2 * (h // 2 + 1) * DMA)
            for qc in range(2):
                idx = 2 * h + qc
                base = A_MM_G * idx
                par = idx % 2
                for tag in A_SEQ:
                    kind = "S" if tag[0] == "S" else ("sum" if tag[:3] == "sum" else "AV")
                    k = int(tag[len(kind):])
                    if kind == "S":
                        pS = [pq[0][:, 0:512], pq[0][:, 512:1024], pq[1][:, 0:512]][k % 3]
                        if k >= 3:
                            tensor.wait_ge(sems.a_ex, A_EX_G * idx + k - 2)
                        elif idx >= 1:
                            tensor.wait_ge(sems.a_ex, A_EX_G * idx)
                        tensor.matmul(out=pS, lhsT=a.kh[h % 2][:, 128 * k:128 * k + 128],
                                      rhs=p.qt[:, 1024 * h + 512 * qc:1024 * h + 512 * (qc + 1)],
                                      start=True, stop=True).then_inc(sems.a_mm, 1)
                    elif kind == "sum":
                        tensor.wait_ge(sems.a_ve, A_VE_G * idx + k + 1)
                        if k == 0 and idx >= 2:
                            tensor.wait_ge(sems.a_ve, A_VE_G * (idx - 1))
                        tensor.matmul(out=pq[3][0:1, 512 * par:512 * par + 512],
                                      lhsT=s.ones[:], rhs=a.e[k % 3][:],
                                      start=(k == 0), stop=(k == 15)).then_inc(sems.a_mm, 1)
                    else:
                        tensor.matmul(out=pq[2][:, 512 * par:512 * par + 512],
                                      lhsT=a.vh[h % 2][:, 128 * k:128 * k + 128],
                                      rhs=a.e[k % 3][:],
                                      start=(k == 0), stop=(k == 15)).then_inc(sems.a_mm, 1)

    def a_scalar(scalar):
        for h in range(H):
            for qc in range(2):
                idx = 2 * h + qc
                for k in range(16):
                    pS = [pq[0][:, 0:512], pq[0][:, 512:1024], pq[1][:, 0:512]][k % 3]
                    scalar.wait_ge(sems.a_mm, A_MM_G * idx + A_POS[f"S{k}"])
                    if k >= 3:
                        scalar.wait_ge(sems.a_mm, A_MM_G * idx + A_POS[f"AV{k - 3}"])
                    elif idx >= 1:
                        scalar.wait_ge(sems.a_mm, A_MM_G * (idx - 1) + A_POS[f"AV{k + 13}"])
                    scalar.activation(out=a.e[k % 3][:], in_=pS, func=AF.Exp,
                                      scale=SCALE).then_inc(sems.a_ex, 1)

    def a_vector(vector):
        vector.wait_ge(sems.a_sm, DMA)
        for h in range(H):
            for qc in range(2):
                idx = 2 * h + qc
                par = idx % 2
                for k in range(16):
                    vector.wait_ge(sems.a_ex, A_EX_G * idx + k + 1)
                    msl = a.mask[:, 1024 * k + 512 * qc:1024 * k + 512 * (qc + 1)]
                    vector.tensor_tensor(out=a.e[k % 3][:], in0=a.e[k % 3][:],
                                         in1=msl, op=ALU.mult).then_inc(sems.a_ve, 1)
                vector.wait_ge(sems.a_mm, A_MM_G * idx + A_POS["sum15"])
                if idx >= 3:
                    vector.wait_ge(a_r[(idx - 3) % 2], ((idx - 3) // 2 + 1) * DMA)
                vector.reciprocal(out=s.rows[0:1, 512 * (idx % 3):512 * (idx % 3) + 512],
                                  in_=pq[3][0:1, 512 * par:512 * par + 512]).then_inc(sems.a_ve, 1)
                vector.wait_ge(a_b[idx % 2], (idx // 2 + 1) * DMA)
                vector.wait_ge(sems.a_mm, A_MM_G * idx + A_POS["AV15"])
                vector.tensor_tensor(out=a.attn[:, 1024 * h + 512 * qc:1024 * h + 512 * (qc + 1)],
                                     in0=pq[2][:, 512 * par:512 * par + 512],
                                     in1=a.recb[idx % 2][:], op=ALU.mult).then_inc(sems.a_ve, 1)

    A_TOT_VE = A_VE_G * 64
    A_TOT_MM = A_MM_G * 64

    w = NS()
    w.tx = p.qt
    w.ab = sbuf("w_ab", CONST2, [128, 1024], F32)
    w.bb = sbuf("w_bb", CONST2 + 4096, [128, 1024], F32)
    w.u = sbuf("w_u", CONST2 + 8192, [128, 1024], F32)
    w.x2 = [sbuf(f"w_x2{i}", CONST2 + 12288 + i * 4096, [128, 1024], F32) for i in range(2)]
    w.rows = sbuf("w_rows", CONST2 + 20480, [128, 1024], F32)

    def w_sync(sync):
        for mc2 in range(16):
            for m2 in range(2):
                m = 2 * mc2 + m2
                if m >= 2:
                    sync.wait_ge(sems.w_ve, m - 1)
                sync.dma_start(out=p.xstage[m % 2][:],
                               in_=t.x_own[128 * m:128 * m + 128, :]).then_inc(p_x[m % 2], DMA)
            for k in range(32):
                i = mc2 * 32 + k
                if i >= 2:
                    sync.wait_ge(sems.w_wl, i - 1)
                sync.dma_start(out=p.wstage[i % 2][:],
                               in_=t.wo[128 * k:128 * k + 128, 256 * mc2:256 * mc2 + 256]).then_inc(p_w[i % 2], DMA)
        sync.wait_ge(sems.w_ve, 37)
        sync.dma_start(out=t.rows_sp[4, :], in_=w.u[0:1, :]).then_inc(sems.w_rw, DMA)
        sync.dma_start(out=t.rows_sp[5, :], in_=w.rows[0:1, :]).then_inc(sems.w_rw, DMA)
        sync.wait_ge(sems.w_rw, 2 * DMA)
        sync.dma_start(out=w.ab[:], in_=t.rows_sp[4, :].unsqueeze(0).partition_broadcast(128)).then_inc(sems.w_rw, DMA)
        sync.dma_start(out=w.bb[:], in_=t.rows_sp[5, :].unsqueeze(0).partition_broadcast(128)).then_inc(sems.w_rw, DMA)
        for m in range(32):
            sync.wait_ge(sems.w_ve, 32 + 5 + 3 * m + 2)
            sync.dma_start(out=t.xmid[128 * m:128 * m + 128, :],
                           in_=w.x2[m % 2][:]).then_inc([sems.w_s0, sems.w_s1][m % 2], DMA)

    def w_scalar(scalar):
        for mc2 in range(16):
            for k in range(32):
                i = mc2 * 32 + k
                scalar.wait_ge(p_w[i % 2], (768 + i // 2 + 1) * DMA)
                if i >= 2:
                    scalar.wait_ge(sems.w_mm, (i - 1) * 4)
                scalar.copy(out=p.wbf[i % 2][:], in_=p.wstage[i % 2][:]).then_inc(sems.w_wl, 1)
            for m2 in range(2):
                m = (mc2 - 1) * 2 + m2
                if 0 <= m:
                    scalar.wait_ge(sems.w_ve, m + 1)
                    if m >= 2:
                        scalar.wait_ge(sems.w_st, 4 * (m - 1))
                    scalar.activation(out=p.kbf[m % 2][:], in_=w.tx[:, 1024 * m:1024 * (m + 1)],
                                      func=AF.Square).then_inc(sems.w_sc, 1)
        for m in (30, 31):
            scalar.wait_ge(sems.w_ve, m + 1)
            scalar.wait_ge(sems.w_st, 4 * (m - 1))
            scalar.activation(out=p.kbf[m % 2][:], in_=w.tx[:, 1024 * m:1024 * (m + 1)],
                              func=AF.Square).then_inc(sems.w_sc, 1)
        scalar.wait_ge(sems.w_ve, 34)
        scalar.activation(out=pq[2][0:1, 0:1024], in_=w.rows[0:1, :],
                          func=AF.Square).then_inc(sems.w_sc, 1)
        scalar.wait_ge(sems.w_ve, 36)
        scalar.activation(out=pq[2][0:1, 0:1024], in_=w.u[0:1, :],
                          func=AF.Sqrt).then_inc(sems.w_sc, 1)

    def w_tensor(tensor):
        for mc2 in range(16):
            for k in range(32):
                i = mc2 * 32 + k
                tensor.wait_ge(sems.w_wl, i + 1)
                if k == 0:
                    if mc2 >= 1:
                        tensor.wait_ge(sems.w_ve, 2 * mc2)
                for m2 in range(2):
                    for c5 in range(2):
                        tensor.matmul(out=pq[m2][:, 512 * c5:512 * (c5 + 1)],
                                      lhsT=p.wbf[i % 2][:, 128 * m2:128 * (m2 + 1)],
                                      rhs=a.attn[:, 1024 * k + 512 * c5:1024 * k + 512 * (c5 + 1)],
                                      start=(k == 0), stop=(k == 31)).then_inc(sems.w_mm, 1)
            for m2 in range(2):
                m = (mc2 - 1) * 2 + m2
                if 0 <= m < 32:
                    tensor.wait_ge(sems.w_ve, m + 1)
                    tensor.wait_ge(sems.w_sc, m + 1)
                    for c5 in range(2):
                        tensor.matmul(out=pq[2][0:1, 512 * c5:512 * (c5 + 1)],
                                      lhsT=s.ones[:],
                                      rhs=w.tx[:, 1024 * m + 512 * c5:1024 * m + 512 * (c5 + 1)],
                                      start=(m == 0), stop=(m == 31)).then_inc(sems.w_st, 1)
                        tensor.matmul(out=pq[3][0:1, 512 * c5:512 * (c5 + 1)],
                                      lhsT=s.ones[:],
                                      rhs=p.kbf[m % 2][:, 512 * c5:512 * (c5 + 1)],
                                      start=(m == 0), stop=(m == 31)).then_inc(sems.w_st, 1)
        for m in (30, 31):
            tensor.wait_ge(sems.w_ve, m + 1)
            tensor.wait_ge(sems.w_sc, m + 1)
            for c5 in range(2):
                tensor.matmul(out=pq[2][0:1, 512 * c5:512 * (c5 + 1)],
                              lhsT=s.ones[:],
                              rhs=w.tx[:, 1024 * m + 512 * c5:1024 * m + 512 * (c5 + 1)],
                              start=(m == 0), stop=(m == 31)).then_inc(sems.w_st, 1)
                tensor.matmul(out=pq[3][0:1, 512 * c5:512 * (c5 + 1)],
                              lhsT=s.ones[:],
                              rhs=p.kbf[m % 2][:, 512 * c5:512 * (c5 + 1)],
                              start=(m == 0), stop=(m == 31)).then_inc(sems.w_st, 1)

    W_TOT_MM = 2048

    def w_vector(vector):
        for mc2 in range(16):
            for m2 in range(2):
                m = mc2 * 2 + m2
                vector.wait_ge(sems.w_mm, 128 * (mc2 + 1))
                vector.wait_ge(p_x[m % 2], (304 + m // 2 + 1) * DMA)
                vector.tensor_add(out=w.tx[:, 1024 * m:1024 * (m + 1)],
                                  in0=pq[m2][:, 0:1024],
                                  in1=p.xstage[m % 2][:]).then_inc(sems.w_ve, 1)
        vector.wait_ge(sems.w_mm, W_TOT_MM)
        vector.wait_ge(sems.w_st, 128)
        vector.tensor_scalar(out=w.rows[0:1, :], in0=pq[2][0:1, 0:1024],
                             scalar1=1.0 / D, scalar2=None, op0=ALU.mult).then_inc(sems.w_ve, 1)
        vector.tensor_scalar(out=w.u[0:1, :], in0=pq[3][0:1, 0:1024],
                             scalar1=1.0 / D, scalar2=None, op0=ALU.mult).then_inc(sems.w_ve, 1)
        vector.wait_ge(sems.w_sc, 33)
        vector.tensor_sub(out=w.u[0:1, :], in0=w.u[0:1, :],
                          in1=pq[2][0:1, 0:1024]).then_inc(sems.w_ve, 1)
        vector.wait_ge(sems.w_ve, 35)
        vector.tensor_scalar(out=w.u[0:1, :], in0=w.u[0:1, :],
                             scalar1=EPS, scalar2=None, op0=ALU.add).then_inc(sems.w_ve, 1)
        vector.wait_ge(sems.w_sc, 34)
        vector.reciprocal(out=w.u[0:1, :], in_=pq[2][0:1, 0:1024]).then_inc(sems.w_ve, 1)
        vector.wait_ge(sems.w_rw, 4 * DMA)
        for m in range(32):
            if m >= 2:
                vector.wait_ge([sems.w_s0, sems.w_s1][m % 2], (m // 2) * DMA)
            vector.tensor_sub(out=w.u[:], in0=w.tx[:, 1024 * m:1024 * (m + 1)],
                              in1=w.bb[:]).then_inc(sems.w_ve, 1)
            vector.wait_ge(sems.w_ve, 32 + 5 + 3 * m + 1)
            vector.tensor_mul(out=w.x2[m % 2][:], in0=w.u[:], in1=w.ab[:]).then_inc(sems.w_ve, 1)
            vector.wait_ge(sems.w_ve, 32 + 5 + 3 * m + 2)
            vector.tensor_copy(out=w.tx[:, 1024 * m:1024 * (m + 1)],
                               in_=w.x2[m % 2][:]).then_inc(sems.w_ve, 1)

    W_TOT_VE = 32 + 5 + 96

    f = NS()
    f.wring = [sbuf(f"f_wr{i}", CONST2 + i * 8192, [128, 8 * 512], BF16) for i in range(3)]
    f.hreg = sbuf("f_hreg", R3, [128, KB_DN * 512], BF16)
    f.xs = sbuf("f_xs", STAGE, [KB_GU, 512], BF16)
    f.hs = sbuf("f_hs", STAGE + 1024, [KB_DN, 512], BF16)
    f.nbg = [sbuf(f"f_nbg{i}", STAGE + 2048 + i * 1024, [KB_DN, 512], BF16) for i in range(2)]
    f.gsil = [sbuf(f"f_gsil{i}", STAGE + 4096 + i * 1024, [128, 512], BF16) for i in range(2)]
    f.ind = [sbuf(f"f_ind{i}", STAGE + 6144 + i * 256, [128, KB_DN], BF16) for i in range(2)]
    f.xm = [sbuf(f"f_xm{i}", STAGE + 6656 + i * 2048, [128, 512], F32) for i in range(2)]
    f.t2 = [sbuf(f"f_t2{i}", STAGE + 10752 + i * 2048, [128, 512], F32) for i in range(2)]
    f.t2bf = [sbuf(f"f_t2bf{i}", STAGE + 14848 + i * 1024, [128, 512], BF16) for i in range(2)]
    f.sq = [sbuf(f"f_sq{i}", STAGE + 16896 + i * 1024, [128, 512], BF16) for i in range(2)]
    f.rows2 = sbuf("f_rows2", STAGE + 18944, [128, 512], F32)
    f.ab2 = sbuf("f_ab2", STAGE, [128, 512], F32)
    f.bb2 = sbuf("f_bb2", STAGE + 2048, [128, 512], F32)
    f.t2r = [sbuf(f"f_t2r{i}", STAGE + 4096 + i * 2048, [128, 512], F32) for i in range(2)]
    f.ou = sbuf("f_ou", STAGE + 8704, [128, 512], F32)
    f.ob = [sbuf(f"f_ob{i}", STAGE + 10752 + i * 2048, [128, 512], F32) for i in range(2)]

    fq_sems = [sems.f_q0, sems.f_q1, sems.f_q2]
    fn_sems = [sems.f_n0, sems.f_n1]
    fm_sems = [sems.f_m0, sems.f_m1]
    ft_sems = [sems.f_t0, sems.f_t1]
    fr_sems = [sems.f_r0, sems.f_r1]
    fo_sems = [sems.f_o0, sems.f_o1]

    DN_KG = [8] * 10 + [6]
    F_FILLS_CH = 43 * 4 + 8 * 11
    def f_mm_after_gu_fill(mc, kq):
        return 32 + mc * 134 + 4 + 128 * 0 + (kq + 1) * 32
    def f_mm_after_dn_fill(mc8, kgi):
        base = 32 + 43 * 134
        return base + mc8 * 348 + 4 + 4 * (sum(DN_KG[:kgi]) + DN_KG[kgi])
    F_MM_CH = 32 + 43 * 134 + 8 * 348
    F_VE_XS = 1
    def f_ve_ind(mc, j2):
        return 1 + mc * 4 + j2 + 1
    def f_ve_h(mc, j2):
        return 1 + mc * 4 + 2 + j2 + 1
    F_VE_GU = 1 + 43 * 4
    F_VE_HS = F_VE_GU + 1
    def f_ve_t2(m):
        return F_VE_HS + 2 * m + 1
    def f_ve_t2bf(m):
        return F_VE_HS + 2 * m + 2
    F_VE_DN = F_VE_HS + 64
    F_VE_ROWS = F_VE_DN + 5
    def f_ve_ob(m):
        return F_VE_ROWS + 2 * m + 2
    F_VE_CH = F_VE_ROWS + 64
    F_SC_CH = 86 + 32 + 2
    F_S2_CH = 64

    def f_sync(sync, c):
        mmb = F_MM_CH * c
        veb = F_VE_CH * c
        fqb = F_FILLS_CH * c
        fill_done = []
        for mc in range(MC_GU):
            for kq in range(4):
                fill_done.append(mmb + 32 + mc * 134 + 4 + 32 * (kq + 1))
        for mc8 in range(MC_DN):
            k0 = 0
            for kg in DN_KG:
                k0 += kg
                fill_done.append(mmb + 32 + 43 * 134 + mc8 * 348 + 4 + 4 * k0)
        fq = fqb
        nfill_gu = [26 * c, 25 * c]
        for mc in range(MC_GU):
            if mc >= 2:
                sync.wait_ge(sems.f_mm, mmb + 32 + (mc - 2) * 134 + 4)
            elif c > 0:
                sync.wait_ge(sems.f_mm, mmb)
            sync.dma_start(out=f.nbg[mc % 2][0:KB_GU, :],
                           in_=t.negbg_gu[:, 512 * mc:512 * (mc + 1)]).then_inc(fn_sems[mc % 2], DMA)
            for kq in range(4):
                pf = fq - 3
                if pf >= 0 and pf >= fqb:
                    sync.wait_ge(sems.f_mm, fill_done[pf - fqb])
                elif pf >= 0:
                    sync.wait_ge(sems.f_mm, mmb)
                sync.dma_start(out=f.wring[fq % 3][:],
                               in_=t.wgu_c[mc, 8 * kq:8 * kq + 8].rearrange("k p n -> p k n")).then_inc(fq_sems[fq % 3], DMA)
                fq += 1
        for mc8 in range(MC_DN):
            if mc8 >= 2:
                sync.wait_ge(sems.f_mm, mmb + 32 + 43 * 134 + (mc8 - 2) * 348 + 4)
            else:
                sync.wait_ge(sems.f_mm, mmb + 32 + (42 - (mc8 % 2)) * 134 + 4)
            sync.dma_start(out=f.nbg[mc8 % 2][0:KB_DN, :],
                           in_=t.negbg_dn[:, 512 * mc8:512 * (mc8 + 1)]).then_inc(fn_sems[mc8 % 2], DMA)
            k0 = 0
            for kgi, kg in enumerate(DN_KG):
                pf = fq - 3
                if pf >= fqb:
                    sync.wait_ge(sems.f_mm, fill_done[pf - fqb])
                sync.dma_start(out=f.wring[fq % 3][:, 0:512 * kg],
                               in_=t.wdn_c[mc8, k0:k0 + kg].rearrange("k p n -> p k n")).then_inc(fq_sems[fq % 3], DMA)
                fq += 1
                k0 += kg
            for m2 in range(4):
                m = 4 * mc8 + m2
                if m >= 2:
                    sync.wait_ge(sems.f_ve, veb + f_ve_t2(m - 2))
                elif c > 0:
                    sync.wait_ge(sems.f_ve, veb)
                sync.dma_start(out=f.xm[m % 2][:],
                               in_=t.xmid[128 * m:128 * m + 128, 512 * c:512 * (c + 1)]).then_inc(fm_sems[m % 2], DMA)
            for m2 in range(4):
                m = 4 * mc8 + m2
                sync.wait_ge(sems.f_ve, veb + f_ve_t2(m))
                sync.dma_start(out=t.t2_sp[128 * m:128 * m + 128, :],
                               in_=f.t2[m % 2][:]).then_inc(ft_sems[m % 2], DMA)
        sync.wait_ge(sems.f_ve, veb + F_VE_ROWS)
        sync.dma_start(out=t.rows_sp[6, 0:512], in_=f.ou[0:1, :]).then_inc(sems.f_ab, DMA)
        sync.dma_start(out=t.rows_sp[7, 0:512], in_=f.rows2[0:1, :]).then_inc(sems.f_ab, DMA)
        sync.wait_ge(sems.f_ab, (4 * c + 2) * DMA)
        sync.dma_start(out=f.ab2[:], in_=t.rows_sp[6, 0:512].unsqueeze(0).partition_broadcast(128)).then_inc(sems.f_ab, DMA)
        sync.dma_start(out=f.bb2[:], in_=t.rows_sp[7, 0:512].unsqueeze(0).partition_broadcast(128)).then_inc(sems.f_ab, DMA)
        for m in range(32):
            if m >= 2:
                sync.wait_ge(sems.f_ve, veb + f_ve_ob(m - 2))
            sync.dma_start(out=f.t2r[m % 2][:],
                           in_=t.t2_sp[128 * m:128 * m + 128, :]).then_inc(fr_sems[m % 2], DMA)
            sync.wait_ge(sems.f_ve, veb + f_ve_ob(m) if False else veb + F_VE_ROWS + 2 * m + 2)
            sync.dma_start(out=t.outT[128 * m:128 * m + 128, 512 * c:512 * (c + 1)],
                           in_=f.ob[m % 2][:]).then_inc(fo_sems[m % 2], DMA)

    def f_tensor(tensor, c):
        mmb = F_MM_CH * c
        veb = F_VE_CH * c
        s2b = F_S2_CH * c
        fqb = F_FILLS_CH * c
        fq = fqb
        if c > 0:
            tensor.wait_ge(sems.f_ve, F_VE_CH * c)
        for k in range(32):
            tensor.matmul(out=pq[3][0:KB_GU, 0:512],
                          lhsT=s.ind_gu[:, 32 * k:32 * (k + 1)],
                          rhs=w.tx[:, 1024 * k + 512 * c:1024 * k + 512 * (c + 1)],
                          start=(k == 0), stop=(k == 31)).then_inc(sems.f_mm, 1)
        pj = lambda j2: [pq[0], pq[1]][j2 // 2][:, 512 * (j2 % 2):512 * (j2 % 2) + 512]
        for mc in range(MC_GU):
            tensor.wait_ge(sems.f_ve, veb + F_VE_XS)
            tensor.wait_ge(fn_sems[mc % 2], ([26, 25][mc % 2] * c + mc // 2 + 1) * DMA)
            if mc >= 1:
                tensor.wait_ge(sems.f_ve, veb + f_ve_h(mc - 1, 1))
                tensor.wait_ge(sems.f_sc, F_SC_CH * c + 2 * mc)
            for j2 in range(4):
                tensor.matmul(out=pj(j2), lhsT=f.nbg[mc % 2][0:KB_GU, 128 * j2:128 * (j2 + 1)],
                              rhs=f.xs[:], start=True, stop=False).then_inc(sems.f_mm, 1)
            for kq in range(4):
                tensor.wait_ge(fq_sems[fq % 3], ((fq - fq % 3) // 3 + 1) * DMA)
                for k8 in range(8):
                    k = 8 * kq + k8
                    last = (k == 31)
                    for j2 in range(4):
                        tensor.matmul(out=pj(j2),
                                      lhsT=f.wring[fq % 3][:, 512 * k8 + 128 * j2:512 * k8 + 128 * (j2 + 1)],
                                      rhs=w.tx[:, 1024 * k + 512 * c:1024 * k + 512 * (c + 1)],
                                      start=False, stop=last).then_inc(sems.f_mm, 1)
                fq += 1
            for j2 in range(2):
                jj = 2 * mc + j2
                tensor.wait_ge(sems.f_ve, veb + f_ve_h(mc, j2))
                tensor.matmul(out=pq[2][0:KB_DN, 0:512],
                              lhsT=f.ind[jj % 2][:],
                              rhs=f.hreg[:, 512 * jj:512 * (jj + 1)],
                              start=(jj == 0), stop=(jj == KB_DN - 1)).then_inc(sems.f_mm, 1)
        for mc8 in range(MC_DN):
            tensor.wait_ge(sems.f_ve, veb + F_VE_HS)
            tensor.wait_ge(sems.f_sc, F_SC_CH * c + 86)
            tensor.wait_ge(fn_sems[mc8 % 2], ([26, 25][mc8 % 2] * c + [22, 21][mc8 % 2] + mc8 // 2 + 1) * DMA)
            if mc8 >= 1:
                tensor.wait_ge(sems.f_ve, veb + f_ve_t2(4 * mc8 - 1))
            for j2 in range(4):
                tensor.matmul(out=pj(j2),
                              lhsT=f.nbg[mc8 % 2][0:KB_DN, 128 * j2:128 * (j2 + 1)],
                              rhs=f.hs[:], start=True, stop=False).then_inc(sems.f_mm, 1)
            k0 = 0
            for kgi, kg in enumerate(DN_KG):
                tensor.wait_ge(fq_sems[fq % 3], ((fq - fq % 3) // 3 + 1) * DMA)
                for kk in range(kg):
                    k = k0 + kk
                    last = (k == KB_DN - 1)
                    for j2 in range(4):
                        tensor.matmul(out=pj(j2),
                                      lhsT=f.wring[fq % 3][:, 512 * kk + 128 * j2:512 * kk + 128 * (j2 + 1)],
                                      rhs=f.hreg[:, 512 * k:512 * (k + 1)],
                                      start=False, stop=last).then_inc(sems.f_mm, 1)
                fq += 1
                k0 += kg
            for m2 in range(4):
                m = 4 * mc8 + m2
                tensor.wait_ge(sems.f_ve, veb + f_ve_t2bf(m))
                tensor.wait_ge(sems.f_sc, F_SC_CH * c + 86 + m + 1)
                tensor.matmul(out=pq[3][0:1, 0:512], lhsT=s.ones[:],
                              rhs=f.t2bf[m % 2][:],
                              start=(m == 0), stop=(m == 31)).then_inc(sems.f_s2, 1)
                tensor.matmul(out=pq[3][0:1, 512:1024], lhsT=s.ones[:],
                              rhs=f.sq[m % 2][:],
                              start=(m == 0), stop=(m == 31)).then_inc(sems.f_s2, 1)

    def f_scalar(scalar, c):
        scb = F_SC_CH * c
        veb = F_VE_CH * c
        mmb = F_MM_CH * c
        for mc in range(MC_GU):
            for j2 in range(2):
                jj = 2 * mc + j2
                gpos = mmb + 32 + mc * 134 + 4 + 128
                scalar.wait_ge(sems.f_mm, gpos)
                if jj >= 2:
                    scalar.wait_ge(sems.f_ve, veb + f_ve_h(*divmod(jj - 2, 2)))
                scalar.activation(out=f.gsil[jj % 2][:],
                                  in_=[pq[0], pq[1]][j2][:, 0:512],
                                  func=AF.Silu).then_inc(sems.f_sc, 1)
        for m in range(32):
            scalar.wait_ge(sems.f_ve, veb + f_ve_t2bf(m))
            if m >= 2:
                scalar.wait_ge(sems.f_s2, F_S2_CH * c + 2 * (m - 1))
            scalar.activation(out=f.sq[m % 2][:], in_=f.t2bf[m % 2][:],
                              func=AF.Square).then_inc(sems.f_sc, 1)
        scalar.wait_ge(sems.f_ve, veb + F_VE_DN + 2)
        scalar.activation(out=pq[3][0:1, 0:512], in_=f.rows2[0:1, :],
                          func=AF.Square).then_inc(sems.f_sc, 1)
        scalar.wait_ge(sems.f_ve, veb + F_VE_DN + 4)
        scalar.activation(out=pq[3][0:1, 0:512], in_=f.ou[0:1, :],
                          func=AF.Sqrt).then_inc(sems.f_sc, 1)

    def f_vector(vector, c):
        veb = F_VE_CH * c
        mmb = F_MM_CH * c
        scb = F_SC_CH * c
        pj2 = lambda j2: [pq[0], pq[1]][j2 // 2][:, 512 * (j2 % 2):512 * (j2 % 2) + 512]
        vector.wait_ge(sems.f_mm, mmb + 32)
        vector.tensor_copy(out=f.xs[:], in_=pq[3][0:KB_GU, 0:512]).then_inc(sems.f_ve, 1)
        for mc in range(MC_GU):
            for j2 in range(2):
                jj = 2 * mc + j2
                if jj >= 2:
                    vector.wait_ge(sems.f_mm, mmb + 32 + (mc - 1) * 134 + 133 + j2)
                vector.tensor_scalar(out=f.ind[jj % 2][:], in0=s.iota_row[:, 0:KB_DN],
                                     scalar1=s.gcolf_dn[:, jj:jj + 1], scalar2=None,
                                     op0=ALU.is_equal).then_inc(sems.f_ve, 1)
            for j2 in range(2):
                jj = 2 * mc + j2
                vector.wait_ge(sems.f_sc, scb + 2 * mc + j2 + 1)
                vector.tensor_tensor(out=f.hreg[:, 512 * jj:512 * (jj + 1)],
                                     in0=f.gsil[jj % 2][:],
                                     in1=[pq[0], pq[1]][j2][:, 512:1024],
                                     op=ALU.mult).then_inc(sems.f_ve, 1)
        vector.wait_ge(sems.f_mm, mmb + 32 + 43 * 134)
        vector.tensor_copy(out=f.hs[:], in_=pq[2][0:KB_DN, 0:512]).then_inc(sems.f_ve, 1)
        for mc8 in range(MC_DN):
            for m2 in range(4):
                m = 4 * mc8 + m2
                vector.wait_ge(sems.f_mm, mmb + 32 + 43 * 134 + mc8 * 348 + 4 + 4 * KB_DN)
                vector.wait_ge(fm_sems[m % 2], (16 * c + m // 2 + 1) * DMA)
                if m >= 2:
                    vector.wait_ge(ft_sems[m % 2], (16 * c + (m - 2) // 2 + 1) * DMA)
                elif c > 0:
                    vector.wait_ge(fo_sems[m % 2], 16 * c * DMA)
                vector.tensor_add(out=f.t2[m % 2][:], in0=pj2(m % 4),
                                  in1=f.xm[m % 2][:]).then_inc(sems.f_ve, 1)
                vector.wait_ge(sems.f_ve, veb + f_ve_t2(m))
                if m >= 2:
                    vector.wait_ge(sems.f_sc, F_SC_CH * c + 85 + m)
                    vector.wait_ge(sems.f_s2, F_S2_CH * c + 2 * (m - 1))
                vector.tensor_copy(out=f.t2bf[m % 2][:], in_=f.t2[m % 2][:]).then_inc(sems.f_ve, 1)
        vector.wait_ge(sems.f_s2, F_S2_CH * (c + 1))
        vector.tensor_scalar(out=f.rows2[0:1, :], in0=pq[3][0:1, 0:512],
                             scalar1=1.0 / D, scalar2=None, op0=ALU.mult).then_inc(sems.f_ve, 1)
        vector.tensor_scalar(out=f.ou[0:1, :], in0=pq[3][0:1, 512:1024],
                             scalar1=1.0 / D, scalar2=None, op0=ALU.mult).then_inc(sems.f_ve, 1)
        vector.wait_ge(sems.f_sc, F_SC_CH * c + 119)
        vector.tensor_sub(out=f.ou[0:1, :], in0=f.ou[0:1, :],
                          in1=pq[3][0:1, 0:512]).then_inc(sems.f_ve, 1)
        vector.wait_ge(sems.f_ve, veb + F_VE_DN + 3)
        vector.tensor_scalar(out=f.ou[0:1, :], in0=f.ou[0:1, :],
                             scalar1=EPS, scalar2=None, op0=ALU.add).then_inc(sems.f_ve, 1)
        vector.wait_ge(sems.f_sc, F_SC_CH * (c + 1))
        vector.reciprocal(out=f.ou[0:1, :], in_=pq[3][0:1, 0:512]).then_inc(sems.f_ve, 1)
        vector.wait_ge(sems.f_ab, (4 * c + 4) * DMA)
        for m in range(32):
            vector.wait_ge(fr_sems[m % 2], (16 * c + m // 2 + 1) * DMA)
            if m >= 2:
                vector.wait_ge(fo_sems[m % 2], (16 * c + (m - 2) // 2 + 1) * DMA)
            vector.tensor_sub(out=f.ou[:], in0=f.t2r[m % 2][:],
                              in1=f.bb2[:]).then_inc(sems.f_ve, 1)
            vector.wait_ge(sems.f_ve, veb + F_VE_ROWS + 2 * m + 1)
            if m < 2:
                vector.wait_ge(ft_sems[m % 2], 16 * (c + 1) * DMA)
            vector.tensor_mul(out=f.ob[m % 2][:], in0=f.ou[:],
                              in1=f.ab2[:]).then_inc(sems.f_ve, 1)

    with nc.Block() as block:
        @block.sync
        def _(sync):
            nbar = 0
            if "D" in phases:
                d_sync(sync)
                sync.wait_ge(sems.d_st, d_tot_st * DMA)
                sync.wait_ge(sems.d_ld, d_tot_ld * DMA)
                for j, dwj in enumerate([sems.d_w0, sems.d_w1, sems.d_w2, sems.d_w3]):
                    sync.wait_ge(dwj, (d_tot_tt // 4) * DMA)
                if debug:
                    sync.dma_start(out=t.gbf_dump[:], in_=d.gbf_dn[:]).then_inc(sems.d_l0, DMA)
                    sync.dma_start(out=t.indT_dump[:], in_=d.indT[:]).then_inc(sems.d_l0, DMA)
                    sync.dma_start(out=t.indgu_dump[:], in_=s.ind_gu[:]).then_inc(sems.d_l0, DMA)
                    sync.wait_ge(sems.d_l0, (7 + 3) * DMA)
            nbar += 1
            sync.sem_inc(sems.bar, 1)
            if "P" in phases:
                sync.wait_ge(sems.bar, nbar)
                p_sync(sync)
                if debug:
                    sync.dma_start(out=t.qt_dump[:], in_=p.qt[:]).then_inc(sems.p_rp, DMA)
                    sync.wait_ge(sems.p_rp, DMA)
            nbar += 1
            sync.sem_inc(sems.bar, 1)
            if "A" in phases:
                sync.wait_ge(sems.bar, nbar)
                a_sync(sync)
                sync.wait_ge(sems.a_ve, A_TOT_VE)
                if debug:
                    sync.dma_start(out=t.attn_dump[:], in_=a.attn[:]).then_inc(sems.p_rp, DMA)
                    sync.wait_ge(sems.p_rp, 2 * DMA)
            nbar += 1
            sync.sem_inc(sems.bar, 1)
            if "W" in phases:
                sync.wait_ge(sems.bar, nbar)
                w_sync(sync)
                sync.wait_ge(sems.w_ve, W_TOT_VE)
                sync.wait_ge(sems.w_s0, 16 * DMA)
                sync.wait_ge(sems.w_s1, 16 * DMA)
            nbar += 1
            sync.sem_inc(sems.bar, 1)
            if "F" in phases:
                sync.wait_ge(sems.bar, nbar)
                for cch in range(2):
                    f_sync(sync, cch)
                sync.wait_ge(sems.f_o0, 32 * DMA)
                sync.wait_ge(sems.f_o1, 32 * DMA)

        @block.vector
        def _(vector):
            if "D" in phases:
                d_vector(vector)
            if "P" in phases:
                vector.wait_ge(sems.bar, 1)
                p_vector(vector)
            if "A" in phases:
                vector.wait_ge(sems.bar, 2)
                a_vector(vector)
            if "W" in phases:
                vector.wait_ge(sems.bar, 3)
                w_vector(vector)
            if "F" in phases:
                vector.wait_ge(sems.bar, 4)
                for cch in range(2):
                    f_vector(vector, cch)

        @block.tensor
        def _(tensor):
            if "D" in phases:
                d_tensor(tensor)
            if "P" in phases:
                tensor.wait_ge(sems.bar, 1)
                p_tensor(tensor)
            if "A" in phases:
                tensor.wait_ge(sems.bar, 2)
                a_tensor(tensor)
            if "W" in phases:
                tensor.wait_ge(sems.bar, 3)
                w_tensor(tensor)
            if "F" in phases:
                tensor.wait_ge(sems.bar, 4)
                for cch in range(2):
                    f_tensor(tensor, cch)

        @block.scalar
        def _(scalar):
            if "D" in phases:
                d_scalar(scalar)
            if "P" in phases:
                scalar.wait_ge(sems.bar, 1)
                p_scalar(scalar)
            if "A" in phases:
                scalar.wait_ge(sems.bar, 2)
                a_scalar(scalar)
            if "W" in phases:
                scalar.wait_ge(sems.bar, 3)
                w_scalar(scalar)
            if "F" in phases:
                scalar.wait_ge(sems.bar, 4)
                for cch in range(2):
                    f_scalar(scalar, cch)

        @block.gpsimd
        def _(gpsimd):
            pass

    ctx.close()
    return nc


def prep_inputs(L0_attn_x, L0_attn_wq, L0_attn_wk, L0_attn_wv, L0_attn_wo,
                L0_ffn_qweight_gu, L0_ffn_qzeros_gu, L0_ffn_scales_gu, L0_ffn_g_idx_gu,
                L0_ffn_qweight_down, L0_ffn_qzeros_down, L0_ffn_scales_down, L0_ffn_g_idx_down):
    ph = full_perm(D)
    pi = full_perm(INTER)
    rp = rope_perm()
    blocks = []
    for j in range(KB_DN):
        blocks.append(pi[128 * j:128 * (j + 1)])
        blocks.append(INTER + pi[128 * j:128 * (j + 1)])
    pgu = np.concatenate(blocks)
    wp = []
    for j in range(KB_DN):
        wp.append(np.arange(16 * j, 16 * (j + 1)))
        wp.append(1376 + np.arange(16 * j, 16 * (j + 1)))
    word_perm = np.concatenate(wp)

    wq = np.ascontiguousarray(np.asarray(L0_attn_wq)[ph][:, rp])
    wk = np.ascontiguousarray(np.asarray(L0_attn_wk)[ph][:, rp])
    wv = np.ascontiguousarray(np.asarray(L0_attn_wv)[ph])
    wo = np.ascontiguousarray(np.asarray(L0_attn_wo)[:, ph])
    qw_gu = np.ascontiguousarray(np.asarray(L0_ffn_qweight_gu)[:, pgu])
    sc_gu = np.ascontiguousarray(np.asarray(L0_ffn_scales_gu)[:, pgu])
    gi_gu = np.ascontiguousarray(np.asarray(L0_ffn_g_idx_gu)[ph]).astype(np.int32)
    qw_dn = np.ascontiguousarray(np.asarray(L0_ffn_qweight_down)[:, ph])
    sc_dn = np.ascontiguousarray(np.asarray(L0_ffn_scales_down)[:, ph])
    gi_dn = np.ascontiguousarray(np.asarray(L0_ffn_g_idx_down)[pi]).astype(np.int32)
    qz_gu = np.ascontiguousarray(np.asarray(L0_ffn_qzeros_gu)[:, word_perm])
    qz_dn = np.ascontiguousarray(np.asarray(L0_ffn_qzeros_down))

    inv = 1.0 / (10000.0 ** (np.arange(0, HD, 2)[: HD // 2].astype(np.float32) / HD))
    tt = np.arange(S)
    f = np.outer(tt, inv).astype(np.float32)
    cos_k = np.ascontiguousarray(np.cos(f).T)
    sin_k = np.ascontiguousarray(np.sin(f).T)

    common = dict(
        wq=wq, wk=wk, wv=wv, wo=wo,
        qw_gu=qw_gu, qz_gu=qz_gu, sc_gu=sc_gu,
        qw_dn=qw_dn, qz_dn=qz_dn, sc_dn=sc_dn,
        gi_bc_gu=np.ascontiguousarray(np.broadcast_to(gi_gu[None, :], (KB_GU, D))),
        gi_col_gu=np.ascontiguousarray(gi_gu.reshape(KB_GU, 128).T),
        gi_bc_dn=np.ascontiguousarray(np.broadcast_to(gi_dn[None, :], (KB_DN, INTER))),
        gi_col_dn=np.ascontiguousarray(gi_dn.reshape(KB_DN, 128).T),
        cos_k=cos_k, sin_k=sin_k,
        ones_col=np.ones((128, 1), BF),
        iota_row=np.ascontiguousarray(np.broadcast_to(np.arange(KB_DN, dtype=np.float32), (128, KB_DN))),
        iota_col=np.arange(KB_DN, dtype=np.float32).reshape(KB_DN, 1),
        mask_col=np.ascontiguousarray(
            np.left_shift(np.uint32(15), 4 * (np.arange(128, dtype=np.uint32) // 16)).view(np.int32).reshape(128, 1)),
        pscale_col=(2.0 ** (-4.0 * (np.arange(128) // 16))).astype(np.float32).reshape(128, 1),
    )

    x = np.asarray(L0_attn_x)
    in_maps = []
    for c in range(N_CORES):
        b, half = c // 2, c % 2
        q0 = half * TOKQ
        xTb = np.ascontiguousarray(x[b].T[ph])
        m = dict(common)
        m["xT"] = xTb
        m["x_own"] = np.ascontiguousarray(xTb[:, q0:q0 + TOKQ])
        m["cos_q"] = np.ascontiguousarray(cos_k[:, q0:q0 + TOKQ])
        m["sin_q"] = np.ascontiguousarray(sin_k[:, q0:q0 + TOKQ])
        kv = (np.arange(16)[None, :, None] * 128 + np.arange(128)[:, None, None])
        qq = (q0 + np.arange(TOKQ))[None, None, :]
        mk = (kv <= qq).astype(np.uint8)
        m["mask"] = np.ascontiguousarray(mk.reshape(128, 16 * TOKQ))
        in_maps.append(m)
    return in_maps


def kernel(**inputs):
    in_maps = prep_inputs(**inputs)
    nc = build_program()
    res = run_bass_kernel_spmd(nc, in_maps, core_ids=list(range(N_CORES)))
    inv_ph = np.argsort(full_perm(D))
    out = np.empty((B, S, D), np.float32)
    for c in range(N_CORES):
        b, half = c // 2, c % 2
        out[b, half * TOKQ:(half + 1) * TOKQ, :] = res.results[c]["outT"][inv_ph].T
    return out


# version 03
# speedup vs baseline: 1.3857x; 1.3857x over previous
import contextlib
import numpy as np
import ml_dtypes
import concourse.bass as bass
import concourse.mybir as mybir
from concourse.bass_utils import run_bass_kernel_spmd

F32 = mybir.dt.float32
BF16 = mybir.dt.bfloat16
U32 = mybir.dt.uint32
I32 = mybir.dt.int32
U8 = mybir.dt.uint8
AF = mybir.ActivationFunctionType
ALU = mybir.AluOpType
BF = ml_dtypes.bfloat16

D = 4096
S = 2048
B = 4
H = 32
HD = 128
INTER = 11008
NGU = 2 * INTER
KB_GU = D // 128
KB_DN = INTER // 128
MC_GU = NGU // 512
MC_DN = D // 512
TOKQ = 1024
CH = 512
N_CORES = 8
EPS = 1e-5
SCALE = 1.0 / float(np.sqrt(np.float32(HD)))

DMA = 16
SZ = {F32: 4, U32: 4, I32: 4, BF16: 2, U8: 1}


def perm128():
    p = np.arange(128)
    return 8 * (p % 16) + p // 16


def full_perm(n):
    return (np.arange(0, n, 128)[:, None] + perm128()[None, :]).ravel()


def rope_perm():
    base = np.concatenate([np.arange(0, HD, 2), np.arange(1, HD, 2)])
    return (np.arange(0, D, HD)[:, None] + base[None, :]).ravel()


class NS:
    pass


def build_program(phases=("D", "P", "A", "W", "F"), debug=False):
    nc = bass.Bass("TRN2", target_bir_lowering=False)
    ctx = contextlib.ExitStack()

    def inp(name, shape, dtype):
        return nc.dram_tensor(name, list(shape), dtype, kind="ExternalInput").ap()

    t = NS()
    t.xT = inp("xT", [D, S], F32)
    t.x_own = inp("x_own", [D, TOKQ], F32)
    t.wq = inp("wq", [D, D], F32)
    t.wk = inp("wk", [D, D], F32)
    t.wv = inp("wv", [D, D], F32)
    t.wo = inp("wo", [D, D], F32)
    t.qw_gu = inp("qw_gu", [D // 8, NGU], U32)
    t.qz_gu = inp("qz_gu", [KB_GU, NGU // 8], U32)
    t.sc_gu = inp("sc_gu", [KB_GU, NGU], F32)
    t.qw_dn = inp("qw_dn", [INTER // 8, D], U32)
    t.qz_dn = inp("qz_dn", [KB_DN, D // 8], U32)
    t.sc_dn = inp("sc_dn", [KB_DN, D], F32)
    t.gi_bc_gu = inp("gi_bc_gu", [KB_GU, D], I32)
    t.gi_col_gu = inp("gi_col_gu", [128, KB_GU], I32)
    t.gi_bc_dn = inp("gi_bc_dn", [KB_DN, INTER], I32)
    t.gi_col_dn = inp("gi_col_dn", [128, KB_DN], I32)
    t.cos_k = inp("cos_k", [64, S], F32)
    t.sin_k = inp("sin_k", [64, S], F32)
    t.cos_q = inp("cos_q", [64, TOKQ], F32)
    t.sin_q = inp("sin_q", [64, TOKQ], F32)
    t.mask = inp("mask", [128, 16 * TOKQ], U8)
    t.ones_col = inp("ones_col", [128, 1], BF16)
    t.iota_row = inp("iota_row", [128, KB_DN], F32)
    t.iota_col = inp("iota_col", [KB_DN, 1], F32)
    t.mask_col = inp("mask_col", [128, 1], I32)
    t.pscale_col = inp("pscale_col", [128, 1], F32)

    t.outT = nc.dram_tensor("outT", [D, TOKQ], F32, kind="ExternalOutput").ap()

    dbg = "ExternalOutput" if debug else "Internal"
    t.wgu_c = nc.dram_tensor("wgu_c", [MC_GU, KB_GU, 128, 512], BF16, kind=dbg).ap()
    t.wdn_c = nc.dram_tensor("wdn_c", [MC_DN, KB_DN, 128, 512], BF16, kind=dbg).ap()
    t.negbg_gu = nc.dram_tensor("negbg_gu", [KB_GU, NGU], BF16, kind=dbg).ap()
    t.negbg_dn = nc.dram_tensor("negbg_dn", [KB_DN, D], BF16, kind=dbg).ap()
    if debug:
        t.gbf_dump = nc.dram_tensor("gbf_dump", [KB_DN, INTER], BF16, kind="ExternalOutput").ap()
        t.indT_dump = nc.dram_tensor("indT_dump", [KB_DN, 512], BF16, kind="ExternalOutput").ap()
        t.indgu_dump = nc.dram_tensor("indgu_dump", [128, KB_GU * 32], BF16, kind="ExternalOutput").ap()
        t.qt_dump = nc.dram_tensor("qt_dump", [128, 32 * 1024], BF16, kind="ExternalOutput").ap()
        t.attn_dump = nc.dram_tensor("attn_dump", [128, 32 * 1024], BF16, kind="ExternalOutput").ap()
    t.kT_sp = nc.dram_tensor("kT_sp", [D, S], BF16, kind=dbg).ap()
    t.v_sp = nc.dram_tensor("v_sp", [H, 16, 128, 128], BF16, kind=dbg).ap()
    t.xmid = nc.dram_tensor("xmid", [D, TOKQ], F32, kind="Internal").ap()
    t.t2_sp = nc.dram_tensor("t2_sp", [D, CH], F32, kind="Internal").ap()
    t.rows_sp = nc.dram_tensor("rows_sp", [16, TOKQ], F32, kind="Internal").ap()

    SBUF_BASE = 18432

    def sbuf(name, off, shape, dtype):
        nbytes = int(np.prod(shape[1:])) * SZ[dtype]
        assert SBUF_BASE + off + nbytes <= 229376, f"{name}: {off}+{nbytes} over budget"
        return nc.alloc_sbuf_tensor_at(name, list(shape), dtype, offset=SBUF_BASE + off)

    s = NS()
    s.ones = sbuf("s_ones", 0, [128, 1], BF16)
    s.iota_row = sbuf("s_iota_row", 64, [128, KB_DN], F32)
    s.iota_col = sbuf("s_iota_col", 448, [KB_DN, 1], F32)
    s.mask = sbuf("s_mask", 512, [128, 1], I32)
    s.pscale = sbuf("s_pscale", 3648, [128, 1], F32)
    s.gcolf_gu = sbuf("s_gcolf_gu", 576, [128, KB_GU], F32)
    s.gcolf_dn = sbuf("s_gcolf_dn", 704, [128, KB_DN], F32)
    s.gcol_gu_i = sbuf("s_gcol_gu_i", 1088, [128, KB_GU], I32)
    s.gcol_dn_i = sbuf("s_gcol_dn_i", 1216, [128, KB_DN], I32)
    s.ind_gu = sbuf("s_ind_gu", 1600, [128, KB_GU * 32], BF16)
    s.rows = sbuf("s_rows", 3712, [128, 1536], F32)
    CONST2 = 9856
    STAGE = 34432
    R2 = 55936
    R3 = 121472
    TOP = 188032
    LIM = 208 * 1024

    sems = NS()
    for n in ["d_ld", "d_l0", "d_pr", "d_ve", "d_mm", "d_st", "d_fr", "d_ix",
              "d_q0", "d_q1", "d_w0", "d_w1", "d_w2", "d_w3",
              "p_x0", "p_x1", "p_w0", "p_w1", "p_cs", "p_v0", "p_v1", "p_k0", "p_k1", "p_ch", "d_ch",
              "a_k0", "a_k1", "a_r0", "a_r1", "a_b0", "a_b1", "w_s0", "w_s1",
              "p_ld", "p_wl", "p_cv", "p_mm", "p_ve", "p_st", "p_fr", "p_rp",
              "a_ld", "a_mm", "a_ex", "a_ve", "a_sc", "a_st", "a_fr", "a_sm",
              "w_ld", "w_wl", "w_mm", "w_ve", "w_sc", "w_st", "w_rw",
              "f_ld", "f_wl", "f_mm", "f_ve", "f_sc", "f_st", "f_hs", "f_x", "f_s2",
              "f_q0", "f_q1", "f_q2", "f_n0", "f_n1", "f_m0", "f_m1", "f_t0", "f_t1",
              "f_r0", "f_r1", "f_o0", "f_o1", "f_ab",
              "bar"]:
        setattr(sems, n, ctx.enter_context(nc.semaphore(n)))

    pq = [ctx.enter_context(nc.psum_tensor(f"pq{i}", [128, 1024], F32)) for i in range(4)]

    d = NS()
    d.scf = sbuf("d_scf", STAGE, [KB_DN, 2048], F32)
    d.qzb = sbuf("d_qzb", STAGE + 8192, [KB_DN, 256], U32)
    d.zq = sbuf("d_zq", STAGE + 9216, [KB_DN, 2048], U32)
    d.nbg = sbuf("d_nbg", STAGE + 17408, [KB_DN, 2048], BF16)
    d.qrep = [sbuf(f"d_qrep{i}", R2 + i * 8192, [128, 2048], U32) for i in range(2)]
    d.scbf_gu = sbuf("d_scbf_gu", R2 + 16384, [KB_GU, NGU], BF16)
    d.scbf_dn = sbuf("d_scbf_dn", R3, [KB_DN, D], BF16)
    d.indT = sbuf("d_indT", R3 + 8192, [KB_DN, 128 * 4], BF16)
    d.nib = [sbuf(f"d_nib{i}", R3 + 9280 + i * 2048, [128, 512], U32) for i in range(2)]
    d.wt = [sbuf(f"d_wt{i}", R3 + 13376 + i * 1024, [128, 512], BF16) for i in range(4)]
    d.nsc = sbuf("d_nsc", R3 + 29760, [KB_DN, 2048], F32)
    d.nibf = [sbuf(f"d_nibf{i}", R3 + 17472 + i * 2048, [128, 512], F32) for i in range(2)]
    d.gist = [sbuf(f"d_gist{i}", R3 + 21568 + i * 8192, [KB_DN, 2048], I32) for i in range(2)]
    d.gbf_gu = sbuf("d_gbf_gu", R3 + 37952, [KB_GU, D], BF16)
    d.gbf_dn = sbuf("d_gbf_dn", R3 + 46144, [KB_DN, INTER], BF16)

    gu_chunks = [(i * 2048, 2048) for i in range(10)] + [(20480, 1536)]
    dn_chunks = [(0, 2048), (2048, 2048)]
    prep = [("gu", c0, w) for c0, w in gu_chunks] + [("dn", c0, w) for c0, w in dn_chunks]
    n_prep = len(prep)

    d_main = []
    gix = 0
    for G in range(KB_GU):
        for c0, w in gu_chunks:
            d_main.append(("gu", G, c0, w, gix))
        gix += 1
    for G in range(KB_DN):
        for c0, w in dn_chunks:
            d_main.append(("dn", G, c0, w, gix))
        gix += 1
    n_gix = gix
    ts_cum = []
    c = 0
    for (_, _, _, w, _) in d_main:
        c += w // 512
        ts_cum.append(c)
    d_tot_tt = c
    d_tot_ld = 2 * n_prep
    d_tot_st = n_prep
    g_last_tt = {}
    g_first_it = {}
    for i2, (_, _, _, w, g) in enumerate(d_main):
        g_last_tt[g] = ts_cum[i2]
        g_first_it.setdefault(g, i2)

    def d_sync(sync):
        sync.dma_start(out=s.ones[:], in_=t.ones_col[:]).then_inc(sems.d_l0, DMA)
        sync.dma_start(out=s.iota_row[:], in_=t.iota_row[:]).then_inc(sems.d_l0, DMA)
        sync.dma_start(out=s.iota_col[:], in_=t.iota_col[:]).then_inc(sems.d_l0, DMA)
        sync.dma_start(out=s.mask[:], in_=t.mask_col[:]).then_inc(sems.d_l0, DMA)
        sync.dma_start(out=s.pscale[:], in_=t.pscale_col[:]).then_inc(sems.d_l0, DMA)
        sync.dma_start(out=s.gcol_gu_i[:], in_=t.gi_col_gu[:]).then_inc(sems.d_l0, DMA)
        sync.dma_start(out=s.gcol_dn_i[:], in_=t.gi_col_dn[:]).then_inc(sems.d_l0, DMA)
        gchunks = [("gu", c) for c in range(0, D, 2048)] + [("dn", c) for c in range(0, INTER, 2048)]
        d_sync.gchunks = gchunks
        for gi, (kind, c0) in enumerate(gchunks):
            w2 = min(2048, (D if kind == "gu" else INTER) - c0)
            nb2 = KB_GU if kind == "gu" else KB_DN
            gsrc = t.gi_bc_gu if kind == "gu" else t.gi_bc_dn
            if gi >= 2:
                sync.wait_ge(sems.d_ix, gi)
            sync.dma_start(out=d.gist[gi % 2][0:nb2, 0:w2],
                           in_=gsrc[:, c0:c0 + w2]).then_inc([sems.d_q0, sems.d_q1][gi % 2], DMA)
        for i, (kind, c0, w) in enumerate(prep):
            nb = KB_GU if kind == "gu" else KB_DN
            scsrc = t.sc_gu if kind == "gu" else t.sc_dn
            qzsrc = t.qz_gu if kind == "gu" else t.qz_dn
            sync.wait_ge(sems.d_pr, i)
            sync.dma_start(out=d.scf[0:nb, 0:w], in_=scsrc[:, c0:c0 + w]).then_inc(sems.d_ld, DMA)
            sync.dma_start(out=d.qzb[0:nb, 0:w // 8], in_=qzsrc[:, c0 // 8:(c0 + w) // 8]).then_inc(sems.d_ld, DMA)
            sync.wait_ge(sems.d_pr, i + 1)
            nbgdst = t.negbg_gu if kind == "gu" else t.negbg_dn
            sync.dma_start(out=nbgdst[:, c0:c0 + w], in_=d.nbg[0:nb, 0:w]).then_inc(sems.d_st, DMA)
        for it, (kind, G, c0, w, g) in enumerate(d_main):
            qrep = d.qrep[it % 2]
            dq = [sems.d_q0, sems.d_q1][it % 2]
            qw = t.qw_gu if kind == "gu" else t.qw_dn
            if it >= 2:
                sync.wait_ge(sems.d_ve, ts_cum[it - 2])
            for j in range(8):
                sync.dma_start(out=qrep[16 * j:16 * j + 16, 0:w],
                               in_=qw[16 * G:16 * G + 16, c0:c0 + w]).then_inc(dq, DMA)

    def d_vector(vector):
        vector.wait_ge(sems.d_l0, 7 * DMA)
        vector.tensor_copy(out=s.gcolf_gu[:], in_=s.gcol_gu_i[:])
        vector.tensor_copy(out=s.gcolf_dn[:], in_=s.gcol_dn_i[:]).then_inc(sems.d_ix, 1)
        for gi, (kind, c0) in enumerate(d_sync.gchunks):
            w2 = min(2048, (D if kind == "gu" else INTER) - c0)
            nb2 = KB_GU if kind == "gu" else KB_DN
            gdst = d.gbf_gu if kind == "gu" else d.gbf_dn
            vector.wait_ge([sems.d_q0, sems.d_q1][gi % 2], (gi // 2 + 1) * DMA)
            vector.tensor_copy(out=gdst[0:nb2, c0:c0 + w2],
                               in_=d.gist[gi % 2][0:nb2, 0:w2]).then_inc(sems.d_ix, 1)
        for i, (kind, c0, w) in enumerate(prep):
            nb = KB_GU if kind == "gu" else KB_DN
            scbf = d.scbf_gu if kind == "gu" else d.scbf_dn
            if i == 0:
                vector.wait_ge(sems.d_ix, 9)
            vector.wait_ge(sems.d_ld, (2 * (i + 1)) * DMA)
            if i >= 1:
                vector.wait_ge(sems.d_st, i * DMA)
                vector.wait_ge(sems.d_pr, i)
            vector.tensor_copy(out=scbf[0:nb, c0:c0 + w], in_=d.scf[0:nb, 0:w])
            vector.tensor_scalar(out=d.nsc[0:nb, 0:w], in0=d.scf[0:nb, 0:w],
                                 scalar1=-1.0, scalar2=None, op0=ALU.mult).then_inc(sems.d_ch, 1)
            nblk = w // 128
            for j in range(8):
                src = d.qzb[0:nb, 0:w // 8].rearrange("p (b c) -> p b c", c=16)
                dst = d.zq[0:nb, 0:w].rearrange("p (b c) -> p b c", c=128)[:, :, 16 * j:16 * j + 16]
                ins = vector.tensor_scalar(out=dst, in0=src, scalar1=4 * j, scalar2=15,
                                           op0=ALU.logical_shift_right, op1=ALU.bitwise_and)
                if j == 7:
                    ins.then_inc(sems.d_ch, 1)
            vector.wait_ge(sems.d_ch, 2 * (i + 1))
            vector.scalar_tensor_tensor(out=d.nbg[0:nb, 0:w], in0=d.zq[0:nb, 0:w],
                                        scalar=1.0, in1=d.nsc[0:nb, 0:w],
                                        op0=ALU.add, op1=ALU.mult).then_inc(sems.d_pr, 1)
        vector.wait_ge(sems.d_ix, 10)
        tsc = 0
        for it, (kind, G, c0, w, g) in enumerate(d_main):
            qrep = d.qrep[it % 2]
            nb = KB_GU if kind == "gu" else KB_DN
            if c0 == 0:
                if g >= 4:
                    vector.wait_ge(sems.d_mm, g_last_tt[g - 4])
                gbf = d.gbf_gu if kind == "gu" else d.gbf_dn
                vector.tensor_scalar(out=d.indT[0:nb, 128 * (g % 4):128 * (g % 4) + 128],
                                     in0=gbf[0:nb, 128 * G:128 * G + 128],
                                     scalar1=s.iota_col[0:nb, :], scalar2=None,
                                     op0=ALU.is_equal).then_inc(sems.d_ix, 1)
                if kind == "gu":
                    vector.tensor_scalar(out=s.ind_gu[:, 32 * G:32 * G + 32],
                                         in0=s.iota_row[:, 0:KB_GU],
                                         scalar1=s.gcolf_gu[:, G:G + 1], scalar2=None,
                                         op0=ALU.is_equal)
            vector.wait_ge([sems.d_q0, sems.d_q1][it % 2], (4 + 8 * (it // 2 + 1)) * DMA)
            d_w = [sems.d_w0, sems.d_w1, sems.d_w2, sems.d_w3]
            for u in range(w // 512):
                tsc += 1
                nib = d.nib[tsc % 2]
                nibf = d.nibf[tsc % 2]
                if tsc > 2:
                    vector.wait_ge(sems.d_ch, 2 * n_prep + tsc - 2)
                vector.tensor_scalar(out=nib[:], in0=qrep[:, 512 * u:512 * u + 512],
                                     scalar1=s.mask[:], scalar2=None,
                                     op0=ALU.bitwise_and).then_inc(sems.d_ve, 1)
                vector.wait_ge(sems.d_ve, tsc)
                if tsc > 2:
                    vector.wait_ge(sems.d_fr, tsc - 2)
                vector.tensor_scalar(out=nibf[:], in0=nib[:],
                                     scalar1=s.pscale[:], scalar2=None,
                                     op0=ALU.mult).then_inc(sems.d_ch, 1)
                vector.wait_ge(sems.d_mm, tsc)
                vector.wait_ge(sems.d_ch, 2 * n_prep + tsc)
                if tsc > 4:
                    vector.wait_ge(d_w[(tsc - 1) % 4], ((tsc - 1) // 4) * DMA)
                vector.tensor_tensor(out=d.wt[(tsc - 1) % 4][:], in0=nibf[:],
                                     in1=pq[tsc % 2][:, 0:512], op=ALU.mult).then_inc(sems.d_fr, 1)

    def d_tensor(tensor):
        tensor.wait_ge(sems.d_pr, n_prep)
        ttc = 0
        ixc = 0
        for it, (kind, G, c0, w, g) in enumerate(d_main):
            nb = KB_GU if kind == "gu" else KB_DN
            scbf = d.scbf_gu if kind == "gu" else d.scbf_dn
            if c0 == 0:
                tensor.wait_ge(sems.d_ix, 10 + (g + 1))
            for u in range(w // 512):
                ttc += 1
                if ttc > 2:
                    tensor.wait_ge(sems.d_fr, ttc - 2)
                tensor.matmul(out=pq[ttc % 2][:, 0:512],
                              lhsT=d.indT[0:nb, 128 * (g % 4):128 * (g % 4) + 128],
                              rhs=scbf[0:nb, c0 + 512 * u:c0 + 512 * (u + 1)],
                              start=True, stop=True).then_inc(sems.d_mm, 1)

    def d_scalar(scalar):
        scalar.wait_ge(sems.d_ix, 9)
        scalar.nop()
        scalar.sem_inc(sems.d_ix, 1)
        d_w = [sems.d_w0, sems.d_w1, sems.d_w2, sems.d_w3]
        ttc = 0
        for it, (kind, G, c0, w, g) in enumerate(d_main):
            cache = t.wgu_c if kind == "gu" else t.wdn_c
            for u in range(w // 512):
                ttc += 1
                scalar.wait_ge(sems.d_fr, ttc)
                mc = (c0 + 512 * u) // 512
                scalar.dma_start(out=cache[mc, G], in_=d.wt[(ttc - 1) % 4][:]).then_inc(d_w[(ttc - 1) % 4], DMA)

    p = NS()
    p.cosk = sbuf("p_cosk", CONST2, [64, 1024], F32)
    p.sink = sbuf("p_sink", CONST2 + 4096, [64, 1024], F32)
    p.cosq = sbuf("p_cosq", CONST2 + 8192, [64, 1024], F32)
    p.sinq = sbuf("p_sinq", CONST2 + 12288, [64, 1024], F32)
    p.rt0 = sbuf("p_rt0", CONST2 + 16384, [64, 1024], F32)
    p.rt1 = sbuf("p_rt1", CONST2 + 20480, [64, 1024], F32)
    p.rt2 = sbuf("p_rt2", STAGE + 17408, [64, 1024], F32)
    p.rt3 = sbuf("p_rt3", R3 + 66560, [64, 1024], F32)
    p.xstage = [sbuf(f"p_xst{i}", STAGE + i * 4096, [128, 1024], F32) for i in range(2)]
    p.wstage = [sbuf(f"p_wst{i}", STAGE + 8192 + i * 1024, [128, 256], F32) for i in range(2)]
    p.wbf = [sbuf(f"p_wbf{i}", STAGE + 10240 + i * 512, [128, 256], BF16) for i in range(2)]
    p.kbf = [sbuf(f"p_kbf{i}", STAGE + 11264 + i * 2048, [128, 1024], BF16) for i in range(2)]
    p.vbf = [sbuf(f"p_vbf{i}", STAGE + 15360 + i * 1024, [128, 512], BF16) for i in range(2)]
    p.xh = sbuf("p_xh", R3, [128, 32 * 1024], BF16)
    p.wvbf = sbuf("p_wvbf", R2, [128, 32 * 512], BF16)
    p.qt = sbuf("p_qt", R2, [128, 32 * 1024], BF16)

    p_x = [sems.p_x0, sems.p_x1]
    p_w = [sems.p_w0, sems.p_w1]
    p_v = [sems.p_v0, sems.p_v1]
    p_k = [sems.p_k0, sems.p_k1]
    PLD_H = 802
    PCV_H = 32
    PWL_H = 768
    PMM_H = 4096
    PVE_H = 96
    PST_H = 288
    P_TOT_LD = 2 * PLD_H + 546
    P_TOT_CV = 96
    P_TOT_WL = 2 * PWL_H + 512
    P_TOT_MM = 2 * PMM_H + 2048
    P_TOT_VE = 2 * PVE_H + 32
    P_TOT_ST = 2 * PST_H

    def p_xl_sync(sync, c, xsrc, cos_src, sin_src, cdst, sdst, cv_base):
        sync.dma_start(out=cdst[:], in_=cos_src).then_inc(sems.p_cs, DMA)
        sync.dma_start(out=sdst[:], in_=sin_src).then_inc(sems.p_cs, DMA)
        for k in range(32):
            if k >= 2:
                sync.wait_ge(sems.p_cv, cv_base + k - 1)
            sync.dma_start(out=p.xstage[k % 2][:],
                           in_=xsrc[128 * k:128 * k + 128, :]).then_inc(p_x[k % 2], DMA)

    def p_xl_vector(vector, c, xfill_base):
        for k in range(32):
            vector.wait_ge(p_x[k % 2], (xfill_base + k // 2 + 1) * DMA)
            vector.tensor_copy(out=p.xh[:, 1024 * k:1024 * (k + 1)],
                               in_=p.xstage[k % 2][:]).then_inc(sems.p_cv, 1)
            c["cv"] += 1

    def p_proj_sync(sync, c, w_src, wl_base, ve_base, spill, st_base):
        for mc2 in range(16):
            for k in range(32):
                i = mc2 * 32 + k
                if i >= 2:
                    sync.wait_ge(sems.p_wl, wl_base + i - 1)
                sync.dma_start(out=p.wstage[i % 2][:],
                               in_=w_src[128 * k:128 * k + 128, 256 * mc2:256 * mc2 + 256]).then_inc(p_w[i % 2], DMA)
            if spill is not None and mc2 >= 1:
                for m2 in range(2):
                    j = (mc2 - 1) * 2 + m2
                    sync.wait_ge(sems.p_ve, ve_base + j + 1)
                    sync.dma_start(out=spill(j), in_=p.kbf[j % 2][:]).then_inc(p_k[j % 2], DMA)
                    c["st"] += 1
        if spill is not None:
            for m2 in range(2):
                j = 30 + m2
                sync.wait_ge(sems.p_ve, ve_base + j + 1)
                sync.dma_start(out=spill(j), in_=p.kbf[j % 2][:]).then_inc(p_k[j % 2], DMA)
                c["st"] += 1

    def p_proj_scalar(scalar, c, wfill_base, mm_base):
        for i in range(512):
            scalar.wait_ge(p_w[i % 2], (wfill_base + i // 2 + 1) * DMA)
            if i >= 2:
                scalar.wait_ge(sems.p_mm, mm_base + (i - 1) * 4)
            scalar.copy(out=p.wbf[i % 2][:], in_=p.wstage[i % 2][:]).then_inc(sems.p_wl, 1)
            c["wl"] += 1

    def p_proj_tensor(tensor, c, wl_base, cv_base, ve_base):
        for mc2 in range(16):
            for k in range(32):
                i = mc2 * 32 + k
                tensor.wait_ge(sems.p_wl, wl_base + i + 1)
                if k == 0:
                    tensor.wait_ge(sems.p_cv, cv_base + 32)
                    if mc2 >= 2:
                        tensor.wait_ge(sems.p_ve, ve_base + 2 * (mc2 - 2) + 2)
                    else:
                        tensor.wait_ge(sems.p_ve, ve_base)
                for m2 in range(2):
                    for c5 in range(2):
                        tensor.matmul(out=pq[2 * (mc2 % 2) + m2][:, 512 * c5:512 * (c5 + 1)],
                                      lhsT=p.wbf[i % 2][:, 128 * m2:128 * (m2 + 1)],
                                      rhs=p.xh[:, 1024 * k + 512 * c5:1024 * k + 512 * (c5 + 1)],
                                      start=(k == 0), stop=(k == 31)).then_inc(sems.p_mm, 1)
                        c["mm"] += 1

    def p_proj_vector(vector, c, mm_base, st_base, ve_base, cosg, sing, out_of_j, cs_val):
        for mc2 in range(16):
            for m2 in range(2):
                j = mc2 * 2 + m2
                if j == 0:
                    vector.wait_ge(sems.p_cs, cs_val * DMA)
                vector.wait_ge(sems.p_mm, mm_base + 128 * (mc2 + 1))
                if j > 0:
                    vector.wait_ge(sems.p_ve, ve_base + j)
                dst = out_of_j(j)
                if dst is None:
                    dstap = p.kbf[j % 2]
                    if j >= 2 or st_base > 0:
                        vector.wait_ge(p_k[j % 2], (st_base + j // 2) * DMA)
                else:
                    dstap = dst
                pp = pq[2 * (mc2 % 2) + m2]
                ch = c["ch"]
                vector.tensor_mul(out=p.rt0[:], in0=pp[0:64, :], in1=cosg[:]).then_inc(sems.p_ch, 1)
                vector.tensor_mul(out=p.rt1[:], in0=pp[64:128, :], in1=sing[:]).then_inc(sems.p_ch, 1)
                vector.wait_ge(sems.p_ch, ch + 2)
                vector.tensor_sub(out=dstap[0:64, :], in0=p.rt0[:], in1=p.rt1[:])
                vector.tensor_mul(out=p.rt2[:], in0=pp[0:64, :], in1=sing[:]).then_inc(sems.p_ch, 1)
                vector.tensor_mul(out=p.rt3[:], in0=pp[64:128, :], in1=cosg[:]).then_inc(sems.p_ch, 1)
                vector.wait_ge(sems.p_ch, ch + 4)
                c["ch"] += 4
                vector.tensor_add(out=dstap[64:128, :], in0=p.rt2[:],
                                  in1=p.rt3[:]).then_inc(sems.p_ve, 1)
                c["ve"] += 1

    def p_v_sync(sync, c, th, wl_base, ve_base, st_base):
        for nc8 in range(8):
            if nc8 >= 1:
                for tokm in range(8):
                    e = (nc8 - 1) * 8 + tokm
                    sync.wait_ge(sems.p_ve, ve_base + e + 1)
                    for h4 in range(4):
                        sync.dma_start(out=t.v_sp[4 * (nc8 - 1) + h4, 8 * th + tokm],
                                       in_=p.vbf[e % 2][:, 128 * h4:128 * (h4 + 1)]).then_inc(p_v[e % 2], DMA)
                        c["st"] += 1
            for k in range(32):
                i = nc8 * 32 + k
                if i >= 2:
                    sync.wait_ge(sems.p_wl, wl_base + i - 1)
                sync.dma_start(out=p.xstage[i % 2][:, 0:512],
                               in_=t.wv[128 * k:128 * k + 128, 512 * nc8:512 * nc8 + 512]).then_inc(p_x[i % 2], DMA)
        for tokm in range(8):
            e = 7 * 8 + tokm
            sync.wait_ge(sems.p_ve, ve_base + e + 1)
            for h4 in range(4):
                sync.dma_start(out=t.v_sp[4 * 7 + h4, 8 * th + tokm],
                               in_=p.vbf[e % 2][:, 128 * h4:128 * (h4 + 1)]).then_inc(p_v[e % 2], DMA)
                c["st"] += 1

    def p_v_scalar(scalar, c, xfill_base, mm_base):
        for nc8 in range(8):
            for k in range(32):
                i = nc8 * 32 + k
                scalar.wait_ge(p_x[i % 2], (xfill_base + i // 2 + 1) * DMA)
                if nc8 >= 1:
                    scalar.wait_ge(sems.p_mm, mm_base + 256 * nc8)
                scalar.copy(out=p.wvbf[:, 512 * k:512 * (k + 1)],
                            in_=p.xstage[i % 2][:, 0:512]).then_inc(sems.p_wl, 1)
                c["wl"] += 1

    def p_v_tensor(tensor, c, wl_base, ve_base):
        for nc8 in range(8):
            for tokm in range(8):
                g = nc8 * 8 + tokm
                if g >= 2:
                    tensor.wait_ge(sems.p_ve, ve_base + g - 1)
                else:
                    tensor.wait_ge(sems.p_ve, ve_base)
                for k in range(32):
                    tensor.wait_ge(sems.p_wl, wl_base + nc8 * 32 + k + 1)
                    tensor.matmul(out=pq[0][:, 512 * (g % 2):512 * (g % 2) + 512],
                                  lhsT=p.xh[:, 1024 * k + 128 * tokm:1024 * k + 128 * (tokm + 1)],
                                  rhs=p.wvbf[:, 512 * k:512 * (k + 1)],
                                  start=(k == 0), stop=(k == 31)).then_inc(sems.p_mm, 1)
                    c["mm"] += 1

    def p_v_vector(vector, c, mm_base, th):
        for nc8 in range(8):
            for tokm in range(8):
                e = nc8 * 8 + tokm
                vector.wait_ge(sems.p_mm, mm_base + (e + 1) * 32)
                if e >= 2 or th == 1:
                    vector.wait_ge(p_v[e % 2], 4 * (32 * th + e // 2) * DMA)
                vector.tensor_copy(out=p.vbf[e % 2][:],
                                   in_=pq[0][:, 512 * (e % 2):512 * (e % 2) + 512]).then_inc(sems.p_ve, 1)
                c["ve"] += 1

    def p_sync(sync):
        c = {"ld": 0, "st": 0}
        for th in range(2):
            sl = slice(1024 * th, 1024 * (th + 1))
            cv_base = 32 * th
            wl_base = PWL_H * th
            ve_base = PVE_H * th
            st_base_k = PST_H * th * DMA
            p_xl_sync(sync, c, t.xT[:, sl], t.cos_k[:, sl], t.sin_k[:, sl], p.cosk, p.sink, cv_base)
            def kspill(j, th=th):
                return t.kT_sp.rearrange("(a b) c -> a b c", b=128)[j, :, sl]
            p_proj_sync(sync, c, t.wk, wl_base, ve_base, kspill, PST_H * th)
            p_v_sync(sync, c, th, wl_base + 512, ve_base + 32, PST_H * th + 32)
            if th == 0:
                sync.wait_ge(sems.p_mm, PMM_H)
                sync.wait_ge(sems.p_ve, PVE_H)
                sync.wait_ge(sems.p_wl, PWL_H)
        sync.wait_ge(sems.p_mm, 2 * PMM_H)
        sync.wait_ge(sems.p_wl, 2 * PWL_H)
        sync.dma_start(out=p.cosq[:], in_=t.cos_q[:]).then_inc(sems.p_cs, DMA)
        sync.dma_start(out=p.sinq[:], in_=t.sin_q[:]).then_inc(sems.p_cs, DMA)
        for k in range(32):
            if k >= 2:
                sync.wait_ge(sems.p_cv, 64 + k - 1)
            sync.dma_start(out=p.xstage[k % 2][:],
                           in_=t.x_own[128 * k:128 * k + 128, :]).then_inc(p_x[k % 2], DMA)
        for mc2 in range(16):
            for k in range(32):
                i = mc2 * 32 + k
                if i >= 2:
                    sync.wait_ge(sems.p_wl, 2 * PWL_H + i - 1)
                sync.dma_start(out=p.wstage[i % 2][:],
                               in_=t.wq[128 * k:128 * k + 128, 256 * mc2:256 * mc2 + 256]).then_inc(p_w[i % 2], DMA)
        sync.wait_ge(p_k[0], 32 * DMA)
        sync.wait_ge(p_k[1], 32 * DMA)
        sync.wait_ge(p_v[0], 2 * 32 * 4 * DMA)
        sync.wait_ge(p_v[1], 2 * 32 * 4 * DMA)
        sync.wait_ge(sems.p_ve, P_TOT_VE)
        sync.wait_ge(p_x[0], 304 * DMA)
        sync.wait_ge(p_x[1], 304 * DMA)
        sync.wait_ge(p_w[0], 768 * DMA)
        sync.wait_ge(p_w[1], 768 * DMA)
        sync.wait_ge(sems.p_cs, 6 * DMA)

    def p_xl_vector2(vector, xfill_base):
        for k in range(32):
            vector.wait_ge(p_x[k % 2], (xfill_base + k // 2 + 1) * DMA)
            vector.tensor_copy(out=p.xh[:, 1024 * k:1024 * (k + 1)],
                               in_=p.xstage[k % 2][:]).then_inc(sems.p_cv, 1)

    def p_vector(vector):
        c = {"cv": 0, "ve": 0, "ch": 0}
        for th in range(2):
            p_xl_vector(vector, c, 144 * th)
            p_proj_vector(vector, c, PMM_H * th, 16 * th, PVE_H * th,
                          p.cosk, p.sink, lambda j: None, 2 * (th + 1))
            p_v_vector(vector, c, PMM_H * th + 2048, th)
        p_xl_vector2(vector, 288)
        p_proj_vector(vector, c, 2 * PMM_H, 0, 2 * PVE_H,
                      p.cosq, p.sinq,
                      lambda j: p.qt[:, 1024 * j:1024 * (j + 1)], 6)

    def p_scalar(scalar):
        c = {"wl": 0}
        for th in range(2):
            p_proj_scalar(scalar, c, 256 * th, PMM_H * th)
            p_v_scalar(scalar, c, 144 * th + 16, PMM_H * th + 2048)
        for i in range(512):
            scalar.wait_ge(p_w[i % 2], (512 + i // 2 + 1) * DMA)
            if i >= 2:
                scalar.wait_ge(sems.p_mm, 2 * PMM_H + (i - 1) * 4)
            scalar.copy(out=p.wbf[i % 2][:], in_=p.wstage[i % 2][:]).then_inc(sems.p_wl, 1)

    def p_tensor(tensor):
        c = {"mm": 0}
        for th in range(2):
            p_proj_tensor(tensor, c, PWL_H * th, 32 * th, PVE_H * th)
            p_v_tensor(tensor, c, PWL_H * th + 512, PVE_H * th + 32)
        tensor.wait_ge(sems.p_cv, 96)
        p_proj_tensor(tensor, c, 2 * PWL_H, 64, 2 * PVE_H)

    a = NS()
    a.kh = [sbuf(f"a_kh{i}", R3 + i * 4096, [128, 2048], BF16) for i in range(2)]
    a.vh = [sbuf(f"a_vh{i}", R3 + 8192 + i * 4096, [128, 2048], BF16) for i in range(2)]
    a.e = [sbuf(f"a_e{i}", R3 + 16384 + i * 1024, [128, 512], BF16) for i in range(3)]
    a.attn = sbuf("a_attn", R3 + 19456, [128, 32 * 1024], BF16)
    a.mask = sbuf("a_mask", STAGE, [128, 16 * 1024], U8)
    a.recb = [sbuf(f"a_recb{i}", STAGE + 16384 + i * 2048, [128, 512], F32) for i in range(2)]

    A_SEQ = ["S0", "S1"]
    for k in range(14):
        A_SEQ += [f"sum{k}", f"AV{k}", f"S{k + 2}"]
    A_SEQ += ["sum14", "AV14", "sum15", "AV15"]
    A_POS = {tag: i + 1 for i, tag in enumerate(A_SEQ)}
    A_MM_G = 48
    A_VE_G = 18
    A_EX_G = 16

    a_kv = [sems.a_k0, sems.a_k1]
    a_r = [sems.a_r0, sems.a_r1]
    a_b = [sems.a_b0, sems.a_b1]

    def a_sync(sync):
        sync.dma_start(out=a.mask[:], in_=t.mask[:]).then_inc(sems.a_sm, DMA)
        for h in range(H):
            if h >= 2:
                sync.wait_ge(sems.a_mm, 96 * (h - 1))
            sync.dma_start(out=a.kh[h % 2][:], in_=t.kT_sp[128 * h:128 * h + 128, :]).then_inc(a_kv[h % 2], DMA)
            sync.dma_start(out=a.vh[h % 2][:],
                           in_=t.v_sp[h].rearrange("t p d -> p t d")).then_inc(a_kv[h % 2], DMA)
            for qc in range(2):
                idx = 2 * h + qc
                sync.wait_ge(sems.a_ve, A_VE_G * idx + 17)
                if idx >= 2:
                    sync.wait_ge(sems.a_ve, A_VE_G * (idx - 1))
                sync.dma_start(out=t.rows_sp[idx % 4, 0:512],
                               in_=s.rows[0:1, 512 * (idx % 3):512 * (idx % 3) + 512]).then_inc(a_r[idx % 2], DMA)
                sync.wait_ge(a_r[idx % 2], (idx // 2 + 1) * DMA)
                sync.dma_start(out=a.recb[idx % 2][:],
                               in_=t.rows_sp[idx % 4, 0:512].unsqueeze(0).partition_broadcast(128)).then_inc(a_b[idx % 2], DMA)

    def a_tensor(tensor):
        for h in range(H):
            tensor.wait_ge(a_kv[h % 2], 2 * (h // 2 + 1) * DMA)
            for qc in range(2):
                idx = 2 * h + qc
                base = A_MM_G * idx
                par = idx % 2
                for tag in A_SEQ:
                    kind = "S" if tag[0] == "S" else ("sum" if tag[:3] == "sum" else "AV")
                    k = int(tag[len(kind):])
                    if kind == "S":
                        pS = [pq[0][:, 0:512], pq[0][:, 512:1024], pq[1][:, 0:512]][k % 3]
                        if k >= 3:
                            tensor.wait_ge(sems.a_ex, A_EX_G * idx + k - 2)
                        elif idx >= 1:
                            tensor.wait_ge(sems.a_ex, A_EX_G * idx)
                        tensor.matmul(out=pS, lhsT=a.kh[h % 2][:, 128 * k:128 * k + 128],
                                      rhs=p.qt[:, 1024 * h + 512 * qc:1024 * h + 512 * (qc + 1)],
                                      start=True, stop=True).then_inc(sems.a_mm, 1)
                    elif kind == "sum":
                        tensor.wait_ge(sems.a_ve, A_VE_G * idx + k + 1)
                        if k == 0 and idx >= 2:
                            tensor.wait_ge(sems.a_ve, A_VE_G * (idx - 1))
                        tensor.matmul(out=pq[3][0:1, 512 * par:512 * par + 512],
                                      lhsT=s.ones[:], rhs=a.e[k % 3][:],
                                      start=(k == 0), stop=(k == 15)).then_inc(sems.a_mm, 1)
                    else:
                        tensor.matmul(out=pq[2][:, 512 * par:512 * par + 512],
                                      lhsT=a.vh[h % 2][:, 128 * k:128 * k + 128],
                                      rhs=a.e[k % 3][:],
                                      start=(k == 0), stop=(k == 15)).then_inc(sems.a_mm, 1)

    def a_scalar(scalar):
        for h in range(H):
            for qc in range(2):
                idx = 2 * h + qc
                for k in range(16):
                    pS = [pq[0][:, 0:512], pq[0][:, 512:1024], pq[1][:, 0:512]][k % 3]
                    scalar.wait_ge(sems.a_mm, A_MM_G * idx + A_POS[f"S{k}"])
                    if k >= 3:
                        scalar.wait_ge(sems.a_mm, A_MM_G * idx + A_POS[f"AV{k - 3}"])
                    elif idx >= 1:
                        scalar.wait_ge(sems.a_mm, A_MM_G * (idx - 1) + A_POS[f"AV{k + 13}"])
                    scalar.activation(out=a.e[k % 3][:], in_=pS, func=AF.Exp,
                                      scale=SCALE).then_inc(sems.a_ex, 1)

    def a_vector(vector):
        vector.wait_ge(sems.a_sm, DMA)
        for h in range(H):
            for qc in range(2):
                idx = 2 * h + qc
                par = idx % 2
                for k in range(16):
                    vector.wait_ge(sems.a_ex, A_EX_G * idx + k + 1)
                    msl = a.mask[:, 1024 * k + 512 * qc:1024 * k + 512 * (qc + 1)]
                    vector.tensor_tensor(out=a.e[k % 3][:], in0=a.e[k % 3][:],
                                         in1=msl, op=ALU.mult).then_inc(sems.a_ve, 1)
                vector.wait_ge(sems.a_mm, A_MM_G * idx + A_POS["sum15"])
                if idx >= 3:
                    vector.wait_ge(a_r[(idx - 3) % 2], ((idx - 3) // 2 + 1) * DMA)
                vector.reciprocal(out=s.rows[0:1, 512 * (idx % 3):512 * (idx % 3) + 512],
                                  in_=pq[3][0:1, 512 * par:512 * par + 512]).then_inc(sems.a_ve, 1)
                vector.wait_ge(a_b[idx % 2], (idx // 2 + 1) * DMA)
                vector.wait_ge(sems.a_mm, A_MM_G * idx + A_POS["AV15"])
                vector.tensor_tensor(out=a.attn[:, 1024 * h + 512 * qc:1024 * h + 512 * (qc + 1)],
                                     in0=pq[2][:, 512 * par:512 * par + 512],
                                     in1=a.recb[idx % 2][:], op=ALU.mult).then_inc(sems.a_ve, 1)

    A_TOT_VE = A_VE_G * 64
    A_TOT_MM = A_MM_G * 64

    w = NS()
    w.tx = p.qt
    w.ab = sbuf("w_ab", CONST2, [128, 1024], F32)
    w.bb = sbuf("w_bb", CONST2 + 4096, [128, 1024], F32)
    w.u = sbuf("w_u", CONST2 + 8192, [128, 1024], F32)
    w.x2 = [sbuf(f"w_x2{i}", CONST2 + 12288 + i * 4096, [128, 1024], F32) for i in range(2)]
    w.rows = sbuf("w_rows", CONST2 + 20480, [128, 1024], F32)

    def w_sync(sync):
        for mc2 in range(16):
            for m2 in range(2):
                m = 2 * mc2 + m2
                if m >= 2:
                    sync.wait_ge(sems.w_ve, m - 1)
                sync.dma_start(out=p.xstage[m % 2][:],
                               in_=t.x_own[128 * m:128 * m + 128, :]).then_inc(p_x[m % 2], DMA)
            for k in range(32):
                i = mc2 * 32 + k
                if i >= 2:
                    sync.wait_ge(sems.w_wl, i - 1)
                sync.dma_start(out=p.wstage[i % 2][:],
                               in_=t.wo[128 * k:128 * k + 128, 256 * mc2:256 * mc2 + 256]).then_inc(p_w[i % 2], DMA)
        sync.wait_ge(sems.w_ve, 37)
        sync.dma_start(out=t.rows_sp[4, :], in_=w.u[0:1, :]).then_inc(sems.w_rw, DMA)
        sync.dma_start(out=t.rows_sp[5, :], in_=w.rows[0:1, :]).then_inc(sems.w_rw, DMA)
        sync.wait_ge(sems.w_rw, 2 * DMA)
        sync.dma_start(out=w.ab[:], in_=t.rows_sp[4, :].unsqueeze(0).partition_broadcast(128)).then_inc(sems.w_rw, DMA)
        sync.dma_start(out=w.bb[:], in_=t.rows_sp[5, :].unsqueeze(0).partition_broadcast(128)).then_inc(sems.w_rw, DMA)
        for m in range(32):
            sync.wait_ge(sems.w_ve, 32 + 5 + 3 * m + 2)
            sync.dma_start(out=t.xmid[128 * m:128 * m + 128, :],
                           in_=w.x2[m % 2][:]).then_inc([sems.w_s0, sems.w_s1][m % 2], DMA)

    def w_scalar(scalar):
        for mc2 in range(16):
            for k in range(32):
                i = mc2 * 32 + k
                scalar.wait_ge(p_w[i % 2], (768 + i // 2 + 1) * DMA)
                if i >= 2:
                    scalar.wait_ge(sems.w_mm, (i - 1) * 4)
                scalar.copy(out=p.wbf[i % 2][:], in_=p.wstage[i % 2][:]).then_inc(sems.w_wl, 1)
            for m2 in range(2):
                m = (mc2 - 1) * 2 + m2
                if 0 <= m:
                    scalar.wait_ge(sems.w_ve, m + 1)
                    if m >= 2:
                        scalar.wait_ge(sems.w_st, 4 * (m - 1))
                    scalar.activation(out=p.kbf[m % 2][:], in_=w.tx[:, 1024 * m:1024 * (m + 1)],
                                      func=AF.Square).then_inc(sems.w_sc, 1)
        for m in (30, 31):
            scalar.wait_ge(sems.w_ve, m + 1)
            scalar.wait_ge(sems.w_st, 4 * (m - 1))
            scalar.activation(out=p.kbf[m % 2][:], in_=w.tx[:, 1024 * m:1024 * (m + 1)],
                              func=AF.Square).then_inc(sems.w_sc, 1)
        scalar.wait_ge(sems.w_ve, 34)
        scalar.activation(out=pq[2][0:1, 0:1024], in_=w.rows[0:1, :],
                          func=AF.Square).then_inc(sems.w_sc, 1)
        scalar.wait_ge(sems.w_ve, 36)
        scalar.activation(out=pq[2][0:1, 0:1024], in_=w.u[0:1, :],
                          func=AF.Sqrt).then_inc(sems.w_sc, 1)

    def w_tensor(tensor):
        for mc2 in range(16):
            for k in range(32):
                i = mc2 * 32 + k
                tensor.wait_ge(sems.w_wl, i + 1)
                if k == 0:
                    if mc2 >= 1:
                        tensor.wait_ge(sems.w_ve, 2 * mc2)
                for m2 in range(2):
                    for c5 in range(2):
                        tensor.matmul(out=pq[m2][:, 512 * c5:512 * (c5 + 1)],
                                      lhsT=p.wbf[i % 2][:, 128 * m2:128 * (m2 + 1)],
                                      rhs=a.attn[:, 1024 * k + 512 * c5:1024 * k + 512 * (c5 + 1)],
                                      start=(k == 0), stop=(k == 31)).then_inc(sems.w_mm, 1)
            for m2 in range(2):
                m = (mc2 - 1) * 2 + m2
                if 0 <= m < 32:
                    tensor.wait_ge(sems.w_ve, m + 1)
                    tensor.wait_ge(sems.w_sc, m + 1)
                    for c5 in range(2):
                        tensor.matmul(out=pq[2][0:1, 512 * c5:512 * (c5 + 1)],
                                      lhsT=s.ones[:],
                                      rhs=w.tx[:, 1024 * m + 512 * c5:1024 * m + 512 * (c5 + 1)],
                                      start=(m == 0), stop=(m == 31)).then_inc(sems.w_st, 1)
                        tensor.matmul(out=pq[3][0:1, 512 * c5:512 * (c5 + 1)],
                                      lhsT=s.ones[:],
                                      rhs=p.kbf[m % 2][:, 512 * c5:512 * (c5 + 1)],
                                      start=(m == 0), stop=(m == 31)).then_inc(sems.w_st, 1)
        for m in (30, 31):
            tensor.wait_ge(sems.w_ve, m + 1)
            tensor.wait_ge(sems.w_sc, m + 1)
            for c5 in range(2):
                tensor.matmul(out=pq[2][0:1, 512 * c5:512 * (c5 + 1)],
                              lhsT=s.ones[:],
                              rhs=w.tx[:, 1024 * m + 512 * c5:1024 * m + 512 * (c5 + 1)],
                              start=(m == 0), stop=(m == 31)).then_inc(sems.w_st, 1)
                tensor.matmul(out=pq[3][0:1, 512 * c5:512 * (c5 + 1)],
                              lhsT=s.ones[:],
                              rhs=p.kbf[m % 2][:, 512 * c5:512 * (c5 + 1)],
                              start=(m == 0), stop=(m == 31)).then_inc(sems.w_st, 1)

    W_TOT_MM = 2048

    def w_vector(vector):
        for mc2 in range(16):
            for m2 in range(2):
                m = mc2 * 2 + m2
                vector.wait_ge(sems.w_mm, 128 * (mc2 + 1))
                vector.wait_ge(p_x[m % 2], (304 + m // 2 + 1) * DMA)
                vector.tensor_add(out=w.tx[:, 1024 * m:1024 * (m + 1)],
                                  in0=pq[m2][:, 0:1024],
                                  in1=p.xstage[m % 2][:]).then_inc(sems.w_ve, 1)
        vector.wait_ge(sems.w_mm, W_TOT_MM)
        vector.wait_ge(sems.w_st, 128)
        vector.tensor_scalar(out=w.rows[0:1, :], in0=pq[2][0:1, 0:1024],
                             scalar1=1.0 / D, scalar2=None, op0=ALU.mult).then_inc(sems.w_ve, 1)
        vector.tensor_scalar(out=w.u[0:1, :], in0=pq[3][0:1, 0:1024],
                             scalar1=1.0 / D, scalar2=None, op0=ALU.mult).then_inc(sems.w_ve, 1)
        vector.wait_ge(sems.w_sc, 33)
        vector.tensor_sub(out=w.u[0:1, :], in0=w.u[0:1, :],
                          in1=pq[2][0:1, 0:1024]).then_inc(sems.w_ve, 1)
        vector.wait_ge(sems.w_ve, 35)
        vector.tensor_scalar(out=w.u[0:1, :], in0=w.u[0:1, :],
                             scalar1=EPS, scalar2=None, op0=ALU.add).then_inc(sems.w_ve, 1)
        vector.wait_ge(sems.w_sc, 34)
        vector.reciprocal(out=w.u[0:1, :], in_=pq[2][0:1, 0:1024]).then_inc(sems.w_ve, 1)
        vector.wait_ge(sems.w_rw, 4 * DMA)
        for m in range(32):
            if m >= 2:
                vector.wait_ge([sems.w_s0, sems.w_s1][m % 2], (m // 2) * DMA)
            vector.tensor_sub(out=w.u[:], in0=w.tx[:, 1024 * m:1024 * (m + 1)],
                              in1=w.bb[:]).then_inc(sems.w_ve, 1)
            vector.wait_ge(sems.w_ve, 32 + 5 + 3 * m + 1)
            vector.tensor_mul(out=w.x2[m % 2][:], in0=w.u[:], in1=w.ab[:]).then_inc(sems.w_ve, 1)
            vector.wait_ge(sems.w_ve, 32 + 5 + 3 * m + 2)
            vector.tensor_copy(out=w.tx[:, 1024 * m:1024 * (m + 1)],
                               in_=w.x2[m % 2][:]).then_inc(sems.w_ve, 1)

    W_TOT_VE = 32 + 5 + 96

    f = NS()
    f.wring = [sbuf(f"f_wr{i}", CONST2 + i * 8192, [128, 8 * 512], BF16) for i in range(3)]
    f.hreg = sbuf("f_hreg", R3, [128, KB_DN * 512], BF16)
    f.xs = sbuf("f_xs", STAGE, [KB_GU, 512], BF16)
    f.hs = sbuf("f_hs", STAGE + 1024, [KB_DN, 512], BF16)
    f.nbg = [sbuf(f"f_nbg{i}", STAGE + 2048 + i * 1024, [KB_DN, 512], BF16) for i in range(2)]
    f.gsil = [sbuf(f"f_gsil{i}", STAGE + 4096 + i * 1024, [128, 512], BF16) for i in range(2)]
    f.ind = [sbuf(f"f_ind{i}", STAGE + 6144 + i * 256, [128, KB_DN], BF16) for i in range(2)]
    f.xm = [sbuf(f"f_xm{i}", STAGE + 6656 + i * 2048, [128, 512], F32) for i in range(2)]
    f.t2 = [sbuf(f"f_t2{i}", STAGE + 10752 + i * 2048, [128, 512], F32) for i in range(2)]
    f.t2bf = [sbuf(f"f_t2bf{i}", STAGE + 14848 + i * 1024, [128, 512], BF16) for i in range(2)]
    f.sq = [sbuf(f"f_sq{i}", STAGE + 16896 + i * 1024, [128, 512], BF16) for i in range(2)]
    f.rows2 = sbuf("f_rows2", STAGE + 18944, [128, 512], F32)
    f.ab2 = sbuf("f_ab2", STAGE, [128, 512], F32)
    f.bb2 = sbuf("f_bb2", STAGE + 2048, [128, 512], F32)
    f.t2r = [sbuf(f"f_t2r{i}", STAGE + 4096 + i * 2048, [128, 512], F32) for i in range(2)]
    f.ou = sbuf("f_ou", STAGE + 8704, [128, 512], F32)
    f.ob = [sbuf(f"f_ob{i}", STAGE + 10752 + i * 2048, [128, 512], F32) for i in range(2)]

    fq_sems = [sems.f_q0, sems.f_q1, sems.f_q2]
    fn_sems = [sems.f_n0, sems.f_n1]
    fm_sems = [sems.f_m0, sems.f_m1]
    ft_sems = [sems.f_t0, sems.f_t1]
    fr_sems = [sems.f_r0, sems.f_r1]
    fo_sems = [sems.f_o0, sems.f_o1]

    DN_KG = [8] * 10 + [6]
    F_FILLS_CH = 43 * 4 + 8 * 11
    def f_mm_after_gu_fill(mc, kq):
        return 32 + mc * 134 + 4 + 128 * 0 + (kq + 1) * 32
    def f_mm_after_dn_fill(mc8, kgi):
        base = 32 + 43 * 134
        return base + mc8 * 348 + 4 + 4 * (sum(DN_KG[:kgi]) + DN_KG[kgi])
    F_MM_CH = 32 + 43 * 134 + 8 * 348
    F_VE_XS = 1
    def f_ve_ind(mc, j2):
        return 1 + mc * 4 + j2 + 1
    def f_ve_h(mc, j2):
        return 1 + mc * 4 + 2 + j2 + 1
    F_VE_GU = 1 + 43 * 4
    F_VE_HS = F_VE_GU + 1
    def f_ve_t2(m):
        return F_VE_HS + 2 * m + 1
    def f_ve_t2bf(m):
        return F_VE_HS + 2 * m + 2
    F_VE_DN = F_VE_HS + 64
    F_VE_ROWS = F_VE_DN + 5
    def f_ve_ob(m):
        return F_VE_ROWS + 2 * m + 2
    F_VE_CH = F_VE_ROWS + 64
    F_SC_CH = 86 + 32 + 2
    F_S2_CH = 64

    def f_sync(sync, c):
        mmb = F_MM_CH * c
        veb = F_VE_CH * c
        fqb = F_FILLS_CH * c
        fill_done = []
        for mc in range(MC_GU):
            for kq in range(4):
                fill_done.append(mmb + 32 + mc * 134 + 4 + 32 * (kq + 1))
        for mc8 in range(MC_DN):
            k0 = 0
            for kg in DN_KG:
                k0 += kg
                fill_done.append(mmb + 32 + 43 * 134 + mc8 * 348 + 4 + 4 * k0)
        fq = fqb
        nfill_gu = [26 * c, 25 * c]
        for mc in range(MC_GU):
            if mc >= 2:
                sync.wait_ge(sems.f_mm, mmb + 32 + (mc - 2) * 134 + 4)
            elif c > 0:
                sync.wait_ge(sems.f_mm, mmb)
            sync.dma_start(out=f.nbg[mc % 2][0:KB_GU, :],
                           in_=t.negbg_gu[:, 512 * mc:512 * (mc + 1)]).then_inc(fn_sems[mc % 2], DMA)
            for kq in range(4):
                pf = fq - 3
                if pf >= 0 and pf >= fqb:
                    sync.wait_ge(sems.f_mm, fill_done[pf - fqb])
                elif pf >= 0:
                    sync.wait_ge(sems.f_mm, mmb)
                sync.dma_start(out=f.wring[fq % 3][:],
                               in_=t.wgu_c[mc, 8 * kq:8 * kq + 8].rearrange("k p n -> p k n")).then_inc(fq_sems[fq % 3], DMA)
                fq += 1
        for mc8 in range(MC_DN):
            if mc8 >= 2:
                sync.wait_ge(sems.f_mm, mmb + 32 + 43 * 134 + (mc8 - 2) * 348 + 4)
            else:
                sync.wait_ge(sems.f_mm, mmb + 32 + (42 - (mc8 % 2)) * 134 + 4)
            sync.dma_start(out=f.nbg[mc8 % 2][0:KB_DN, :],
                           in_=t.negbg_dn[:, 512 * mc8:512 * (mc8 + 1)]).then_inc(fn_sems[mc8 % 2], DMA)
            k0 = 0
            for kgi, kg in enumerate(DN_KG):
                pf = fq - 3
                if pf >= fqb:
                    sync.wait_ge(sems.f_mm, fill_done[pf - fqb])
                sync.dma_start(out=f.wring[fq % 3][:, 0:512 * kg],
                               in_=t.wdn_c[mc8, k0:k0 + kg].rearrange("k p n -> p k n")).then_inc(fq_sems[fq % 3], DMA)
                fq += 1
                k0 += kg
            for m2 in range(4):
                m = 4 * mc8 + m2
                if m >= 2:
                    sync.wait_ge(sems.f_ve, veb + f_ve_t2(m - 2))
                elif c > 0:
                    sync.wait_ge(sems.f_ve, veb)
                sync.dma_start(out=f.xm[m % 2][:],
                               in_=t.xmid[128 * m:128 * m + 128, 512 * c:512 * (c + 1)]).then_inc(fm_sems[m % 2], DMA)
            for m2 in range(4):
                m = 4 * mc8 + m2
                sync.wait_ge(sems.f_ve, veb + f_ve_t2(m))
                sync.dma_start(out=t.t2_sp[128 * m:128 * m + 128, :],
                               in_=f.t2[m % 2][:]).then_inc(ft_sems[m % 2], DMA)
        sync.wait_ge(sems.f_ve, veb + F_VE_ROWS)
        sync.dma_start(out=t.rows_sp[6, 0:512], in_=f.ou[0:1, :]).then_inc(sems.f_ab, DMA)
        sync.dma_start(out=t.rows_sp[7, 0:512], in_=f.rows2[0:1, :]).then_inc(sems.f_ab, DMA)
        sync.wait_ge(sems.f_ab, (4 * c + 2) * DMA)
        sync.dma_start(out=f.ab2[:], in_=t.rows_sp[6, 0:512].unsqueeze(0).partition_broadcast(128)).then_inc(sems.f_ab, DMA)
        sync.dma_start(out=f.bb2[:], in_=t.rows_sp[7, 0:512].unsqueeze(0).partition_broadcast(128)).then_inc(sems.f_ab, DMA)
        for m in range(32):
            if m >= 2:
                sync.wait_ge(sems.f_ve, veb + f_ve_ob(m - 2))
            sync.dma_start(out=f.t2r[m % 2][:],
                           in_=t.t2_sp[128 * m:128 * m + 128, :]).then_inc(fr_sems[m % 2], DMA)
            sync.wait_ge(sems.f_ve, veb + f_ve_ob(m) if False else veb + F_VE_ROWS + 2 * m + 2)
            sync.dma_start(out=t.outT[128 * m:128 * m + 128, 512 * c:512 * (c + 1)],
                           in_=f.ob[m % 2][:]).then_inc(fo_sems[m % 2], DMA)

    def f_tensor(tensor, c):
        mmb = F_MM_CH * c
        veb = F_VE_CH * c
        s2b = F_S2_CH * c
        fqb = F_FILLS_CH * c
        fq = fqb
        if c > 0:
            tensor.wait_ge(sems.f_ve, F_VE_CH * c)
        for k in range(32):
            tensor.matmul(out=pq[3][0:KB_GU, 0:512],
                          lhsT=s.ind_gu[:, 32 * k:32 * (k + 1)],
                          rhs=w.tx[:, 1024 * k + 512 * c:1024 * k + 512 * (c + 1)],
                          start=(k == 0), stop=(k == 31)).then_inc(sems.f_mm, 1)
        pj = lambda j2: [pq[0], pq[1]][j2 // 2][:, 512 * (j2 % 2):512 * (j2 % 2) + 512]
        for mc in range(MC_GU):
            tensor.wait_ge(sems.f_ve, veb + F_VE_XS)
            tensor.wait_ge(fn_sems[mc % 2], ([26, 25][mc % 2] * c + mc // 2 + 1) * DMA)
            if mc >= 1:
                tensor.wait_ge(sems.f_ve, veb + f_ve_h(mc - 1, 1))
                tensor.wait_ge(sems.f_sc, F_SC_CH * c + 2 * mc)
            for j2 in range(4):
                tensor.matmul(out=pj(j2), lhsT=f.nbg[mc % 2][0:KB_GU, 128 * j2:128 * (j2 + 1)],
                              rhs=f.xs[:], start=True, stop=False).then_inc(sems.f_mm, 1)
            for kq in range(4):
                tensor.wait_ge(fq_sems[fq % 3], ((fq - fq % 3) // 3 + 1) * DMA)
                for k8 in range(8):
                    k = 8 * kq + k8
                    last = (k == 31)
                    for j2 in range(4):
                        tensor.matmul(out=pj(j2),
                                      lhsT=f.wring[fq % 3][:, 512 * k8 + 128 * j2:512 * k8 + 128 * (j2 + 1)],
                                      rhs=w.tx[:, 1024 * k + 512 * c:1024 * k + 512 * (c + 1)],
                                      start=False, stop=last).then_inc(sems.f_mm, 1)
                fq += 1
            for j2 in range(2):
                jj = 2 * mc + j2
                tensor.wait_ge(sems.f_ve, veb + f_ve_h(mc, j2))
                tensor.matmul(out=pq[2][0:KB_DN, 0:512],
                              lhsT=f.ind[jj % 2][:],
                              rhs=f.hreg[:, 512 * jj:512 * (jj + 1)],
                              start=(jj == 0), stop=(jj == KB_DN - 1)).then_inc(sems.f_mm, 1)
        for mc8 in range(MC_DN):
            tensor.wait_ge(sems.f_ve, veb + F_VE_HS)
            tensor.wait_ge(sems.f_sc, F_SC_CH * c + 86)
            tensor.wait_ge(fn_sems[mc8 % 2], ([26, 25][mc8 % 2] * c + [22, 21][mc8 % 2] + mc8 // 2 + 1) * DMA)
            if mc8 >= 1:
                tensor.wait_ge(sems.f_ve, veb + f_ve_t2(4 * mc8 - 1))
            for j2 in range(4):
                tensor.matmul(out=pj(j2),
                              lhsT=f.nbg[mc8 % 2][0:KB_DN, 128 * j2:128 * (j2 + 1)],
                              rhs=f.hs[:], start=True, stop=False).then_inc(sems.f_mm, 1)
            k0 = 0
            for kgi, kg in enumerate(DN_KG):
                tensor.wait_ge(fq_sems[fq % 3], ((fq - fq % 3) // 3 + 1) * DMA)
                for kk in range(kg):
                    k = k0 + kk
                    last = (k == KB_DN - 1)
                    for j2 in range(4):
                        tensor.matmul(out=pj(j2),
                                      lhsT=f.wring[fq % 3][:, 512 * kk + 128 * j2:512 * kk + 128 * (j2 + 1)],
                                      rhs=f.hreg[:, 512 * k:512 * (k + 1)],
                                      start=False, stop=last).then_inc(sems.f_mm, 1)
                fq += 1
                k0 += kg
            for m2 in range(4):
                m = 4 * mc8 + m2
                tensor.wait_ge(sems.f_ve, veb + f_ve_t2bf(m))
                tensor.wait_ge(sems.f_sc, F_SC_CH * c + 86 + m + 1)
                tensor.matmul(out=pq[3][0:1, 0:512], lhsT=s.ones[:],
                              rhs=f.t2bf[m % 2][:],
                              start=(m == 0), stop=(m == 31)).then_inc(sems.f_s2, 1)
                tensor.matmul(out=pq[3][0:1, 512:1024], lhsT=s.ones[:],
                              rhs=f.sq[m % 2][:],
                              start=(m == 0), stop=(m == 31)).then_inc(sems.f_s2, 1)

    def f_scalar(scalar, c):
        scb = F_SC_CH * c
        veb = F_VE_CH * c
        mmb = F_MM_CH * c
        for mc in range(MC_GU):
            for j2 in range(2):
                jj = 2 * mc + j2
                gpos = mmb + 32 + mc * 134 + 4 + 128
                scalar.wait_ge(sems.f_mm, gpos)
                if jj >= 2:
                    scalar.wait_ge(sems.f_ve, veb + f_ve_h(*divmod(jj - 2, 2)))
                scalar.activation(out=f.gsil[jj % 2][:],
                                  in_=[pq[0], pq[1]][j2][:, 0:512],
                                  func=AF.Silu).then_inc(sems.f_sc, 1)
        for m in range(32):
            scalar.wait_ge(sems.f_ve, veb + f_ve_t2bf(m))
            if m >= 2:
                scalar.wait_ge(sems.f_s2, F_S2_CH * c + 2 * (m - 1))
            scalar.activation(out=f.sq[m % 2][:], in_=f.t2bf[m % 2][:],
                              func=AF.Square).then_inc(sems.f_sc, 1)
        scalar.wait_ge(sems.f_ve, veb + F_VE_DN + 2)
        scalar.activation(out=pq[3][0:1, 0:512], in_=f.rows2[0:1, :],
                          func=AF.Square).then_inc(sems.f_sc, 1)
        scalar.wait_ge(sems.f_ve, veb + F_VE_DN + 4)
        scalar.activation(out=pq[3][0:1, 0:512], in_=f.ou[0:1, :],
                          func=AF.Sqrt).then_inc(sems.f_sc, 1)

    def f_vector(vector, c):
        veb = F_VE_CH * c
        mmb = F_MM_CH * c
        scb = F_SC_CH * c
        pj2 = lambda j2: [pq[0], pq[1]][j2 // 2][:, 512 * (j2 % 2):512 * (j2 % 2) + 512]
        vector.wait_ge(sems.f_mm, mmb + 32)
        vector.tensor_copy(out=f.xs[:], in_=pq[3][0:KB_GU, 0:512]).then_inc(sems.f_ve, 1)
        for mc in range(MC_GU):
            for j2 in range(2):
                jj = 2 * mc + j2
                if jj >= 2:
                    vector.wait_ge(sems.f_mm, mmb + 32 + (mc - 1) * 134 + 133 + j2)
                vector.tensor_scalar(out=f.ind[jj % 2][:], in0=s.iota_row[:, 0:KB_DN],
                                     scalar1=s.gcolf_dn[:, jj:jj + 1], scalar2=None,
                                     op0=ALU.is_equal).then_inc(sems.f_ve, 1)
            for j2 in range(2):
                jj = 2 * mc + j2
                vector.wait_ge(sems.f_sc, scb + 2 * mc + j2 + 1)
                vector.tensor_tensor(out=f.hreg[:, 512 * jj:512 * (jj + 1)],
                                     in0=f.gsil[jj % 2][:],
                                     in1=[pq[0], pq[1]][j2][:, 512:1024],
                                     op=ALU.mult).then_inc(sems.f_ve, 1)
        vector.wait_ge(sems.f_mm, mmb + 32 + 43 * 134)
        vector.tensor_copy(out=f.hs[:], in_=pq[2][0:KB_DN, 0:512]).then_inc(sems.f_ve, 1)
        for mc8 in range(MC_DN):
            for m2 in range(4):
                m = 4 * mc8 + m2
                vector.wait_ge(sems.f_mm, mmb + 32 + 43 * 134 + mc8 * 348 + 4 + 4 * KB_DN)
                vector.wait_ge(fm_sems[m % 2], (16 * c + m // 2 + 1) * DMA)
                if m >= 2:
                    vector.wait_ge(ft_sems[m % 2], (16 * c + (m - 2) // 2 + 1) * DMA)
                elif c > 0:
                    vector.wait_ge(fo_sems[m % 2], 16 * c * DMA)
                vector.tensor_add(out=f.t2[m % 2][:], in0=pj2(m % 4),
                                  in1=f.xm[m % 2][:]).then_inc(sems.f_ve, 1)
                vector.wait_ge(sems.f_ve, veb + f_ve_t2(m))
                if m >= 2:
                    vector.wait_ge(sems.f_sc, F_SC_CH * c + 85 + m)
                    vector.wait_ge(sems.f_s2, F_S2_CH * c + 2 * (m - 1))
                vector.tensor_copy(out=f.t2bf[m % 2][:], in_=f.t2[m % 2][:]).then_inc(sems.f_ve, 1)
        vector.wait_ge(sems.f_s2, F_S2_CH * (c + 1))
        vector.tensor_scalar(out=f.rows2[0:1, :], in0=pq[3][0:1, 0:512],
                             scalar1=1.0 / D, scalar2=None, op0=ALU.mult).then_inc(sems.f_ve, 1)
        vector.tensor_scalar(out=f.ou[0:1, :], in0=pq[3][0:1, 512:1024],
                             scalar1=1.0 / D, scalar2=None, op0=ALU.mult).then_inc(sems.f_ve, 1)
        vector.wait_ge(sems.f_sc, F_SC_CH * c + 119)
        vector.tensor_sub(out=f.ou[0:1, :], in0=f.ou[0:1, :],
                          in1=pq[3][0:1, 0:512]).then_inc(sems.f_ve, 1)
        vector.wait_ge(sems.f_ve, veb + F_VE_DN + 3)
        vector.tensor_scalar(out=f.ou[0:1, :], in0=f.ou[0:1, :],
                             scalar1=EPS, scalar2=None, op0=ALU.add).then_inc(sems.f_ve, 1)
        vector.wait_ge(sems.f_sc, F_SC_CH * (c + 1))
        vector.reciprocal(out=f.ou[0:1, :], in_=pq[3][0:1, 0:512]).then_inc(sems.f_ve, 1)
        vector.wait_ge(sems.f_ab, (4 * c + 4) * DMA)
        for m in range(32):
            vector.wait_ge(fr_sems[m % 2], (16 * c + m // 2 + 1) * DMA)
            if m >= 2:
                vector.wait_ge(fo_sems[m % 2], (16 * c + (m - 2) // 2 + 1) * DMA)
            vector.tensor_sub(out=f.ou[:], in0=f.t2r[m % 2][:],
                              in1=f.bb2[:]).then_inc(sems.f_ve, 1)
            vector.wait_ge(sems.f_ve, veb + F_VE_ROWS + 2 * m + 1)
            if m < 2:
                vector.wait_ge(ft_sems[m % 2], 16 * (c + 1) * DMA)
            vector.tensor_mul(out=f.ob[m % 2][:], in0=f.ou[:],
                              in1=f.ab2[:]).then_inc(sems.f_ve, 1)

    with nc.Block() as block:
        @block.sync
        def _(sync):
            nbar = 0
            if "D" in phases:
                d_sync(sync)
                sync.wait_ge(sems.d_st, d_tot_st * DMA)
                sync.wait_ge(sems.d_ld, d_tot_ld * DMA)
                for j, dwj in enumerate([sems.d_w0, sems.d_w1, sems.d_w2, sems.d_w3]):
                    sync.wait_ge(dwj, (d_tot_tt // 4) * DMA)
                if debug:
                    sync.dma_start(out=t.gbf_dump[:], in_=d.gbf_dn[:]).then_inc(sems.d_l0, DMA)
                    sync.dma_start(out=t.indT_dump[:], in_=d.indT[:]).then_inc(sems.d_l0, DMA)
                    sync.dma_start(out=t.indgu_dump[:], in_=s.ind_gu[:]).then_inc(sems.d_l0, DMA)
                    sync.wait_ge(sems.d_l0, (7 + 3) * DMA)
            nbar += 1
            sync.sem_inc(sems.bar, 1)
            if "P" in phases:
                sync.wait_ge(sems.bar, nbar)
                p_sync(sync)
                if debug:
                    sync.dma_start(out=t.qt_dump[:], in_=p.qt[:]).then_inc(sems.p_rp, DMA)
                    sync.wait_ge(sems.p_rp, DMA)
            nbar += 1
            sync.sem_inc(sems.bar, 1)
            if "A" in phases:
                sync.wait_ge(sems.bar, nbar)
                a_sync(sync)
                sync.wait_ge(sems.a_ve, A_TOT_VE)
                if debug:
                    sync.dma_start(out=t.attn_dump[:], in_=a.attn[:]).then_inc(sems.p_rp, DMA)
                    sync.wait_ge(sems.p_rp, 2 * DMA)
            nbar += 1
            sync.sem_inc(sems.bar, 1)
            if "W" in phases:
                sync.wait_ge(sems.bar, nbar)
                w_sync(sync)
                sync.wait_ge(sems.w_ve, W_TOT_VE)
                sync.wait_ge(sems.w_s0, 16 * DMA)
                sync.wait_ge(sems.w_s1, 16 * DMA)
            nbar += 1
            sync.sem_inc(sems.bar, 1)
            if "F" in phases:
                sync.wait_ge(sems.bar, nbar)
                for cch in range(2):
                    f_sync(sync, cch)
                sync.wait_ge(sems.f_o0, 32 * DMA)
                sync.wait_ge(sems.f_o1, 32 * DMA)

        @block.vector
        def _(vector):
            if "D" in phases:
                d_vector(vector)
            if "P" in phases:
                vector.wait_ge(sems.bar, 1)
                p_vector(vector)
            if "A" in phases:
                vector.wait_ge(sems.bar, 2)
                a_vector(vector)
            if "W" in phases:
                vector.wait_ge(sems.bar, 3)
                w_vector(vector)
            if "F" in phases:
                vector.wait_ge(sems.bar, 4)
                for cch in range(2):
                    f_vector(vector, cch)

        @block.tensor
        def _(tensor):
            if "D" in phases:
                d_tensor(tensor)
            if "P" in phases:
                tensor.wait_ge(sems.bar, 1)
                p_tensor(tensor)
            if "A" in phases:
                tensor.wait_ge(sems.bar, 2)
                a_tensor(tensor)
            if "W" in phases:
                tensor.wait_ge(sems.bar, 3)
                w_tensor(tensor)
            if "F" in phases:
                tensor.wait_ge(sems.bar, 4)
                for cch in range(2):
                    f_tensor(tensor, cch)

        @block.scalar
        def _(scalar):
            if "D" in phases:
                d_scalar(scalar)
            if "P" in phases:
                scalar.wait_ge(sems.bar, 1)
                p_scalar(scalar)
            if "A" in phases:
                scalar.wait_ge(sems.bar, 2)
                a_scalar(scalar)
            if "W" in phases:
                scalar.wait_ge(sems.bar, 3)
                w_scalar(scalar)
            if "F" in phases:
                scalar.wait_ge(sems.bar, 4)
                for cch in range(2):
                    f_scalar(scalar, cch)

        @block.gpsimd
        def _(gpsimd):
            pass

    ctx.close()
    return nc


def prep_inputs(L0_attn_x, L0_attn_wq, L0_attn_wk, L0_attn_wv, L0_attn_wo,
                L0_ffn_qweight_gu, L0_ffn_qzeros_gu, L0_ffn_scales_gu, L0_ffn_g_idx_gu,
                L0_ffn_qweight_down, L0_ffn_qzeros_down, L0_ffn_scales_down, L0_ffn_g_idx_down):
    ph = full_perm(D)
    pi = full_perm(INTER)
    rp = rope_perm()
    blocks = []
    for j in range(KB_DN):
        blocks.append(pi[128 * j:128 * (j + 1)])
        blocks.append(INTER + pi[128 * j:128 * (j + 1)])
    pgu = np.concatenate(blocks)
    wp = []
    for j in range(KB_DN):
        wp.append(np.arange(16 * j, 16 * (j + 1)))
        wp.append(1376 + np.arange(16 * j, 16 * (j + 1)))
    word_perm = np.concatenate(wp)

    wq = np.ascontiguousarray(np.asarray(L0_attn_wq)[ph][:, rp])
    wk = np.ascontiguousarray(np.asarray(L0_attn_wk)[ph][:, rp])
    wv = np.ascontiguousarray(np.asarray(L0_attn_wv)[ph])
    wo = np.ascontiguousarray(np.asarray(L0_attn_wo)[:, ph])
    qw_gu = np.ascontiguousarray(np.asarray(L0_ffn_qweight_gu)[:, pgu])
    sc_gu = np.ascontiguousarray(np.asarray(L0_ffn_scales_gu)[:, pgu])
    gi_gu = np.ascontiguousarray(np.asarray(L0_ffn_g_idx_gu)[ph]).astype(np.int32)
    qw_dn = np.ascontiguousarray(np.asarray(L0_ffn_qweight_down)[:, ph])
    sc_dn = np.ascontiguousarray(np.asarray(L0_ffn_scales_down)[:, ph])
    gi_dn = np.ascontiguousarray(np.asarray(L0_ffn_g_idx_down)[pi]).astype(np.int32)
    qz_gu = np.ascontiguousarray(np.asarray(L0_ffn_qzeros_gu)[:, word_perm])
    qz_dn = np.ascontiguousarray(np.asarray(L0_ffn_qzeros_down))

    inv = 1.0 / (10000.0 ** (np.arange(0, HD, 2)[: HD // 2].astype(np.float32) / HD))
    tt = np.arange(S)
    f = np.outer(tt, inv).astype(np.float32)
    cos_k = np.ascontiguousarray(np.cos(f).T)
    sin_k = np.ascontiguousarray(np.sin(f).T)

    common = dict(
        wq=wq, wk=wk, wv=wv, wo=wo,
        qw_gu=qw_gu, qz_gu=qz_gu, sc_gu=sc_gu,
        qw_dn=qw_dn, qz_dn=qz_dn, sc_dn=sc_dn,
        gi_bc_gu=np.ascontiguousarray(np.broadcast_to(gi_gu[None, :], (KB_GU, D))),
        gi_col_gu=np.ascontiguousarray(gi_gu.reshape(KB_GU, 128).T),
        gi_bc_dn=np.ascontiguousarray(np.broadcast_to(gi_dn[None, :], (KB_DN, INTER))),
        gi_col_dn=np.ascontiguousarray(gi_dn.reshape(KB_DN, 128).T),
        cos_k=cos_k, sin_k=sin_k,
        ones_col=np.ones((128, 1), BF),
        iota_row=np.ascontiguousarray(np.broadcast_to(np.arange(KB_DN, dtype=np.float32), (128, KB_DN))),
        iota_col=np.arange(KB_DN, dtype=np.float32).reshape(KB_DN, 1),
        mask_col=np.ascontiguousarray(
            np.left_shift(np.uint32(15), 4 * (np.arange(128, dtype=np.uint32) // 16)).view(np.int32).reshape(128, 1)),
        pscale_col=(2.0 ** (-4.0 * (np.arange(128) // 16))).astype(np.float32).reshape(128, 1),
    )

    x = np.asarray(L0_attn_x)
    in_maps = []
    for c in range(N_CORES):
        b, half = c // 2, c % 2
        q0 = half * TOKQ
        xTb = np.ascontiguousarray(x[b].T[ph])
        m = dict(common)
        m["xT"] = xTb
        m["x_own"] = np.ascontiguousarray(xTb[:, q0:q0 + TOKQ])
        m["cos_q"] = np.ascontiguousarray(cos_k[:, q0:q0 + TOKQ])
        m["sin_q"] = np.ascontiguousarray(sin_k[:, q0:q0 + TOKQ])
        kv = (np.arange(16)[None, :, None] * 128 + np.arange(128)[:, None, None])
        qq = (q0 + np.arange(TOKQ))[None, None, :]
        mk = (kv <= qq).astype(np.uint8)
        m["mask"] = np.ascontiguousarray(mk.reshape(128, 16 * TOKQ))
        in_maps.append(m)
    return in_maps


def kernel(**inputs):
    in_maps = prep_inputs(**inputs)
    nc = build_program()
    res = run_bass_kernel_spmd(nc, in_maps, core_ids=list(range(N_CORES)))
    inv_ph = np.argsort(full_perm(D))
    out = np.empty((B, S, D), np.float32)
    for c in range(N_CORES):
        b, half = c // 2, c % 2
        out[b, half * TOKQ:(half + 1) * TOKQ, :] = res.results[c]["outT"][inv_ph].T
    return out


# version 05
# speedup vs baseline: 1.4492x; 1.0458x over previous
import contextlib
import numpy as np
import ml_dtypes
import concourse.bass as bass
import concourse.mybir as mybir
from concourse.bass_utils import run_bass_kernel_spmd

F32 = mybir.dt.float32
BF16 = mybir.dt.bfloat16
U32 = mybir.dt.uint32
I32 = mybir.dt.int32
U8 = mybir.dt.uint8
AF = mybir.ActivationFunctionType
ALU = mybir.AluOpType
BF = ml_dtypes.bfloat16

D = 4096
S = 2048
B = 4
H = 32
HD = 128
INTER = 11008
NGU = 2 * INTER
KB_GU = D // 128
KB_DN = INTER // 128
MC_GU = NGU // 512
MC_DN = D // 512
TOKQ = 1024
CH = 512
N_CORES = 8
EPS = 1e-5
SCALE = 1.0 / float(np.sqrt(np.float32(HD)))

DMA = 16
SZ = {F32: 4, U32: 4, I32: 4, BF16: 2, U8: 1}


def perm128():
    p = np.arange(128)
    return 8 * (p % 16) + p // 16


def full_perm(n):
    return (np.arange(0, n, 128)[:, None] + perm128()[None, :]).ravel()


def rope_perm():
    base = np.concatenate([np.arange(0, HD, 2), np.arange(1, HD, 2)])
    return (np.arange(0, D, HD)[:, None] + base[None, :]).ravel()


class NS:
    pass


def build_program(phases=("D", "P", "A", "W", "F"), debug=False):
    nc = bass.Bass("TRN2", target_bir_lowering=False)
    ctx = contextlib.ExitStack()

    def inp(name, shape, dtype):
        return nc.dram_tensor(name, list(shape), dtype, kind="ExternalInput").ap()

    t = NS()
    t.xT = inp("xT", [D, S], F32)
    t.x_own = inp("x_own", [D, TOKQ], F32)
    t.wq = inp("wq", [D, D], F32)
    t.wk = inp("wk", [D, D], F32)
    t.wv = inp("wv", [D, D], F32)
    t.wo = inp("wo", [D, D], F32)
    t.qw_gu = inp("qw_gu", [D // 8, NGU], U32)
    t.qz_gu = inp("qz_gu", [KB_GU, NGU // 8], U32)
    t.sc_gu = inp("sc_gu", [KB_GU, NGU], F32)
    t.qw_dn = inp("qw_dn", [INTER // 8, D], U32)
    t.qz_dn = inp("qz_dn", [KB_DN, D // 8], U32)
    t.sc_dn = inp("sc_dn", [KB_DN, D], F32)
    t.gi_bc_gu = inp("gi_bc_gu", [KB_GU, D], I32)
    t.gi_col_gu = inp("gi_col_gu", [128, KB_GU], I32)
    t.gi_bc_dn = inp("gi_bc_dn", [KB_DN, INTER], I32)
    t.gi_col_dn = inp("gi_col_dn", [128, KB_DN], I32)
    t.cos_k = inp("cos_k", [64, S], F32)
    t.sin_k = inp("sin_k", [64, S], F32)
    t.cos_q = inp("cos_q", [64, TOKQ], F32)
    t.sin_q = inp("sin_q", [64, TOKQ], F32)
    t.mask = inp("mask", [128, 16 * TOKQ], U8)
    t.ones_col = inp("ones_col", [128, 1], BF16)
    t.iota_row = inp("iota_row", [128, KB_DN], F32)
    t.iota_col = inp("iota_col", [KB_DN, 1], F32)
    t.mask_col = inp("mask_col", [128, 1], I32)
    t.pscale_col = inp("pscale_col", [128, 1], F32)

    t.outT = nc.dram_tensor("outT", [D, TOKQ], F32, kind="ExternalOutput").ap()

    dbg = "ExternalOutput" if debug else "Internal"
    t.wgu_c = nc.dram_tensor("wgu_c", [MC_GU, KB_GU, 128, 512], BF16, kind=dbg).ap()
    t.wdn_c = nc.dram_tensor("wdn_c", [MC_DN, KB_DN, 128, 512], BF16, kind=dbg).ap()
    t.negbg_gu = nc.dram_tensor("negbg_gu", [KB_GU, NGU], BF16, kind=dbg).ap()
    t.negbg_dn = nc.dram_tensor("negbg_dn", [KB_DN, D], BF16, kind=dbg).ap()
    if debug:
        t.gbf_dump = nc.dram_tensor("gbf_dump", [KB_DN, INTER], BF16, kind="ExternalOutput").ap()
        t.indT_dump = nc.dram_tensor("indT_dump", [KB_DN, 512], BF16, kind="ExternalOutput").ap()
        t.indgu_dump = nc.dram_tensor("indgu_dump", [128, KB_GU * 32], BF16, kind="ExternalOutput").ap()
        t.qt_dump = nc.dram_tensor("qt_dump", [128, 32 * 1024], BF16, kind="ExternalOutput").ap()
        t.attn_dump = nc.dram_tensor("attn_dump", [128, 32 * 1024], BF16, kind="ExternalOutput").ap()
    t.kT_sp = nc.dram_tensor("kT_sp", [D, S], BF16, kind=dbg).ap()
    t.v_sp = nc.dram_tensor("v_sp", [H, 16, 128, 128], BF16, kind=dbg).ap()
    t.xmid = nc.dram_tensor("xmid", [D, TOKQ], F32, kind="Internal").ap()
    t.t2_sp = nc.dram_tensor("t2_sp", [D, CH], F32, kind="Internal").ap()
    t.rows_sp = nc.dram_tensor("rows_sp", [16, TOKQ], F32, kind="Internal").ap()

    SBUF_BASE = 18432

    def sbuf(name, off, shape, dtype):
        nbytes = int(np.prod(shape[1:])) * SZ[dtype]
        assert SBUF_BASE + off + nbytes <= 229376, f"{name}: {off}+{nbytes} over budget"
        return nc.alloc_sbuf_tensor_at(name, list(shape), dtype, offset=SBUF_BASE + off)

    s = NS()
    s.ones = sbuf("s_ones", 0, [128, 1], BF16)
    s.iota_row = sbuf("s_iota_row", 64, [128, KB_DN], F32)
    s.iota_col = sbuf("s_iota_col", 448, [KB_DN, 1], F32)
    s.mask = sbuf("s_mask", 512, [128, 1], I32)
    s.pscale = sbuf("s_pscale", 3648, [128, 1], F32)
    s.gcolf_gu = sbuf("s_gcolf_gu", 576, [128, KB_GU], F32)
    s.gcolf_dn = sbuf("s_gcolf_dn", 704, [128, KB_DN], F32)
    s.gcol_gu_i = sbuf("s_gcol_gu_i", 1088, [128, KB_GU], I32)
    s.gcol_dn_i = sbuf("s_gcol_dn_i", 1216, [128, KB_DN], I32)
    s.ind_gu = sbuf("s_ind_gu", 1600, [128, KB_GU * 32], BF16)
    s.rows = sbuf("s_rows", 3712, [128, 1536], F32)
    CONST2 = 9856
    STAGE = 34432
    R2 = 55936
    R3 = 121472
    TOP = 188032
    LIM = 208 * 1024

    sems = NS()
    for n in ["d_ld", "d_l0", "d_pr", "d_ve", "d_mm", "d_st", "d_fr", "d_ix",
              "d_q0", "d_q1", "d_w0", "d_w1", "d_w2", "d_w3", "d_sc",
              "p_x0", "p_x1", "p_w0", "p_w1", "p_w2", "p_cs", "p_v0", "p_v1", "p_k0", "p_k1", "p_ch", "d_ch",
              "a_k0", "a_k1", "a_r0", "a_r1", "a_b0", "a_b1", "w_s0", "w_s1",
              "p_ld", "p_wl", "p_cv", "p_mm", "p_ve", "p_st", "p_fr", "p_rp",
              "a_ld", "a_mm", "a_ex", "a_ve", "a_sc", "a_st", "a_fr", "a_sm",
              "w_ld", "w_wl", "w_mm", "w_ve", "w_sc", "w_st", "w_rw",
              "f_ld", "f_wl", "f_mm", "f_ve", "f_sc", "f_st", "f_hs", "f_x", "f_s2",
              "f_q0", "f_q1", "f_q2", "f_n0", "f_n1", "f_m0", "f_m1", "f_t0", "f_t1",
              "f_r0", "f_r1", "f_o0", "f_o1", "f_ab",
              "bar"]:
        setattr(sems, n, ctx.enter_context(nc.semaphore(n)))

    pq = [ctx.enter_context(nc.psum_tensor(f"pq{i}", [128, 1024], F32)) for i in range(4)]

    d = NS()
    d.scf = sbuf("d_scf", STAGE, [KB_DN, 2048], F32)
    d.qzb = sbuf("d_qzb", STAGE + 8192, [KB_DN, 256], U32)
    d.zq = sbuf("d_zq", STAGE + 9216, [KB_DN, 2048], U32)
    d.nbg = sbuf("d_nbg", STAGE + 17408, [KB_DN, 2048], BF16)
    d.qrep = [sbuf(f"d_qrep{i}", R2 + i * 8192, [128, 2048], U32) for i in range(2)]
    d.scbf_gu = sbuf("d_scbf_gu", R2 + 16384, [KB_GU, NGU], BF16)
    d.scbf_dn = sbuf("d_scbf_dn", R3, [KB_DN, D], BF16)
    d.indT = sbuf("d_indT", R3 + 8192, [KB_DN, 128 * 4], BF16)
    d.nib = [sbuf(f"d_nib{i}", R3 + 9280 + i * 2048, [128, 512], U32) for i in range(2)]
    d.wt = [sbuf(f"d_wt{i}", R3 + 13376 + i * 1024, [128, 512], BF16) for i in range(4)]
    d.nsc = sbuf("d_nsc", R3 + 29760, [KB_DN, 2048], F32)
    d.nibf = [sbuf(f"d_nibf{i}", R3 + 17472 + i * 1024, [128, 512], BF16) for i in range(2)]
    d.scb = [sbuf(f"d_scb{i}", R3 + 19520 + i * 1024, [128, 512], BF16) for i in range(2)]
    d.gist = [sbuf(f"d_gist{i}", R3 + 21568 + i * 8192, [KB_DN, 2048], I32) for i in range(2)]
    d.gbf_gu = sbuf("d_gbf_gu", R3 + 37952, [KB_GU, D], BF16)
    d.gbf_dn = sbuf("d_gbf_dn", R3 + 46144, [KB_DN, INTER], BF16)

    gu_chunks = [(i * 2048, 2048) for i in range(10)] + [(20480, 1536)]
    dn_chunks = [(0, 2048), (2048, 2048)]
    prep = [("gu", c0, w) for c0, w in gu_chunks] + [("dn", c0, w) for c0, w in dn_chunks]
    n_prep = len(prep)

    d_main = []
    gix = 0
    for G in range(KB_GU):
        for c0, w in gu_chunks:
            d_main.append(("gu", G, c0, w, gix))
        gix += 1
    for G in range(KB_DN):
        for c0, w in dn_chunks:
            d_main.append(("dn", G, c0, w, gix))
        gix += 1
    n_gix = gix
    ts_cum = []
    c = 0
    for (_, _, _, w, _) in d_main:
        c += w // 512
        ts_cum.append(c)
    d_tot_tt = c
    d_tot_ld = 2 * n_prep
    d_tot_st = n_prep
    g_last_tt = {}
    g_first_it = {}
    for i2, (_, _, _, w, g) in enumerate(d_main):
        g_last_tt[g] = ts_cum[i2]
        g_first_it.setdefault(g, i2)

    def d_sync(sync):
        sync.dma_start(out=s.ones[:], in_=t.ones_col[:]).then_inc(sems.d_l0, DMA)
        sync.dma_start(out=s.iota_row[:], in_=t.iota_row[:]).then_inc(sems.d_l0, DMA)
        sync.dma_start(out=s.iota_col[:], in_=t.iota_col[:]).then_inc(sems.d_l0, DMA)
        sync.dma_start(out=s.mask[:], in_=t.mask_col[:]).then_inc(sems.d_l0, DMA)
        sync.dma_start(out=s.pscale[:], in_=t.pscale_col[:]).then_inc(sems.d_l0, DMA)
        sync.dma_start(out=s.gcol_gu_i[:], in_=t.gi_col_gu[:]).then_inc(sems.d_l0, DMA)
        sync.dma_start(out=s.gcol_dn_i[:], in_=t.gi_col_dn[:]).then_inc(sems.d_l0, DMA)
        gchunks = [("gu", c) for c in range(0, D, 2048)] + [("dn", c) for c in range(0, INTER, 2048)]
        d_sync.gchunks = gchunks
        for gi, (kind, c0) in enumerate(gchunks):
            w2 = min(2048, (D if kind == "gu" else INTER) - c0)
            nb2 = KB_GU if kind == "gu" else KB_DN
            gsrc = t.gi_bc_gu if kind == "gu" else t.gi_bc_dn
            if gi >= 2:
                sync.wait_ge(sems.d_ix, gi)
            sync.dma_start(out=d.gist[gi % 2][0:nb2, 0:w2],
                           in_=gsrc[:, c0:c0 + w2]).then_inc([sems.d_q0, sems.d_q1][gi % 2], DMA)
        for i, (kind, c0, w) in enumerate(prep):
            nb = KB_GU if kind == "gu" else KB_DN
            scsrc = t.sc_gu if kind == "gu" else t.sc_dn
            qzsrc = t.qz_gu if kind == "gu" else t.qz_dn
            sync.wait_ge(sems.d_pr, i)
            sync.dma_start(out=d.scf[0:nb, 0:w], in_=scsrc[:, c0:c0 + w]).then_inc(sems.d_ld, DMA)
            sync.dma_start(out=d.qzb[0:nb, 0:w // 8], in_=qzsrc[:, c0 // 8:(c0 + w) // 8]).then_inc(sems.d_ld, DMA)
            sync.wait_ge(sems.d_pr, i + 1)
            nbgdst = t.negbg_gu if kind == "gu" else t.negbg_dn
            sync.dma_start(out=nbgdst[:, c0:c0 + w], in_=d.nbg[0:nb, 0:w]).then_inc(sems.d_st, DMA)
        for it, (kind, G, c0, w, g) in enumerate(d_main):
            qrep = d.qrep[it % 2]
            dq = [sems.d_q0, sems.d_q1][it % 2]
            qw = t.qw_gu if kind == "gu" else t.qw_dn
            if it >= 2:
                sync.wait_ge(sems.d_ve, ts_cum[it - 2])
            for j in range(8):
                sync.dma_start(out=qrep[16 * j:16 * j + 16, 0:w],
                               in_=qw[16 * G:16 * G + 16, c0:c0 + w]).then_inc(dq, DMA)

    def d_vector(vector):
        vector.wait_ge(sems.d_l0, 7 * DMA)
        vector.tensor_copy(out=s.gcolf_gu[:], in_=s.gcol_gu_i[:])
        vector.tensor_copy(out=s.gcolf_dn[:], in_=s.gcol_dn_i[:]).then_inc(sems.d_ix, 1)
        for gi, (kind, c0) in enumerate(d_sync.gchunks):
            w2 = min(2048, (D if kind == "gu" else INTER) - c0)
            nb2 = KB_GU if kind == "gu" else KB_DN
            gdst = d.gbf_gu if kind == "gu" else d.gbf_dn
            vector.wait_ge([sems.d_q0, sems.d_q1][gi % 2], (gi // 2 + 1) * DMA)
            vector.tensor_copy(out=gdst[0:nb2, c0:c0 + w2],
                               in_=d.gist[gi % 2][0:nb2, 0:w2]).then_inc(sems.d_ix, 1)
        for i, (kind, c0, w) in enumerate(prep):
            nb = KB_GU if kind == "gu" else KB_DN
            scbf = d.scbf_gu if kind == "gu" else d.scbf_dn
            if i == 0:
                vector.wait_ge(sems.d_ix, 9)
            vector.wait_ge(sems.d_ld, (2 * (i + 1)) * DMA)
            if i >= 1:
                vector.wait_ge(sems.d_st, i * DMA)
                vector.wait_ge(sems.d_pr, i)
            vector.tensor_copy(out=scbf[0:nb, c0:c0 + w], in_=d.scf[0:nb, 0:w])
            vector.tensor_scalar(out=d.nsc[0:nb, 0:w], in0=d.scf[0:nb, 0:w],
                                 scalar1=-1.0, scalar2=None, op0=ALU.mult).then_inc(sems.d_ch, 1)
            nblk = w // 128
            for j in range(8):
                src = d.qzb[0:nb, 0:w // 8].rearrange("p (b c) -> p b c", c=16)
                dst = d.zq[0:nb, 0:w].rearrange("p (b c) -> p b c", c=128)[:, :, 16 * j:16 * j + 16]
                ins = vector.tensor_scalar(out=dst, in0=src, scalar1=4 * j, scalar2=15,
                                           op0=ALU.logical_shift_right, op1=ALU.bitwise_and)
                if j == 7:
                    ins.then_inc(sems.d_ch, 1)
            vector.wait_ge(sems.d_ch, 2 * (i + 1))
            vector.scalar_tensor_tensor(out=d.nbg[0:nb, 0:w], in0=d.zq[0:nb, 0:w],
                                        scalar=1.0, in1=d.nsc[0:nb, 0:w],
                                        op0=ALU.add, op1=ALU.mult).then_inc(sems.d_pr, 1)
        vector.wait_ge(sems.d_ix, 10)
        tsc = 0
        for it, (kind, G, c0, w, g) in enumerate(d_main):
            qrep = d.qrep[it % 2]
            nb = KB_GU if kind == "gu" else KB_DN
            if c0 == 0:
                if g >= 4:
                    vector.wait_ge(sems.d_mm, g_last_tt[g - 4])
                gbf = d.gbf_gu if kind == "gu" else d.gbf_dn
                vector.tensor_scalar(out=d.indT[0:nb, 128 * (g % 4):128 * (g % 4) + 128],
                                     in0=gbf[0:nb, 128 * G:128 * G + 128],
                                     scalar1=s.iota_col[0:nb, :], scalar2=None,
                                     op0=ALU.is_equal).then_inc(sems.d_ix, 1)
                if kind == "gu":
                    vector.tensor_scalar(out=s.ind_gu[:, 32 * G:32 * G + 32],
                                         in0=s.iota_row[:, 0:KB_GU],
                                         scalar1=s.gcolf_gu[:, G:G + 1], scalar2=None,
                                         op0=ALU.is_equal)
            vector.wait_ge([sems.d_q0, sems.d_q1][it % 2], (4 + 8 * (it // 2 + 1)) * DMA)
            d_w = [sems.d_w0, sems.d_w1, sems.d_w2, sems.d_w3]
            for u in range(w // 512):
                tsc += 1
                nib = d.nib[tsc % 2]
                nibf = d.nibf[tsc % 2]
                if tsc > 2:
                    vector.wait_ge(sems.d_ch, 2 * n_prep + tsc - 2)
                vector.tensor_scalar(out=nib[:], in0=qrep[:, 512 * u:512 * u + 512],
                                     scalar1=s.mask[:], scalar2=None,
                                     op0=ALU.bitwise_and).then_inc(sems.d_ve, 1)
                vector.wait_ge(sems.d_ve, tsc)
                if tsc > 2:
                    vector.wait_ge(sems.d_fr, tsc - 2)
                vector.tensor_scalar(out=nibf[:], in0=nib[:],
                                     scalar1=s.pscale[:], scalar2=None,
                                     op0=ALU.mult).then_inc(sems.d_ch, 1)
                vector.wait_ge(sems.d_sc, tsc)
                vector.wait_ge(sems.d_ch, 2 * n_prep + tsc)
                if tsc > 4:
                    vector.wait_ge(d_w[(tsc - 1) % 4], ((tsc - 1) // 4) * DMA)
                vector.tensor_tensor(out=d.wt[(tsc - 1) % 4][:], in0=nibf[:],
                                     in1=d.scb[(tsc - 1) % 2][:], op=ALU.mult).then_inc(sems.d_fr, 1)

    def d_tensor(tensor):
        tensor.wait_ge(sems.d_pr, n_prep)
        ttc = 0
        ixc = 0
        for it, (kind, G, c0, w, g) in enumerate(d_main):
            nb = KB_GU if kind == "gu" else KB_DN
            scbf = d.scbf_gu if kind == "gu" else d.scbf_dn
            if c0 == 0:
                tensor.wait_ge(sems.d_ix, 10 + (g + 1))
            for u in range(w // 512):
                ttc += 1
                if ttc > 2:
                    tensor.wait_ge(sems.d_sc, ttc - 2)
                tensor.matmul(out=pq[ttc % 2][:, 0:512],
                              lhsT=d.indT[0:nb, 128 * (g % 4):128 * (g % 4) + 128],
                              rhs=scbf[0:nb, c0 + 512 * u:c0 + 512 * (u + 1)],
                              start=True, stop=True).then_inc(sems.d_mm, 1)

    def d_scalar(scalar):
        scalar.wait_ge(sems.d_ix, 9)
        scalar.nop()
        scalar.sem_inc(sems.d_ix, 1)
        d_w = [sems.d_w0, sems.d_w1, sems.d_w2, sems.d_w3]
        ttc = 0
        for it, (kind, G, c0, w, g) in enumerate(d_main):
            cache = t.wgu_c if kind == "gu" else t.wdn_c
            for u in range(w // 512):
                ttc += 1
                scalar.wait_ge(sems.d_mm, ttc)
                if ttc > 2:
                    scalar.wait_ge(sems.d_fr, ttc - 2)
                scalar.activation(out=d.scb[(ttc - 1) % 2][:], in_=pq[ttc % 2][:, 0:512],
                                  func=AF.Copy).then_inc(sems.d_sc, 1)
                scalar.wait_ge(sems.d_fr, ttc)
                mc = (c0 + 512 * u) // 512
                scalar.dma_start(out=cache[mc, G], in_=d.wt[(ttc - 1) % 4][:]).then_inc(d_w[(ttc - 1) % 4], DMA)

    p = NS()
    p.cosk = sbuf("p_cosk", CONST2, [64, 1024], F32)
    p.sink = sbuf("p_sink", CONST2 + 4096, [64, 1024], F32)
    p.cosq = sbuf("p_cosq", CONST2 + 8192, [64, 1024], F32)
    p.sinq = sbuf("p_sinq", CONST2 + 12288, [64, 1024], F32)
    p.rt0 = sbuf("p_rt0", CONST2 + 16384, [64, 1024], F32)
    p.rt1 = sbuf("p_rt1", CONST2 + 20480, [64, 1024], F32)
    p.rt2 = sbuf("p_rt2", R3 + 70656, [64, 1024], F32)
    p.rt3 = sbuf("p_rt3", R3 + 66560, [64, 1024], F32)
    p.xstage = [sbuf(f"p_xst{i}", STAGE + i * 4096, [128, 1024], F32) for i in range(2)]
    p.wstage = [sbuf(f"p_wst{i}", STAGE + 8192 + i * 2048, [128, 512], F32) for i in range(3)]
    p.wbf = [sbuf(f"p_wbf{i}", STAGE + 14336 + i * 1024, [128, 512], BF16) for i in range(3)]
    p.kbf = [sbuf(f"p_kbf{i}", STAGE + 17408 + i * 2048, [128, 1024], BF16) for i in range(2)]
    p.vbf = [sbuf(f"p_vbf{i}", STAGE + 17408 + i * 2048, [128, 512], BF16) for i in range(2)]
    p.xh = sbuf("p_xh", R3, [128, 32 * 1024], BF16)
    p.wvbf = [sbuf(f"p_wvbf{i}", R2 + 32768 * i, [128, 32 * 512], BF16) for i in range(2)]
    p.qt = sbuf("p_qt", R2, [128, 32 * 1024], BF16)

    p_x = [sems.p_x0, sems.p_x1]
    p_w = [sems.p_w0, sems.p_w1, sems.p_w2]
    p_v = [sems.p_v0, sems.p_v1]
    p_k = [sems.p_k0, sems.p_k1]
    PLD_H = 802
    PCV_H = 32
    PWL_H = 768
    PMM_H = 4096
    PVE_H = 96
    PST_H = 288
    P_TOT_LD = 2 * PLD_H + 546
    P_TOT_CV = 96
    P_TOT_WL = 2 * PWL_H + 512
    P_TOT_MM = 2 * PMM_H + 2048
    P_TOT_VE = 2 * PVE_H + 32
    P_TOT_ST = 2 * PST_H

    def p_xl_sync(sync, c, xsrc, cos_src, sin_src, cdst, sdst, cv_base):
        sync.dma_start(out=cdst[:], in_=cos_src).then_inc(sems.p_cs, DMA)
        sync.dma_start(out=sdst[:], in_=sin_src).then_inc(sems.p_cs, DMA)
        for k in range(32):
            if k >= 2:
                sync.wait_ge(sems.p_cv, cv_base + k - 1)
            sync.dma_start(out=p.xstage[k % 2][:],
                           in_=xsrc[128 * k:128 * k + 128, :]).then_inc(p_x[k % 2], DMA)

    def p_xl_vector(vector, xfill_base, cv_check=None):
        for k in range(32):
            vector.wait_ge(p_x[k % 2], (xfill_base + k // 2 + 1) * DMA)
            vector.tensor_copy(out=p.xh[:, 1024 * k:1024 * (k + 1)],
                               in_=p.xstage[k % 2][:]).then_inc(sems.p_cv, 1)

    IW_K0, IW_K1, IW_Q, IW_W = 0, 256, 512, 768
    IW_TOT = 1280

    def p_wload_queue(eng, c, w_src, iw_base, wid, par, ve_base, spill, sk_base):
        for mc4 in range(8):
            for k in range(32):
                if spill is not None and mc4 >= 1 and k in (0, 2, 4, 6):
                    j = 4 * (mc4 - 1) + k // 2
                    eng.wait_ge(sems.p_ve, ve_base + j + 1)
                    eng.dma_start(out=spill(j), in_=p.kbf[j % 2][:]).then_inc(p_k[j % 2], DMA)
                iw = iw_base + mc4 * 32 + k
                if iw >= 3:
                    eng.wait_ge(sems.p_wl, iw - 2)
                eng.dma_start(out=p.wstage[iw % 3][:, 0:wid],
                              in_=w_src[128 * k:128 * k + 128,
                                        wid * mc4:wid * (mc4 + 1)]).then_inc(p_w[iw % 3], DMA)
        if spill is not None:
            for m4 in range(4):
                j = 28 + m4
                eng.wait_ge(sems.p_ve, ve_base + j + 1)
                eng.dma_start(out=spill(j), in_=p.kbf[j % 2][:]).then_inc(p_k[j % 2], DMA)

    def p_proj_scalar(scalar, iw_base, wid, mm_base, mm_per):
        for i in range(256):
            iw = iw_base + i
            scalar.wait_ge(p_w[iw % 3], (iw // 3 + 1) * DMA)
            if i >= 3:
                scalar.wait_ge(sems.p_mm, mm_base + (i - 2) * mm_per)
            elif iw_base > 0:
                scalar.wait_ge(sems.p_mm, mm_base)
            scalar.copy(out=p.wbf[iw % 3][:, 0:wid],
                        in_=p.wstage[iw % 3][:, 0:wid]).then_inc(sems.p_wl, 1)

    def p_proj_tensor(tensor, iw_base, mm_base, cv_base, ve_base):
        for mc4 in range(8):
            for k in range(32):
                iw = iw_base + mc4 * 32 + k
                tensor.wait_ge(sems.p_wl, iw + 1)
                if k == 0:
                    tensor.wait_ge(sems.p_cv, cv_base + 32)
                    tensor.wait_ge(sems.p_ve, ve_base + 4 * mc4)
                for m4 in range(4):
                    for c5 in range(2):
                        tensor.matmul(out=pq[m4][:, 512 * c5:512 * (c5 + 1)],
                                      lhsT=p.wbf[iw % 3][:, 128 * m4:128 * (m4 + 1)],
                                      rhs=p.xh[:, 1024 * k + 512 * c5:1024 * k + 512 * (c5 + 1)],
                                      start=(k == 0), stop=(k == 31)).then_inc(sems.p_mm, 1)

    def p_proj_vector(vector, c, mm_base, sk_base, ve_base, cosg, sing, out_of_j, cs_val):
        for mc4 in range(8):
            for m4 in range(4):
                j = 4 * mc4 + m4
                if j == 0:
                    vector.wait_ge(sems.p_cs, cs_val * DMA)
                vector.wait_ge(sems.p_mm, mm_base + 256 * (mc4 + 1))
                if j > 0:
                    vector.wait_ge(sems.p_ve, ve_base + j)
                dst = out_of_j(j)
                if dst is None:
                    dstap = p.kbf[j % 2]
                    if j >= 2 or sk_base > 0:
                        vector.wait_ge(p_k[j % 2], (sk_base + j // 2) * DMA)
                    if j < 2 and sk_base > 0:
                        vector.wait_ge(p_v[j % 2], 4 * 32 * DMA)
                else:
                    dstap = dst
                pp = pq[m4]
                ch = c["ch"]
                vector.tensor_mul(out=p.rt0[:], in0=pp[0:64, :], in1=cosg[:]).then_inc(sems.p_ch, 1)
                vector.tensor_mul(out=p.rt1[:], in0=pp[64:128, :], in1=sing[:]).then_inc(sems.p_ch, 1)
                vector.wait_ge(sems.p_ch, ch + 2)
                vector.tensor_sub(out=dstap[0:64, :], in0=p.rt0[:], in1=p.rt1[:])
                vector.tensor_mul(out=p.rt2[:], in0=pp[0:64, :], in1=sing[:]).then_inc(sems.p_ch, 1)
                vector.tensor_mul(out=p.rt3[:], in0=pp[64:128, :], in1=cosg[:]).then_inc(sems.p_ch, 1)
                vector.wait_ge(sems.p_ch, ch + 4)
                c["ch"] += 4
                vector.tensor_add(out=dstap[64:128, :], in0=p.rt2[:],
                                  in1=p.rt3[:]).then_inc(sems.p_ve, 1)
                c["ve"] += 1

    def p_v_sync(sync, c, th, ve_base, st_base):
        XB = 288 * th
        for nc8 in range(8):
            if nc8 >= 1:
                for tokm in range(8):
                    e = (nc8 - 1) * 8 + tokm
                    sync.wait_ge(sems.p_ve, ve_base + e + 1)
                    for h4 in range(4):
                        sync.dma_start(out=t.v_sp[4 * (nc8 - 1) + h4, 8 * th + tokm],
                                       in_=p.vbf[e % 2][:, 128 * h4:128 * (h4 + 1)]).then_inc(p_v[e % 2], DMA)
            for k in range(32):
                i = nc8 * 32 + k
                if i >= 2:
                    sync.wait_ge(sems.p_cv, XB + 32 + i - 1)
                sync.dma_start(out=p.xstage[i % 2][:, 0:512],
                               in_=t.wv[128 * k:128 * k + 128, 512 * nc8:512 * nc8 + 512]).then_inc(p_x[i % 2], DMA)
        for tokm in range(8):
            e = 7 * 8 + tokm
            sync.wait_ge(sems.p_ve, ve_base + e + 1)
            for h4 in range(4):
                sync.dma_start(out=t.v_sp[4 * 7 + h4, 8 * th + tokm],
                               in_=p.vbf[e % 2][:, 128 * h4:128 * (h4 + 1)]).then_inc(p_v[e % 2], DMA)

    def p_v_scalar(scalar, c, th, xfill_base, mm_base):
        XB = 288 * th
        for nc8 in range(8):
            for k in range(32):
                i = nc8 * 32 + k
                scalar.wait_ge(p_x[i % 2], (xfill_base + i // 2 + 1) * DMA)
                if nc8 >= 2:
                    scalar.wait_ge(sems.p_mm, mm_base + 256 * (nc8 - 1))
                scalar.copy(out=p.wvbf[nc8 % 2][:, 512 * k:512 * (k + 1)],
                            in_=p.xstage[i % 2][:, 0:512]).then_inc(sems.p_cv, 1)

    def p_v_tensor(tensor, c, th, ve_base):
        XB = 288 * th
        for nc8 in range(8):
            for tokm in range(8):
                g = nc8 * 8 + tokm
                if g >= 2:
                    tensor.wait_ge(sems.p_ve, ve_base + g - 1)
                else:
                    tensor.wait_ge(sems.p_ve, ve_base)
                for k in range(32):
                    tensor.wait_ge(sems.p_cv, XB + 32 + nc8 * 32 + k + 1)
                    tensor.matmul(out=pq[0][:, 512 * (g % 2):512 * (g % 2) + 512],
                                  lhsT=p.xh[:, 1024 * k + 128 * tokm:1024 * k + 128 * (tokm + 1)],
                                  rhs=p.wvbf[nc8 % 2][:, 512 * k:512 * (k + 1)],
                                  start=(k == 0), stop=(k == 31)).then_inc(sems.p_mm, 1)

    def p_v_vector(vector, c, th, mm_base):
        for nc8 in range(8):
            for tokm in range(8):
                e = nc8 * 8 + tokm
                vector.wait_ge(sems.p_mm, mm_base + (e + 1) * 32)
                if e >= 2:
                    vector.wait_ge(p_v[e % 2], 4 * (32 * th + e // 2) * DMA)
                elif th == 1:
                    vector.wait_ge(p_v[e % 2], 4 * 32 * DMA)
                if e < 2:
                    vector.wait_ge(p_k[e % 2], (16 * (th + 1)) * DMA)
                vector.tensor_copy(out=p.vbf[e % 2][:],
                                   in_=pq[0][:, 512 * (e % 2):512 * (e % 2) + 512]).then_inc(sems.p_ve, 1)
                c["ve"] += 1

    PMM_H = 4096
    PVE_H = 96
    P_TOT_MM = 2 * PMM_H + 2048
    P_TOT_VE = 2 * PVE_H + 32
    P_TOT_CV = 608

    def p_sync(sync):
        c = {}
        for th in range(2):
            sl = slice(1024 * th, 1024 * (th + 1))
            p_xl_sync(sync, c, t.xT[:, sl], t.cos_k[:, sl], t.sin_k[:, sl],
                      p.cosk, p.sink, 288 * th)
            def kspill(j, th=th):
                return t.kT_sp.rearrange("(a b) c -> a b c", b=128)[j, :, sl]
            p_wload_queue(sync, c, t.wk, [IW_K0, IW_K1][th], 512, 0,
                          PVE_H * th, kspill, 16 * th)
            p_v_sync(sync, c, th, PVE_H * th + 32, 0)
            if th == 0:
                sync.wait_ge(sems.p_mm, PMM_H)
                sync.wait_ge(sems.p_ve, PVE_H)
                sync.wait_ge(sems.p_cv, 288)
        sync.wait_ge(sems.p_mm, 2 * PMM_H)
        sync.wait_ge(sems.p_cv, 576)
        sync.dma_start(out=p.cosq[:], in_=t.cos_q[:]).then_inc(sems.p_cs, DMA)
        sync.dma_start(out=p.sinq[:], in_=t.sin_q[:]).then_inc(sems.p_cs, DMA)
        for k in range(32):
            if k >= 2:
                sync.wait_ge(sems.p_cv, 576 + k - 1)
            sync.dma_start(out=p.xstage[k % 2][:],
                           in_=t.x_own[128 * k:128 * k + 128, :]).then_inc(p_x[k % 2], DMA)
        p_wload_queue(sync, c, t.wq, IW_Q, 512, 0, 2 * PVE_H, None, 0)
        sync.wait_ge(p_k[0], 32 * DMA)
        sync.wait_ge(p_k[1], 32 * DMA)
        sync.wait_ge(sems.p_ve, P_TOT_VE)
        sync.wait_ge(p_x[0], 304 * DMA)
        sync.wait_ge(p_x[1], 304 * DMA)
        sync.wait_ge(p_v[0], 2 * 32 * 4 * DMA)
        sync.wait_ge(p_v[1], 2 * 32 * 4 * DMA)
        sync.wait_ge(sems.p_cs, 6 * DMA)

    def p_gpsimd(gpsimd):
        pass

    def p_vector(vector):
        c = {"cv": 0, "ve": 0, "ch": 0}
        for th in range(2):
            p_xl_vector(vector, 144 * th)
            p_proj_vector(vector, c, PMM_H * th, 16 * th, PVE_H * th,
                          p.cosk, p.sink, lambda j: None, 2 * (th + 1))
            p_v_vector(vector, c, th, PMM_H * th + 2048)
        p_xl_vector(vector, 288)
        p_proj_vector(vector, c, 2 * PMM_H, 0, 2 * PVE_H,
                      p.cosq, p.sinq,
                      lambda j: p.qt[:, 1024 * j:1024 * (j + 1)], 6)

    def p_scalar(scalar):
        c = {}
        for th in range(2):
            p_proj_scalar(scalar, [IW_K0, IW_K1][th], 512, PMM_H * th, 8)
            p_v_scalar(scalar, c, th, 144 * th + 16, PMM_H * th + 2048)
        p_proj_scalar(scalar, IW_Q, 512, 2 * PMM_H, 8)

    def p_tensor(tensor):
        for th in range(2):
            p_proj_tensor(tensor, [IW_K0, IW_K1][th], PMM_H * th, 288 * th, PVE_H * th)
            p_v_tensor(tensor, None, th, PVE_H * th + 32)
        p_proj_tensor(tensor, IW_Q, 2 * PMM_H, 576, 2 * PVE_H)

    a = NS()
    a.kh = [sbuf(f"a_kh{i}", R3 + i * 4096, [128, 2048], BF16) for i in range(2)]
    a.vh = [sbuf(f"a_vh{i}", R3 + 8192 + i * 4096, [128, 2048], BF16) for i in range(2)]
    a.e = [sbuf(f"a_e{i}", R3 + 16384 + i * 1024, [128, 512], BF16) for i in range(4)]
    a.attn = sbuf("a_attn", R3 + 20480, [128, 32 * 1024], BF16)
    a.mask = sbuf("a_mask", STAGE, [128, 16 * 1024], U8)
    a.recb = [sbuf(f"a_recb{i}", STAGE + 16384 + i * 2048, [128, 512], F32) for i in range(2)]

    A_SEQ = ["S0", "S1", "S2", "S3"]
    for k in range(12):
        A_SEQ += [f"sum{k}", f"AV{k}", f"S{k + 4}"]
    for k in range(12, 16):
        A_SEQ += [f"sum{k}", f"AV{k}"]
    A_POS = {tag: i + 1 for i, tag in enumerate(A_SEQ)}
    A_MM_G = 48
    A_VE_G = 18
    A_EX_G = 16

    a_kv = [sems.a_k0, sems.a_k1]
    a_r = [sems.a_r0, sems.a_r1]
    a_b = [sems.a_b0, sems.a_b1]

    def a_sync(sync):
        sync.dma_start(out=a.mask[:], in_=t.mask[:]).then_inc(sems.a_sm, DMA)
        for h in range(H):
            if h >= 2:
                sync.wait_ge(sems.a_mm, 96 * (h - 1))
            sync.dma_start(out=a.kh[h % 2][:], in_=t.kT_sp[128 * h:128 * h + 128, :]).then_inc(a_kv[h % 2], DMA)
            sync.dma_start(out=a.vh[h % 2][:],
                           in_=t.v_sp[h].rearrange("t p d -> p t d")).then_inc(a_kv[h % 2], DMA)
            for qc in range(2):
                idx = 2 * h + qc
                sync.wait_ge(sems.a_ve, A_VE_G * idx + 17)
                if idx >= 2:
                    sync.wait_ge(sems.a_ve, A_VE_G * (idx - 1))
                sync.dma_start(out=t.rows_sp[idx % 4, 0:512],
                               in_=s.rows[0:1, 512 * (idx % 3):512 * (idx % 3) + 512]).then_inc(a_r[idx % 2], DMA)
                sync.wait_ge(a_r[idx % 2], (idx // 2 + 1) * DMA)
                sync.dma_start(out=a.recb[idx % 2][:],
                               in_=t.rows_sp[idx % 4, 0:512].unsqueeze(0).partition_broadcast(128)).then_inc(a_b[idx % 2], DMA)

    def a_tensor(tensor):
        for h in range(H):
            tensor.wait_ge(a_kv[h % 2], 2 * (h // 2 + 1) * DMA)
            for qc in range(2):
                idx = 2 * h + qc
                base = A_MM_G * idx
                par = idx % 2
                for tag in A_SEQ:
                    kind = "S" if tag[0] == "S" else ("sum" if tag[:3] == "sum" else "AV")
                    k = int(tag[len(kind):])
                    if kind == "S":
                        pS = [pq[0][:, 0:512], pq[0][:, 512:1024],
                              pq[1][:, 0:512], pq[1][:, 512:1024]][k % 4]
                        if k >= 4:
                            tensor.wait_ge(sems.a_ex, A_EX_G * idx + k - 3)
                        elif idx >= 1:
                            tensor.wait_ge(sems.a_ex, A_EX_G * idx)
                        tensor.matmul(out=pS, lhsT=a.kh[h % 2][:, 128 * k:128 * k + 128],
                                      rhs=p.qt[:, 1024 * h + 512 * qc:1024 * h + 512 * (qc + 1)],
                                      start=True, stop=True).then_inc(sems.a_mm, 1)
                    elif kind == "sum":
                        tensor.wait_ge(sems.a_ve, A_VE_G * idx + k + 1)
                        if k == 0 and idx >= 2:
                            tensor.wait_ge(sems.a_ve, A_VE_G * (idx - 1))
                        tensor.matmul(out=pq[3][0:1, 512 * par:512 * par + 512],
                                      lhsT=s.ones[:], rhs=a.e[k % 4][:],
                                      start=(k == 0), stop=(k == 15)).then_inc(sems.a_mm, 1)
                    else:
                        tensor.matmul(out=pq[2][:, 512 * par:512 * par + 512],
                                      lhsT=a.vh[h % 2][:, 128 * k:128 * k + 128],
                                      rhs=a.e[k % 4][:],
                                      start=(k == 0), stop=(k == 15)).then_inc(sems.a_mm, 1)

    def a_scalar(scalar):
        for h in range(H):
            for qc in range(2):
                idx = 2 * h + qc
                for k in range(16):
                    pS = [pq[0][:, 0:512], pq[0][:, 512:1024],
                          pq[1][:, 0:512], pq[1][:, 512:1024]][k % 4]
                    scalar.wait_ge(sems.a_mm, A_MM_G * idx + A_POS[f"S{k}"])
                    if k >= 4:
                        scalar.wait_ge(sems.a_mm, A_MM_G * idx + A_POS[f"AV{k - 4}"])
                    elif idx >= 1:
                        scalar.wait_ge(sems.a_mm, A_MM_G * (idx - 1) + A_POS[f"AV{k + 12}"])
                    scalar.activation(out=a.e[k % 4][:], in_=pS, func=AF.Exp,
                                      scale=SCALE).then_inc(sems.a_ex, 1)

    def a_vector(vector):
        vector.wait_ge(sems.a_sm, DMA)
        for h in range(H):
            for qc in range(2):
                idx = 2 * h + qc
                par = idx % 2
                for k in range(16):
                    vector.wait_ge(sems.a_ex, A_EX_G * idx + k + 1)
                    msl = a.mask[:, 1024 * k + 512 * qc:1024 * k + 512 * (qc + 1)]
                    vector.tensor_tensor(out=a.e[k % 4][:], in0=a.e[k % 4][:],
                                         in1=msl, op=ALU.mult).then_inc(sems.a_ve, 1)
                vector.wait_ge(sems.a_mm, A_MM_G * idx + A_POS["sum15"])
                if idx >= 3:
                    vector.wait_ge(a_r[(idx - 3) % 2], ((idx - 3) // 2 + 1) * DMA)
                vector.reciprocal(out=s.rows[0:1, 512 * (idx % 3):512 * (idx % 3) + 512],
                                  in_=pq[3][0:1, 512 * par:512 * par + 512]).then_inc(sems.a_ve, 1)
                vector.wait_ge(a_b[idx % 2], (idx // 2 + 1) * DMA)
                vector.wait_ge(sems.a_mm, A_MM_G * idx + A_POS["AV15"])
                vector.tensor_tensor(out=a.attn[:, 1024 * h + 512 * qc:1024 * h + 512 * (qc + 1)],
                                     in0=pq[2][:, 512 * par:512 * par + 512],
                                     in1=a.recb[idx % 2][:], op=ALU.mult).then_inc(sems.a_ve, 1)

    A_TOT_VE = A_VE_G * 64
    A_TOT_MM = A_MM_G * 64

    w = NS()
    w.tx = p.qt
    w.ab = sbuf("w_ab", CONST2, [128, 1024], F32)
    w.bb = sbuf("w_bb", CONST2 + 4096, [128, 1024], F32)
    w.u = sbuf("w_u", CONST2 + 8192, [128, 1024], F32)
    w.x2 = [sbuf(f"w_x2{i}", CONST2 + 12288 + i * 4096, [128, 1024], F32) for i in range(2)]
    w.rows = sbuf("w_rows", CONST2 + 20480, [128, 1024], F32)

    def w_sync(sync):
        for mc2 in range(16):
            for m2 in range(2):
                m = 2 * mc2 + m2
                if m >= 2:
                    sync.wait_ge(sems.w_ve, m - 1)
                sync.dma_start(out=p.xstage[m % 2][:],
                               in_=t.x_own[128 * m:128 * m + 128, :]).then_inc(p_x[m % 2], DMA)
            for k in range(32):
                i = mc2 * 32 + k
                iw = 768 + i
                if iw >= 3:
                    sync.wait_ge(sems.p_wl, iw - 2)
                sync.dma_start(out=p.wstage[iw % 3][:, 0:256],
                               in_=t.wo[128 * k:128 * k + 128, 256 * mc2:256 * mc2 + 256]).then_inc(p_w[iw % 3], DMA)
        sync.wait_ge(sems.w_ve, 37)
        sync.dma_start(out=t.rows_sp[4, :], in_=w.u[0:1, :]).then_inc(sems.w_rw, DMA)
        sync.dma_start(out=t.rows_sp[5, :], in_=w.rows[0:1, :]).then_inc(sems.w_rw, DMA)
        sync.wait_ge(sems.w_rw, 2 * DMA)
        sync.dma_start(out=w.ab[:], in_=t.rows_sp[4, :].unsqueeze(0).partition_broadcast(128)).then_inc(sems.w_rw, DMA)
        sync.dma_start(out=w.bb[:], in_=t.rows_sp[5, :].unsqueeze(0).partition_broadcast(128)).then_inc(sems.w_rw, DMA)
        for m in range(32):
            sync.wait_ge(sems.w_ve, 32 + 5 + 3 * m + 2)
            sync.dma_start(out=t.xmid[128 * m:128 * m + 128, :],
                           in_=w.x2[m % 2][:]).then_inc([sems.w_s0, sems.w_s1][m % 2], DMA)

    def w_scalar(scalar):
        for mc2 in range(16):
            for k in range(32):
                i = mc2 * 32 + k
                iw = 768 + i
                scalar.wait_ge(p_w[iw % 3], (iw // 3 + 1) * DMA)
                if i >= 3:
                    scalar.wait_ge(sems.w_mm, (i - 2) * 4)
                scalar.copy(out=p.wbf[iw % 3][:, 0:256],
                            in_=p.wstage[iw % 3][:, 0:256]).then_inc(sems.p_wl, 1)
            for m2 in range(2):
                m = (mc2 - 1) * 2 + m2
                if 0 <= m:
                    scalar.wait_ge(sems.w_ve, m + 1)
                    if m >= 2:
                        scalar.wait_ge(sems.w_st, 4 * (m - 1))
                    scalar.activation(out=p.kbf[m % 2][:], in_=w.tx[:, 1024 * m:1024 * (m + 1)],
                                      func=AF.Square).then_inc(sems.w_sc, 1)
        for m in (30, 31):
            scalar.wait_ge(sems.w_ve, m + 1)
            scalar.wait_ge(sems.w_st, 4 * (m - 1))
            scalar.activation(out=p.kbf[m % 2][:], in_=w.tx[:, 1024 * m:1024 * (m + 1)],
                              func=AF.Square).then_inc(sems.w_sc, 1)
        scalar.wait_ge(sems.w_ve, 34)
        scalar.activation(out=pq[2][0:1, 0:1024], in_=w.rows[0:1, :],
                          func=AF.Square).then_inc(sems.w_sc, 1)
        scalar.wait_ge(sems.w_ve, 36)
        scalar.activation(out=pq[2][0:1, 0:1024], in_=w.u[0:1, :],
                          func=AF.Sqrt).then_inc(sems.w_sc, 1)

    def w_tensor(tensor):
        for mc2 in range(16):
            for k in range(32):
                i = mc2 * 32 + k
                iw = 768 + i
                tensor.wait_ge(sems.p_wl, 768 + i + 1)
                if k == 0:
                    if mc2 >= 1:
                        tensor.wait_ge(sems.w_ve, 2 * mc2)
                for m2 in range(2):
                    for c5 in range(2):
                        tensor.matmul(out=pq[m2][:, 512 * c5:512 * (c5 + 1)],
                                      lhsT=p.wbf[iw % 3][:, 128 * m2:128 * (m2 + 1)],
                                      rhs=a.attn[:, 1024 * k + 512 * c5:1024 * k + 512 * (c5 + 1)],
                                      start=(k == 0), stop=(k == 31)).then_inc(sems.w_mm, 1)
            for m2 in range(2):
                m = (mc2 - 1) * 2 + m2
                if 0 <= m < 32:
                    tensor.wait_ge(sems.w_ve, m + 1)
                    tensor.wait_ge(sems.w_sc, m + 1)
                    for c5 in range(2):
                        tensor.matmul(out=pq[2][0:1, 512 * c5:512 * (c5 + 1)],
                                      lhsT=s.ones[:],
                                      rhs=w.tx[:, 1024 * m + 512 * c5:1024 * m + 512 * (c5 + 1)],
                                      start=(m == 0), stop=(m == 31)).then_inc(sems.w_st, 1)
                        tensor.matmul(out=pq[3][0:1, 512 * c5:512 * (c5 + 1)],
                                      lhsT=s.ones[:],
                                      rhs=p.kbf[m % 2][:, 512 * c5:512 * (c5 + 1)],
                                      start=(m == 0), stop=(m == 31)).then_inc(sems.w_st, 1)
        for m in (30, 31):
            tensor.wait_ge(sems.w_ve, m + 1)
            tensor.wait_ge(sems.w_sc, m + 1)
            for c5 in range(2):
                tensor.matmul(out=pq[2][0:1, 512 * c5:512 * (c5 + 1)],
                              lhsT=s.ones[:],
                              rhs=w.tx[:, 1024 * m + 512 * c5:1024 * m + 512 * (c5 + 1)],
                              start=(m == 0), stop=(m == 31)).then_inc(sems.w_st, 1)
                tensor.matmul(out=pq[3][0:1, 512 * c5:512 * (c5 + 1)],
                              lhsT=s.ones[:],
                              rhs=p.kbf[m % 2][:, 512 * c5:512 * (c5 + 1)],
                              start=(m == 0), stop=(m == 31)).then_inc(sems.w_st, 1)

    W_TOT_MM = 2048

    def w_vector(vector):
        for mc2 in range(16):
            for m2 in range(2):
                m = mc2 * 2 + m2
                vector.wait_ge(sems.w_mm, 128 * (mc2 + 1))
                vector.wait_ge(p_x[m % 2], (304 + m // 2 + 1) * DMA)
                vector.tensor_add(out=w.tx[:, 1024 * m:1024 * (m + 1)],
                                  in0=pq[m2][:, 0:1024],
                                  in1=p.xstage[m % 2][:]).then_inc(sems.w_ve, 1)
        vector.wait_ge(sems.w_mm, W_TOT_MM)
        vector.wait_ge(sems.w_st, 128)
        vector.tensor_scalar(out=w.rows[0:1, :], in0=pq[2][0:1, 0:1024],
                             scalar1=1.0 / D, scalar2=None, op0=ALU.mult).then_inc(sems.w_ve, 1)
        vector.tensor_scalar(out=w.u[0:1, :], in0=pq[3][0:1, 0:1024],
                             scalar1=1.0 / D, scalar2=None, op0=ALU.mult).then_inc(sems.w_ve, 1)
        vector.wait_ge(sems.w_sc, 33)
        vector.tensor_sub(out=w.u[0:1, :], in0=w.u[0:1, :],
                          in1=pq[2][0:1, 0:1024]).then_inc(sems.w_ve, 1)
        vector.wait_ge(sems.w_ve, 35)
        vector.tensor_scalar(out=w.u[0:1, :], in0=w.u[0:1, :],
                             scalar1=EPS, scalar2=None, op0=ALU.add).then_inc(sems.w_ve, 1)
        vector.wait_ge(sems.w_sc, 34)
        vector.reciprocal(out=w.u[0:1, :], in_=pq[2][0:1, 0:1024]).then_inc(sems.w_ve, 1)
        vector.wait_ge(sems.w_rw, 4 * DMA)
        for m in range(32):
            if m >= 2:
                vector.wait_ge([sems.w_s0, sems.w_s1][m % 2], (m // 2) * DMA)
            vector.tensor_sub(out=w.u[:], in0=w.tx[:, 1024 * m:1024 * (m + 1)],
                              in1=w.bb[:]).then_inc(sems.w_ve, 1)
            vector.wait_ge(sems.w_ve, 32 + 5 + 3 * m + 1)
            vector.tensor_mul(out=w.x2[m % 2][:], in0=w.u[:], in1=w.ab[:]).then_inc(sems.w_ve, 1)
            vector.wait_ge(sems.w_ve, 32 + 5 + 3 * m + 2)
            vector.tensor_copy(out=w.tx[:, 1024 * m:1024 * (m + 1)],
                               in_=w.x2[m % 2][:]).then_inc(sems.w_ve, 1)

    W_TOT_VE = 32 + 5 + 96

    f = NS()
    f.wring = [sbuf(f"f_wr{i}", CONST2 + i * 8192, [128, 8 * 512], BF16) for i in range(3)]
    f.hreg = sbuf("f_hreg", R3, [128, KB_DN * 512], BF16)
    f.xs = sbuf("f_xs", STAGE, [KB_GU, 512], BF16)
    f.hs = sbuf("f_hs", STAGE + 1024, [KB_DN, 512], BF16)
    f.nbg = [sbuf(f"f_nbg{i}", STAGE + 2048 + i * 1024, [KB_DN, 512], BF16) for i in range(2)]
    f.gsil = [sbuf(f"f_gsil{i}", STAGE + 4096 + i * 1024, [128, 512], BF16) for i in range(2)]
    f.ind = [sbuf(f"f_ind{i}", STAGE + 6144 + i * 256, [128, KB_DN], BF16) for i in range(2)]
    f.xm = [sbuf(f"f_xm{i}", STAGE + 6656 + i * 2048, [128, 512], F32) for i in range(2)]
    f.t2 = [sbuf(f"f_t2{i}", STAGE + 10752 + i * 2048, [128, 512], F32) for i in range(2)]
    f.t2bf = [sbuf(f"f_t2bf{i}", STAGE + 14848 + i * 1024, [128, 512], BF16) for i in range(2)]
    f.sq = [sbuf(f"f_sq{i}", STAGE + 16896 + i * 1024, [128, 512], BF16) for i in range(2)]
    f.rows2 = sbuf("f_rows2", STAGE + 18944, [128, 512], F32)
    f.ab2 = sbuf("f_ab2", STAGE, [128, 512], F32)
    f.bb2 = sbuf("f_bb2", STAGE + 2048, [128, 512], F32)
    f.t2r = [sbuf(f"f_t2r{i}", STAGE + 4096 + i * 2048, [128, 512], F32) for i in range(2)]
    f.ou = sbuf("f_ou", STAGE + 8704, [128, 512], F32)
    f.ob = [sbuf(f"f_ob{i}", STAGE + 10752 + i * 2048, [128, 512], F32) for i in range(2)]

    fq_sems = [sems.f_q0, sems.f_q1, sems.f_q2]
    fn_sems = [sems.f_n0, sems.f_n1]
    fm_sems = [sems.f_m0, sems.f_m1]
    ft_sems = [sems.f_t0, sems.f_t1]
    fr_sems = [sems.f_r0, sems.f_r1]
    fo_sems = [sems.f_o0, sems.f_o1]

    DN_KG = [8] * 10 + [6]
    F_FILLS_CH = 43 * 4 + 8 * 11
    def f_mm_after_gu_fill(mc, kq):
        return 32 + mc * 134 + 4 + 128 * 0 + (kq + 1) * 32
    def f_mm_after_dn_fill(mc8, kgi):
        base = 32 + 43 * 134
        return base + mc8 * 348 + 4 + 4 * (sum(DN_KG[:kgi]) + DN_KG[kgi])
    F_MM_CH = 32 + 43 * 134 + 8 * 348
    F_VE_XS = 1
    def f_ve_ind(mc, j2):
        return 1 + mc * 4 + j2 + 1
    def f_ve_h(mc, j2):
        return 1 + mc * 4 + 2 + j2 + 1
    F_VE_GU = 1 + 43 * 4
    F_VE_HS = F_VE_GU + 1
    def f_ve_t2(m):
        return F_VE_HS + 2 * m + 1
    def f_ve_t2bf(m):
        return F_VE_HS + 2 * m + 2
    F_VE_DN = F_VE_HS + 64
    F_VE_ROWS = F_VE_DN + 5
    def f_ve_ob(m):
        return F_VE_ROWS + 2 * m + 2
    F_VE_CH = F_VE_ROWS + 64
    F_SC_CH = 86 + 32 + 2
    F_S2_CH = 64

    def f_sync(sync, c):
        mmb = F_MM_CH * c
        veb = F_VE_CH * c
        fqb = F_FILLS_CH * c
        fill_done = []
        for mc in range(MC_GU):
            for kq in range(4):
                fill_done.append(mmb + 32 + mc * 134 + 4 + 32 * (kq + 1))
        for mc8 in range(MC_DN):
            k0 = 0
            for kg in DN_KG:
                k0 += kg
                fill_done.append(mmb + 32 + 43 * 134 + mc8 * 348 + 4 + 4 * k0)
        fq = fqb
        nfill_gu = [26 * c, 25 * c]
        for mc in range(MC_GU):
            if mc >= 2:
                sync.wait_ge(sems.f_mm, mmb + 32 + (mc - 2) * 134 + 4)
            elif c > 0:
                sync.wait_ge(sems.f_mm, mmb)
            sync.dma_start(out=f.nbg[mc % 2][0:KB_GU, :],
                           in_=t.negbg_gu[:, 512 * mc:512 * (mc + 1)]).then_inc(fn_sems[mc % 2], DMA)
            for kq in range(4):
                pf = fq - 3
                if pf >= 0 and pf >= fqb:
                    sync.wait_ge(sems.f_mm, fill_done[pf - fqb])
                elif pf >= 0:
                    sync.wait_ge(sems.f_mm, mmb)
                sync.dma_start(out=f.wring[fq % 3][:],
                               in_=t.wgu_c[mc, 8 * kq:8 * kq + 8].rearrange("k p n -> p k n")).then_inc(fq_sems[fq % 3], DMA)
                fq += 1
        for mc8 in range(MC_DN):
            if mc8 >= 2:
                sync.wait_ge(sems.f_mm, mmb + 32 + 43 * 134 + (mc8 - 2) * 348 + 4)
            else:
                sync.wait_ge(sems.f_mm, mmb + 32 + (42 - (mc8 % 2)) * 134 + 4)
            sync.dma_start(out=f.nbg[mc8 % 2][0:KB_DN, :],
                           in_=t.negbg_dn[:, 512 * mc8:512 * (mc8 + 1)]).then_inc(fn_sems[mc8 % 2], DMA)
            k0 = 0
            for kgi, kg in enumerate(DN_KG):
                pf = fq - 3
                if pf >= fqb:
                    sync.wait_ge(sems.f_mm, fill_done[pf - fqb])
                sync.dma_start(out=f.wring[fq % 3][:, 0:512 * kg],
                               in_=t.wdn_c[mc8, k0:k0 + kg].rearrange("k p n -> p k n")).then_inc(fq_sems[fq % 3], DMA)
                fq += 1
                k0 += kg
            for m2 in range(4):
                m = 4 * mc8 + m2
                if m >= 2:
                    sync.wait_ge(sems.f_ve, veb + f_ve_t2(m - 2))
                elif c > 0:
                    sync.wait_ge(sems.f_ve, veb)
                sync.dma_start(out=f.xm[m % 2][:],
                               in_=t.xmid[128 * m:128 * m + 128, 512 * c:512 * (c + 1)]).then_inc(fm_sems[m % 2], DMA)
            for m2 in range(4):
                m = 4 * mc8 + m2
                sync.wait_ge(sems.f_ve, veb + f_ve_t2(m))
                sync.dma_start(out=t.t2_sp[128 * m:128 * m + 128, :],
                               in_=f.t2[m % 2][:]).then_inc(ft_sems[m % 2], DMA)
        sync.wait_ge(sems.f_ve, veb + F_VE_ROWS)
        sync.dma_start(out=t.rows_sp[6, 0:512], in_=f.ou[0:1, :]).then_inc(sems.f_ab, DMA)
        sync.dma_start(out=t.rows_sp[7, 0:512], in_=f.rows2[0:1, :]).then_inc(sems.f_ab, DMA)
        sync.wait_ge(sems.f_ab, (4 * c + 2) * DMA)
        sync.dma_start(out=f.ab2[:], in_=t.rows_sp[6, 0:512].unsqueeze(0).partition_broadcast(128)).then_inc(sems.f_ab, DMA)
        sync.dma_start(out=f.bb2[:], in_=t.rows_sp[7, 0:512].unsqueeze(0).partition_broadcast(128)).then_inc(sems.f_ab, DMA)
        for m in range(32):
            if m >= 2:
                sync.wait_ge(sems.f_ve, veb + f_ve_ob(m - 2))
            sync.dma_start(out=f.t2r[m % 2][:],
                           in_=t.t2_sp[128 * m:128 * m + 128, :]).then_inc(fr_sems[m % 2], DMA)
            sync.wait_ge(sems.f_ve, veb + f_ve_ob(m) if False else veb + F_VE_ROWS + 2 * m + 2)
            sync.dma_start(out=t.outT[128 * m:128 * m + 128, 512 * c:512 * (c + 1)],
                           in_=f.ob[m % 2][:]).then_inc(fo_sems[m % 2], DMA)

    def f_tensor(tensor, c):
        mmb = F_MM_CH * c
        veb = F_VE_CH * c
        s2b = F_S2_CH * c
        fqb = F_FILLS_CH * c
        fq = fqb
        if c > 0:
            tensor.wait_ge(sems.f_ve, F_VE_CH * c)
        for k in range(32):
            tensor.matmul(out=pq[3][0:KB_GU, 0:512],
                          lhsT=s.ind_gu[:, 32 * k:32 * (k + 1)],
                          rhs=w.tx[:, 1024 * k + 512 * c:1024 * k + 512 * (c + 1)],
                          start=(k == 0), stop=(k == 31)).then_inc(sems.f_mm, 1)
        PQg = [pq[0], pq[1], pq[3]]
        pj = lambda j2, mc: PQg[(2 * mc + j2 // 2) % 3][:, 512 * (j2 % 2):512 * (j2 % 2) + 512]
        for mc in range(MC_GU):
            tensor.wait_ge(sems.f_ve, veb + F_VE_XS)
            tensor.wait_ge(fn_sems[mc % 2], ([26, 25][mc % 2] * c + mc // 2 + 1) * DMA)
            if mc >= 1:
                pprev = 2 * mc - 2
                tensor.wait_ge(sems.f_ve, veb + f_ve_h(pprev // 2, pprev % 2))
                tensor.wait_ge(sems.f_sc, F_SC_CH * c + max(2 * mc - 1, 0))
            for j2 in range(4):
                tensor.matmul(out=pj(j2, mc), lhsT=f.nbg[mc % 2][0:KB_GU, 128 * j2:128 * (j2 + 1)],
                              rhs=f.xs[:], start=True, stop=False).then_inc(sems.f_mm, 1)
            for kq in range(4):
                tensor.wait_ge(fq_sems[fq % 3], ((fq - fq % 3) // 3 + 1) * DMA)
                for k8 in range(8):
                    k = 8 * kq + k8
                    last = (k == 31)
                    for j2 in range(4):
                        tensor.matmul(out=pj(j2, mc),
                                      lhsT=f.wring[fq % 3][:, 512 * k8 + 128 * j2:512 * k8 + 128 * (j2 + 1)],
                                      rhs=w.tx[:, 1024 * k + 512 * c:1024 * k + 512 * (c + 1)],
                                      start=False, stop=last).then_inc(sems.f_mm, 1)
                fq += 1
            for j2 in range(2):
                jj = 2 * mc + j2
                tensor.wait_ge(sems.f_ve, veb + f_ve_h(mc, j2))
                tensor.matmul(out=pq[2][0:KB_DN, 0:512],
                              lhsT=f.ind[jj % 2][:],
                              rhs=f.hreg[:, 512 * jj:512 * (jj + 1)],
                              start=(jj == 0), stop=(jj == KB_DN - 1)).then_inc(sems.f_mm, 1)
        for mc8 in range(MC_DN):
            tensor.wait_ge(sems.f_ve, veb + F_VE_HS)
            tensor.wait_ge(sems.f_sc, F_SC_CH * c + 86)
            tensor.wait_ge(fn_sems[mc8 % 2], ([26, 25][mc8 % 2] * c + [22, 21][mc8 % 2] + mc8 // 2 + 1) * DMA)
            if mc8 >= 1:
                tensor.wait_ge(sems.f_ve, veb + f_ve_t2(4 * mc8 - 2))
            DSL = [pq[0][:, 0:512], pq[0][:, 512:1024], pq[1][:, 0:512],
                   pq[1][:, 512:1024], pq[2][:, 512:1024]]
            for j2 in range(4):
                m = 4 * mc8 + j2
                tensor.matmul(out=DSL[m % 5],
                              lhsT=f.nbg[mc8 % 2][0:KB_DN, 128 * j2:128 * (j2 + 1)],
                              rhs=f.hs[:], start=True, stop=False).then_inc(sems.f_mm, 1)
            k0 = 0
            for kgi, kg in enumerate(DN_KG):
                tensor.wait_ge(fq_sems[fq % 3], ((fq - fq % 3) // 3 + 1) * DMA)
                for kk in range(kg):
                    k = k0 + kk
                    last = (k == KB_DN - 1)
                    for j2 in range(4):
                        m = 4 * mc8 + j2
                        tensor.matmul(out=DSL[m % 5],
                                      lhsT=f.wring[fq % 3][:, 512 * kk + 128 * j2:512 * kk + 128 * (j2 + 1)],
                                      rhs=f.hreg[:, 512 * k:512 * (k + 1)],
                                      start=False, stop=last).then_inc(sems.f_mm, 1)
                fq += 1
                k0 += kg
            for m2 in range(4):
                m = 4 * mc8 + m2
                tensor.wait_ge(sems.f_ve, veb + f_ve_t2bf(m))
                tensor.wait_ge(sems.f_sc, F_SC_CH * c + 86 + m + 1)
                tensor.matmul(out=pq[3][0:1, 0:512], lhsT=s.ones[:],
                              rhs=f.t2bf[m % 2][:],
                              start=(m == 0), stop=(m == 31)).then_inc(sems.f_s2, 1)
                tensor.matmul(out=pq[3][0:1, 512:1024], lhsT=s.ones[:],
                              rhs=f.sq[m % 2][:],
                              start=(m == 0), stop=(m == 31)).then_inc(sems.f_s2, 1)

    def f_scalar(scalar, c):
        scb = F_SC_CH * c
        veb = F_VE_CH * c
        mmb = F_MM_CH * c
        PQg = [pq[0], pq[1], pq[3]]
        for mc in range(MC_GU):
            for j2 in range(2):
                jj = 2 * mc + j2
                gpos = mmb + 32 + mc * 134 + 4 + 128
                scalar.wait_ge(sems.f_mm, gpos)
                if jj >= 2:
                    scalar.wait_ge(sems.f_ve, veb + f_ve_h(*divmod(jj - 2, 2)))
                scalar.activation(out=f.gsil[jj % 2][:],
                                  in_=PQg[jj % 3][:, 0:512],
                                  func=AF.Silu).then_inc(sems.f_sc, 1)
        for m in range(32):
            scalar.wait_ge(sems.f_ve, veb + f_ve_t2bf(m))
            if m >= 2:
                scalar.wait_ge(sems.f_s2, F_S2_CH * c + 2 * (m - 1))
            scalar.activation(out=f.sq[m % 2][:], in_=f.t2bf[m % 2][:],
                              func=AF.Square).then_inc(sems.f_sc, 1)
        scalar.wait_ge(sems.f_ve, veb + F_VE_DN + 2)
        scalar.activation(out=pq[3][0:1, 0:512], in_=f.rows2[0:1, :],
                          func=AF.Square).then_inc(sems.f_sc, 1)
        scalar.wait_ge(sems.f_ve, veb + F_VE_DN + 4)
        scalar.activation(out=pq[3][0:1, 0:512], in_=f.ou[0:1, :],
                          func=AF.Sqrt).then_inc(sems.f_sc, 1)

    def f_vector(vector, c):
        veb = F_VE_CH * c
        mmb = F_MM_CH * c
        scb = F_SC_CH * c
        PQg2 = [pq[0], pq[1], pq[3]]
        DSL2 = [pq[0][:, 0:512], pq[0][:, 512:1024], pq[1][:, 0:512],
                pq[1][:, 512:1024], pq[2][:, 512:1024]]
        vector.wait_ge(sems.f_mm, mmb + 32)
        vector.tensor_copy(out=f.xs[:], in_=pq[3][0:KB_GU, 0:512]).then_inc(sems.f_ve, 1)
        for mc in range(MC_GU):
            for j2 in range(2):
                jj = 2 * mc + j2
                if jj >= 2:
                    vector.wait_ge(sems.f_mm, mmb + 32 + (mc - 1) * 134 + 133 + j2)
                vector.tensor_scalar(out=f.ind[jj % 2][:], in0=s.iota_row[:, 0:KB_DN],
                                     scalar1=s.gcolf_dn[:, jj:jj + 1], scalar2=None,
                                     op0=ALU.is_equal).then_inc(sems.f_ve, 1)
            for j2 in range(2):
                jj = 2 * mc + j2
                vector.wait_ge(sems.f_sc, scb + 2 * mc + j2 + 1)
                vector.tensor_tensor(out=f.hreg[:, 512 * jj:512 * (jj + 1)],
                                     in0=f.gsil[jj % 2][:],
                                     in1=PQg2[jj % 3][:, 512:1024],
                                     op=ALU.mult).then_inc(sems.f_ve, 1)
        vector.wait_ge(sems.f_mm, mmb + 32 + 43 * 134)
        vector.tensor_copy(out=f.hs[:], in_=pq[2][0:KB_DN, 0:512]).then_inc(sems.f_ve, 1)
        for mc8 in range(MC_DN):
            for m2 in range(4):
                m = 4 * mc8 + m2
                vector.wait_ge(sems.f_mm, mmb + 32 + 43 * 134 + mc8 * 348 + 4 + 4 * KB_DN)
                vector.wait_ge(fm_sems[m % 2], (16 * c + m // 2 + 1) * DMA)
                if m >= 2:
                    vector.wait_ge(ft_sems[m % 2], (16 * c + (m - 2) // 2 + 1) * DMA)
                elif c > 0:
                    vector.wait_ge(fo_sems[m % 2], 16 * c * DMA)
                vector.tensor_add(out=f.t2[m % 2][:], in0=DSL2[m % 5],
                                  in1=f.xm[m % 2][:]).then_inc(sems.f_ve, 1)
                vector.wait_ge(sems.f_ve, veb + f_ve_t2(m))
                if m >= 2:
                    vector.wait_ge(sems.f_sc, F_SC_CH * c + 85 + m)
                    vector.wait_ge(sems.f_s2, F_S2_CH * c + 2 * (m - 1))
                vector.tensor_copy(out=f.t2bf[m % 2][:], in_=f.t2[m % 2][:]).then_inc(sems.f_ve, 1)
        vector.wait_ge(sems.f_s2, F_S2_CH * (c + 1))
        vector.tensor_scalar(out=f.rows2[0:1, :], in0=pq[3][0:1, 0:512],
                             scalar1=1.0 / D, scalar2=None, op0=ALU.mult).then_inc(sems.f_ve, 1)
        vector.tensor_scalar(out=f.ou[0:1, :], in0=pq[3][0:1, 512:1024],
                             scalar1=1.0 / D, scalar2=None, op0=ALU.mult).then_inc(sems.f_ve, 1)
        vector.wait_ge(sems.f_sc, F_SC_CH * c + 119)
        vector.tensor_sub(out=f.ou[0:1, :], in0=f.ou[0:1, :],
                          in1=pq[3][0:1, 0:512]).then_inc(sems.f_ve, 1)
        vector.wait_ge(sems.f_ve, veb + F_VE_DN + 3)
        vector.tensor_scalar(out=f.ou[0:1, :], in0=f.ou[0:1, :],
                             scalar1=EPS, scalar2=None, op0=ALU.add).then_inc(sems.f_ve, 1)
        vector.wait_ge(sems.f_sc, F_SC_CH * (c + 1))
        vector.reciprocal(out=f.ou[0:1, :], in_=pq[3][0:1, 0:512]).then_inc(sems.f_ve, 1)
        vector.wait_ge(sems.f_ab, (4 * c + 4) * DMA)
        for m in range(32):
            vector.wait_ge(fr_sems[m % 2], (16 * c + m // 2 + 1) * DMA)
            if m >= 2:
                vector.wait_ge(fo_sems[m % 2], (16 * c + (m - 2) // 2 + 1) * DMA)
            vector.tensor_sub(out=f.ou[:], in0=f.t2r[m % 2][:],
                              in1=f.bb2[:]).then_inc(sems.f_ve, 1)
            vector.wait_ge(sems.f_ve, veb + F_VE_ROWS + 2 * m + 1)
            if m < 2:
                vector.wait_ge(ft_sems[m % 2], 16 * (c + 1) * DMA)
            vector.tensor_mul(out=f.ob[m % 2][:], in0=f.ou[:],
                              in1=f.ab2[:]).then_inc(sems.f_ve, 1)

    with nc.Block() as block:
        @block.sync
        def _(sync):
            nbar = 0
            if "D" in phases:
                d_sync(sync)
                sync.wait_ge(sems.d_st, d_tot_st * DMA)
                sync.wait_ge(sems.d_ld, d_tot_ld * DMA)
                for j, dwj in enumerate([sems.d_w0, sems.d_w1, sems.d_w2, sems.d_w3]):
                    sync.wait_ge(dwj, (d_tot_tt // 4) * DMA)
                if debug:
                    sync.dma_start(out=t.gbf_dump[:], in_=d.gbf_dn[:]).then_inc(sems.d_l0, DMA)
                    sync.dma_start(out=t.indT_dump[:], in_=d.indT[:]).then_inc(sems.d_l0, DMA)
                    sync.dma_start(out=t.indgu_dump[:], in_=s.ind_gu[:]).then_inc(sems.d_l0, DMA)
                    sync.wait_ge(sems.d_l0, (7 + 3) * DMA)
            nbar += 1
            sync.sem_inc(sems.bar, 1)
            if "P" in phases:
                sync.wait_ge(sems.bar, nbar)
                p_sync(sync)
                if debug:
                    sync.dma_start(out=t.qt_dump[:], in_=p.qt[:]).then_inc(sems.p_rp, DMA)
                    sync.wait_ge(sems.p_rp, DMA)
            nbar += 1
            sync.sem_inc(sems.bar, 1)
            if "A" in phases:
                sync.wait_ge(sems.bar, nbar)
                a_sync(sync)
                sync.wait_ge(sems.a_ve, A_TOT_VE)
                if debug:
                    sync.dma_start(out=t.attn_dump[:], in_=a.attn[:]).then_inc(sems.p_rp, DMA)
                    sync.wait_ge(sems.p_rp, 2 * DMA)
            nbar += 1
            sync.sem_inc(sems.bar, 1)
            if "W" in phases:
                sync.wait_ge(sems.bar, nbar)
                w_sync(sync)
                sync.wait_ge(sems.w_ve, W_TOT_VE)
                sync.wait_ge(sems.w_s0, 16 * DMA)
                sync.wait_ge(sems.w_s1, 16 * DMA)
            nbar += 1
            sync.sem_inc(sems.bar, 1)
            if "F" in phases:
                sync.wait_ge(sems.bar, nbar)
                for cch in range(2):
                    f_sync(sync, cch)
                sync.wait_ge(sems.f_o0, 32 * DMA)
                sync.wait_ge(sems.f_o1, 32 * DMA)

        @block.vector
        def _(vector):
            if "D" in phases:
                d_vector(vector)
            if "P" in phases:
                vector.wait_ge(sems.bar, 1)
                p_vector(vector)
            if "A" in phases:
                vector.wait_ge(sems.bar, 2)
                a_vector(vector)
            if "W" in phases:
                vector.wait_ge(sems.bar, 3)
                w_vector(vector)
            if "F" in phases:
                vector.wait_ge(sems.bar, 4)
                for cch in range(2):
                    f_vector(vector, cch)

        @block.tensor
        def _(tensor):
            if "D" in phases:
                d_tensor(tensor)
            if "P" in phases:
                tensor.wait_ge(sems.bar, 1)
                p_tensor(tensor)
            if "A" in phases:
                tensor.wait_ge(sems.bar, 2)
                a_tensor(tensor)
            if "W" in phases:
                tensor.wait_ge(sems.bar, 3)
                w_tensor(tensor)
            if "F" in phases:
                tensor.wait_ge(sems.bar, 4)
                for cch in range(2):
                    f_tensor(tensor, cch)

        @block.scalar
        def _(scalar):
            if "D" in phases:
                d_scalar(scalar)
            if "P" in phases:
                scalar.wait_ge(sems.bar, 1)
                p_scalar(scalar)
            if "A" in phases:
                scalar.wait_ge(sems.bar, 2)
                a_scalar(scalar)
            if "W" in phases:
                scalar.wait_ge(sems.bar, 3)
                w_scalar(scalar)
            if "F" in phases:
                scalar.wait_ge(sems.bar, 4)
                for cch in range(2):
                    f_scalar(scalar, cch)

        @block.gpsimd
        def _(gpsimd):
            if "P" in phases:
                gpsimd.wait_ge(sems.bar, 1)
                p_gpsimd(gpsimd)


    ctx.close()
    return nc


def prep_inputs(L0_attn_x, L0_attn_wq, L0_attn_wk, L0_attn_wv, L0_attn_wo,
                L0_ffn_qweight_gu, L0_ffn_qzeros_gu, L0_ffn_scales_gu, L0_ffn_g_idx_gu,
                L0_ffn_qweight_down, L0_ffn_qzeros_down, L0_ffn_scales_down, L0_ffn_g_idx_down):
    ph = full_perm(D)
    pi = full_perm(INTER)
    rp = rope_perm()
    blocks = []
    for j in range(KB_DN):
        blocks.append(pi[128 * j:128 * (j + 1)])
        blocks.append(INTER + pi[128 * j:128 * (j + 1)])
    pgu = np.concatenate(blocks)
    wp = []
    for j in range(KB_DN):
        wp.append(np.arange(16 * j, 16 * (j + 1)))
        wp.append(1376 + np.arange(16 * j, 16 * (j + 1)))
    word_perm = np.concatenate(wp)

    wq = np.ascontiguousarray(np.asarray(L0_attn_wq)[ph][:, rp])
    wk = np.ascontiguousarray(np.asarray(L0_attn_wk)[ph][:, rp])
    wv = np.ascontiguousarray(np.asarray(L0_attn_wv)[ph])
    wo = np.ascontiguousarray(np.asarray(L0_attn_wo)[:, ph])
    qw_gu = np.ascontiguousarray(np.asarray(L0_ffn_qweight_gu)[:, pgu])
    sc_gu = np.ascontiguousarray(np.asarray(L0_ffn_scales_gu)[:, pgu])
    gi_gu = np.ascontiguousarray(np.asarray(L0_ffn_g_idx_gu)[ph]).astype(np.int32)
    qw_dn = np.ascontiguousarray(np.asarray(L0_ffn_qweight_down)[:, ph])
    sc_dn = np.ascontiguousarray(np.asarray(L0_ffn_scales_down)[:, ph])
    gi_dn = np.ascontiguousarray(np.asarray(L0_ffn_g_idx_down)[pi]).astype(np.int32)
    qz_gu = np.ascontiguousarray(np.asarray(L0_ffn_qzeros_gu)[:, word_perm])
    qz_dn = np.ascontiguousarray(np.asarray(L0_ffn_qzeros_down))

    inv = 1.0 / (10000.0 ** (np.arange(0, HD, 2)[: HD // 2].astype(np.float32) / HD))
    tt = np.arange(S)
    f = np.outer(tt, inv).astype(np.float32)
    cos_k = np.ascontiguousarray(np.cos(f).T)
    sin_k = np.ascontiguousarray(np.sin(f).T)

    common = dict(
        wq=wq, wk=wk, wv=wv, wo=wo,
        qw_gu=qw_gu, qz_gu=qz_gu, sc_gu=sc_gu,
        qw_dn=qw_dn, qz_dn=qz_dn, sc_dn=sc_dn,
        gi_bc_gu=np.ascontiguousarray(np.broadcast_to(gi_gu[None, :], (KB_GU, D))),
        gi_col_gu=np.ascontiguousarray(gi_gu.reshape(KB_GU, 128).T),
        gi_bc_dn=np.ascontiguousarray(np.broadcast_to(gi_dn[None, :], (KB_DN, INTER))),
        gi_col_dn=np.ascontiguousarray(gi_dn.reshape(KB_DN, 128).T),
        cos_k=cos_k, sin_k=sin_k,
        ones_col=np.ones((128, 1), BF),
        iota_row=np.ascontiguousarray(np.broadcast_to(np.arange(KB_DN, dtype=np.float32), (128, KB_DN))),
        iota_col=np.arange(KB_DN, dtype=np.float32).reshape(KB_DN, 1),
        mask_col=np.ascontiguousarray(
            np.left_shift(np.uint32(15), 4 * (np.arange(128, dtype=np.uint32) // 16)).view(np.int32).reshape(128, 1)),
        pscale_col=(2.0 ** (-4.0 * (np.arange(128) // 16))).astype(np.float32).reshape(128, 1),
    )

    x = np.asarray(L0_attn_x)
    in_maps = []
    for c in range(N_CORES):
        b, half = c // 2, c % 2
        q0 = half * TOKQ
        xTb = np.ascontiguousarray(x[b].T[ph])
        m = dict(common)
        m["xT"] = xTb
        m["x_own"] = np.ascontiguousarray(xTb[:, q0:q0 + TOKQ])
        m["cos_q"] = np.ascontiguousarray(cos_k[:, q0:q0 + TOKQ])
        m["sin_q"] = np.ascontiguousarray(sin_k[:, q0:q0 + TOKQ])
        kv = (np.arange(16)[None, :, None] * 128 + np.arange(128)[:, None, None])
        qq = (q0 + np.arange(TOKQ))[None, None, :]
        mk = (kv <= qq).astype(np.uint8)
        m["mask"] = np.ascontiguousarray(mk.reshape(128, 16 * TOKQ))
        in_maps.append(m)
    return in_maps


def kernel(**inputs):
    in_maps = prep_inputs(**inputs)
    nc = build_program()
    res = run_bass_kernel_spmd(nc, in_maps, core_ids=list(range(N_CORES)))
    inv_ph = np.argsort(full_perm(D))
    out = np.empty((B, S, D), np.float32)
    for c in range(N_CORES):
        b, half = c // 2, c % 2
        out[b, half * TOKQ:(half + 1) * TOKQ, :] = res.results[c]["outT"][inv_ph].T
    return out


# version 06 (final)
# speedup vs baseline: 1.4523x; 1.0022x over previous
import contextlib
import numpy as np
import ml_dtypes
import concourse.bass as bass
import concourse.mybir as mybir
from concourse.bass_utils import run_bass_kernel_spmd

F32 = mybir.dt.float32
BF16 = mybir.dt.bfloat16
U32 = mybir.dt.uint32
I32 = mybir.dt.int32
U8 = mybir.dt.uint8
AF = mybir.ActivationFunctionType
ALU = mybir.AluOpType
BF = ml_dtypes.bfloat16

D = 4096
S = 2048
B = 4
H = 32
HD = 128
INTER = 11008
NGU = 2 * INTER
KB_GU = D // 128
KB_DN = INTER // 128
MC_GU = NGU // 512
MC_DN = D // 512
TOKQ = 1024
CH = 512
N_CORES = 8
EPS = 1e-5
SCALE = 1.0 / float(np.sqrt(np.float32(HD)))

DMA = 16
SZ = {F32: 4, U32: 4, I32: 4, BF16: 2, U8: 1}


def perm128():
    p = np.arange(128)
    return 8 * (p % 16) + p // 16


def full_perm(n):
    return (np.arange(0, n, 128)[:, None] + perm128()[None, :]).ravel()


def rope_perm():
    base = np.concatenate([np.arange(0, HD, 2), np.arange(1, HD, 2)])
    return (np.arange(0, D, HD)[:, None] + base[None, :]).ravel()


class NS:
    pass


def build_program(phases=("D", "P", "A", "W", "F"), debug=False):
    nc = bass.Bass("TRN2", target_bir_lowering=False)
    ctx = contextlib.ExitStack()

    def inp(name, shape, dtype):
        return nc.dram_tensor(name, list(shape), dtype, kind="ExternalInput").ap()

    t = NS()
    t.xT = inp("xT", [D, S], F32)
    t.x_own = inp("x_own", [D, TOKQ], F32)
    t.wq = inp("wq", [D, D], F32)
    t.wk = inp("wk", [D, D], F32)
    t.wv = inp("wv", [D, D], F32)
    t.wo = inp("wo", [D, D], F32)
    t.qw_gu = inp("qw_gu", [D // 8, NGU], U32)
    t.qz_gu = inp("qz_gu", [KB_GU, NGU // 8], U32)
    t.sc_gu = inp("sc_gu", [KB_GU, NGU], F32)
    t.qw_dn = inp("qw_dn", [INTER // 8, D], U32)
    t.qz_dn = inp("qz_dn", [KB_DN, D // 8], U32)
    t.sc_dn = inp("sc_dn", [KB_DN, D], F32)
    t.gi_bc_gu = inp("gi_bc_gu", [KB_GU, D], I32)
    t.gi_col_gu = inp("gi_col_gu", [128, KB_GU], I32)
    t.gi_bc_dn = inp("gi_bc_dn", [KB_DN, INTER], I32)
    t.gi_col_dn = inp("gi_col_dn", [128, KB_DN], I32)
    t.cos_k = inp("cos_k", [64, S], F32)
    t.sin_k = inp("sin_k", [64, S], F32)
    t.cos_q = inp("cos_q", [64, TOKQ], F32)
    t.sin_q = inp("sin_q", [64, TOKQ], F32)
    t.mask = inp("mask", [128, 16 * TOKQ], U8)
    t.ones_col = inp("ones_col", [128, 1], BF16)
    t.iota_row = inp("iota_row", [128, KB_DN], F32)
    t.iota_col = inp("iota_col", [KB_DN, 1], F32)
    t.mask_col = inp("mask_col", [128, 1], I32)
    t.pscale_col = inp("pscale_col", [128, 1], F32)

    t.outT = nc.dram_tensor("outT", [D, TOKQ], F32, kind="ExternalOutput").ap()

    dbg = "ExternalOutput" if debug else "Internal"
    t.wgu_c = nc.dram_tensor("wgu_c", [MC_GU, KB_GU, 128, 512], BF16, kind=dbg).ap()
    t.wdn_c = nc.dram_tensor("wdn_c", [MC_DN, KB_DN, 128, 512], BF16, kind=dbg).ap()
    t.negbg_gu = nc.dram_tensor("negbg_gu", [KB_GU, NGU], BF16, kind=dbg).ap()
    t.negbg_dn = nc.dram_tensor("negbg_dn", [KB_DN, D], BF16, kind=dbg).ap()
    if debug:
        t.gbf_dump = nc.dram_tensor("gbf_dump", [KB_DN, INTER], BF16, kind="ExternalOutput").ap()
        t.indT_dump = nc.dram_tensor("indT_dump", [KB_DN, 512], BF16, kind="ExternalOutput").ap()
        t.indgu_dump = nc.dram_tensor("indgu_dump", [128, KB_GU * 32], BF16, kind="ExternalOutput").ap()
        t.qt_dump = nc.dram_tensor("qt_dump", [128, 32 * 1024], BF16, kind="ExternalOutput").ap()
        t.attn_dump = nc.dram_tensor("attn_dump", [128, 32 * 1024], BF16, kind="ExternalOutput").ap()
    t.kT_sp = nc.dram_tensor("kT_sp", [D, S], BF16, kind=dbg).ap()
    t.v_sp = nc.dram_tensor("v_sp", [H, 16, 128, 128], BF16, kind=dbg).ap()
    t.xmid = nc.dram_tensor("xmid", [D, TOKQ], F32, kind="Internal").ap()
    t.t2_sp = nc.dram_tensor("t2_sp", [D, CH], F32, kind="Internal").ap()
    t.rows_sp = nc.dram_tensor("rows_sp", [16, TOKQ], F32, kind="Internal").ap()

    SBUF_BASE = 18432

    def sbuf(name, off, shape, dtype):
        nbytes = int(np.prod(shape[1:])) * SZ[dtype]
        assert SBUF_BASE + off + nbytes <= 229376, f"{name}: {off}+{nbytes} over budget"
        return nc.alloc_sbuf_tensor_at(name, list(shape), dtype, offset=SBUF_BASE + off)

    s = NS()
    s.ones = sbuf("s_ones", 0, [128, 1], BF16)
    s.iota_row = sbuf("s_iota_row", 64, [128, KB_DN], F32)
    s.iota_col = sbuf("s_iota_col", 448, [KB_DN, 1], F32)
    s.mask = sbuf("s_mask", 512, [128, 1], I32)
    s.pscale = sbuf("s_pscale", 3648, [128, 1], F32)
    s.gcolf_gu = sbuf("s_gcolf_gu", 576, [128, KB_GU], F32)
    s.gcolf_dn = sbuf("s_gcolf_dn", 704, [128, KB_DN], F32)
    s.gcol_gu_i = sbuf("s_gcol_gu_i", 1088, [128, KB_GU], I32)
    s.gcol_dn_i = sbuf("s_gcol_dn_i", 1216, [128, KB_DN], I32)
    s.ind_gu = sbuf("s_ind_gu", 1600, [128, KB_GU * 32], BF16)
    s.rows = sbuf("s_rows", 3712, [128, 1536], F32)
    CONST2 = 9856
    STAGE = 34432
    R2 = 55936
    R3 = 121472
    TOP = 188032
    LIM = 208 * 1024

    sems = NS()
    for n in ["d_ld", "d_l0", "d_pr", "d_ve", "d_mm", "d_st", "d_fr", "d_ix",
              "d_q0", "d_q1", "d_w0", "d_w1", "d_w2", "d_w3", "d_sc",
              "p_x0", "p_x1", "p_w0", "p_w1", "p_w2", "p_cs", "p_v0", "p_v1", "p_k0", "p_k1", "p_ch", "d_ch",
              "a_k0", "a_k1", "a_r0", "a_r1", "a_b0", "a_b1", "w_s0", "w_s1",
              "p_ld", "p_wl", "p_cv", "p_mm", "p_ve", "p_st", "p_fr", "p_rp",
              "a_ld", "a_mm", "a_ex", "a_ve", "a_sc", "a_st", "a_fr", "a_sm",
              "w_ld", "w_wl", "w_mm", "w_ve", "w_sc", "w_st", "w_rw",
              "f_ld", "f_wl", "f_mm", "f_ve", "f_sc", "f_st", "f_hs", "f_x", "f_s2",
              "f_q0", "f_q1", "f_q2", "f_n0", "f_n1", "f_m0", "f_m1", "f_t0", "f_t1",
              "f_r0", "f_r1", "f_o0", "f_o1", "f_ab",
              "bar"]:
        setattr(sems, n, ctx.enter_context(nc.semaphore(n)))

    pq = [ctx.enter_context(nc.psum_tensor(f"pq{i}", [128, 1024], F32)) for i in range(4)]

    d = NS()
    d.scf = sbuf("d_scf", STAGE, [KB_DN, 2048], F32)
    d.qzb = sbuf("d_qzb", STAGE + 8192, [KB_DN, 256], U32)
    d.zq = sbuf("d_zq", STAGE + 9216, [KB_DN, 2048], U32)
    d.nbg = sbuf("d_nbg", STAGE + 17408, [KB_DN, 2048], BF16)
    d.qrep = [sbuf(f"d_qrep{i}", R2 + i * 8192, [128, 2048], U32) for i in range(2)]
    d.scbf_gu = sbuf("d_scbf_gu", R2 + 16384, [KB_GU, NGU], BF16)
    d.scbf_dn = sbuf("d_scbf_dn", R3, [KB_DN, D], BF16)
    d.indT = sbuf("d_indT", R3 + 8192, [KB_DN, 128 * 4], BF16)
    d.nib = [sbuf(f"d_nib{i}", R3 + 9280 + i * 2048, [128, 512], U32) for i in range(2)]
    d.wt = [sbuf(f"d_wt{i}", R3 + 13376 + i * 1024, [128, 512], BF16) for i in range(4)]
    d.nsc = sbuf("d_nsc", R3 + 29760, [KB_DN, 2048], F32)
    d.nibf = [sbuf(f"d_nibf{i}", R3 + 17472 + i * 1024, [128, 512], BF16) for i in range(2)]
    d.scb = [sbuf(f"d_scb{i}", R3 + 19520 + i * 1024, [128, 512], BF16) for i in range(2)]
    d.gist = [sbuf(f"d_gist{i}", R3 + 21568 + i * 8192, [KB_DN, 2048], I32) for i in range(2)]
    d.gbf_gu = sbuf("d_gbf_gu", R3 + 37952, [KB_GU, D], BF16)
    d.gbf_dn = sbuf("d_gbf_dn", R3 + 46144, [KB_DN, INTER], BF16)

    gu_chunks = [(i * 2048, 2048) for i in range(10)] + [(20480, 1536)]
    dn_chunks = [(0, 2048), (2048, 2048)]
    prep = [("gu", c0, w) for c0, w in gu_chunks] + [("dn", c0, w) for c0, w in dn_chunks]
    n_prep = len(prep)

    d_main = []
    gix = 0
    for G in range(KB_GU):
        for c0, w in gu_chunks:
            d_main.append(("gu", G, c0, w, gix))
        gix += 1
    for G in range(KB_DN):
        for c0, w in dn_chunks:
            d_main.append(("dn", G, c0, w, gix))
        gix += 1
    n_gix = gix
    ts_cum = []
    c = 0
    for (_, _, _, w, _) in d_main:
        c += w // 512
        ts_cum.append(c)
    d_tot_tt = c
    d_tot_ld = 2 * n_prep
    d_tot_st = n_prep
    g_last_tt = {}
    g_first_it = {}
    for i2, (_, _, _, w, g) in enumerate(d_main):
        g_last_tt[g] = ts_cum[i2]
        g_first_it.setdefault(g, i2)

    def d_sync(sync):
        sync.dma_start(out=s.ones[:], in_=t.ones_col[:]).then_inc(sems.d_l0, DMA)
        sync.dma_start(out=s.iota_row[:], in_=t.iota_row[:]).then_inc(sems.d_l0, DMA)
        sync.dma_start(out=s.iota_col[:], in_=t.iota_col[:]).then_inc(sems.d_l0, DMA)
        sync.dma_start(out=s.mask[:], in_=t.mask_col[:]).then_inc(sems.d_l0, DMA)
        sync.dma_start(out=s.pscale[:], in_=t.pscale_col[:]).then_inc(sems.d_l0, DMA)
        sync.dma_start(out=s.gcol_gu_i[:], in_=t.gi_col_gu[:]).then_inc(sems.d_l0, DMA)
        sync.dma_start(out=s.gcol_dn_i[:], in_=t.gi_col_dn[:]).then_inc(sems.d_l0, DMA)
        gchunks = [("gu", c) for c in range(0, D, 2048)] + [("dn", c) for c in range(0, INTER, 2048)]
        d_sync.gchunks = gchunks
        for gi, (kind, c0) in enumerate(gchunks):
            w2 = min(2048, (D if kind == "gu" else INTER) - c0)
            nb2 = KB_GU if kind == "gu" else KB_DN
            gsrc = t.gi_bc_gu if kind == "gu" else t.gi_bc_dn
            if gi >= 2:
                sync.wait_ge(sems.d_ix, gi)
            sync.dma_start(out=d.gist[gi % 2][0:nb2, 0:w2],
                           in_=gsrc[:, c0:c0 + w2]).then_inc([sems.d_q0, sems.d_q1][gi % 2], DMA)
        for i, (kind, c0, w) in enumerate(prep):
            nb = KB_GU if kind == "gu" else KB_DN
            scsrc = t.sc_gu if kind == "gu" else t.sc_dn
            qzsrc = t.qz_gu if kind == "gu" else t.qz_dn
            sync.wait_ge(sems.d_pr, i)
            sync.dma_start(out=d.scf[0:nb, 0:w], in_=scsrc[:, c0:c0 + w]).then_inc(sems.d_ld, DMA)
            sync.dma_start(out=d.qzb[0:nb, 0:w // 8], in_=qzsrc[:, c0 // 8:(c0 + w) // 8]).then_inc(sems.d_ld, DMA)
            sync.wait_ge(sems.d_pr, i + 1)
            nbgdst = t.negbg_gu if kind == "gu" else t.negbg_dn
            sync.dma_start(out=nbgdst[:, c0:c0 + w], in_=d.nbg[0:nb, 0:w]).then_inc(sems.d_st, DMA)
        for it, (kind, G, c0, w, g) in enumerate(d_main):
            qrep = d.qrep[it % 2]
            dq = [sems.d_q0, sems.d_q1][it % 2]
            qw = t.qw_gu if kind == "gu" else t.qw_dn
            if it >= 2:
                sync.wait_ge(sems.d_ve, ts_cum[it - 2])
            for j in range(8):
                sync.dma_start(out=qrep[16 * j:16 * j + 16, 0:w],
                               in_=qw[16 * G:16 * G + 16, c0:c0 + w]).then_inc(dq, DMA)

    def d_vector(vector):
        vector.wait_ge(sems.d_l0, 7 * DMA)
        vector.tensor_copy(out=s.gcolf_gu[:], in_=s.gcol_gu_i[:])
        vector.tensor_copy(out=s.gcolf_dn[:], in_=s.gcol_dn_i[:]).then_inc(sems.d_ix, 1)
        for gi, (kind, c0) in enumerate(d_sync.gchunks):
            w2 = min(2048, (D if kind == "gu" else INTER) - c0)
            nb2 = KB_GU if kind == "gu" else KB_DN
            gdst = d.gbf_gu if kind == "gu" else d.gbf_dn
            vector.wait_ge([sems.d_q0, sems.d_q1][gi % 2], (gi // 2 + 1) * DMA)
            vector.tensor_copy(out=gdst[0:nb2, c0:c0 + w2],
                               in_=d.gist[gi % 2][0:nb2, 0:w2]).then_inc(sems.d_ix, 1)
        for i, (kind, c0, w) in enumerate(prep):
            nb = KB_GU if kind == "gu" else KB_DN
            scbf = d.scbf_gu if kind == "gu" else d.scbf_dn
            if i == 0:
                vector.wait_ge(sems.d_ix, 9)
            vector.wait_ge(sems.d_ld, (2 * (i + 1)) * DMA)
            if i >= 1:
                vector.wait_ge(sems.d_st, i * DMA)
                vector.wait_ge(sems.d_pr, i)
            vector.tensor_copy(out=scbf[0:nb, c0:c0 + w], in_=d.scf[0:nb, 0:w])
            vector.tensor_scalar(out=d.nsc[0:nb, 0:w], in0=d.scf[0:nb, 0:w],
                                 scalar1=-1.0, scalar2=None, op0=ALU.mult).then_inc(sems.d_ch, 1)
            nblk = w // 128
            for j in range(8):
                src = d.qzb[0:nb, 0:w // 8].rearrange("p (b c) -> p b c", c=16)
                dst = d.zq[0:nb, 0:w].rearrange("p (b c) -> p b c", c=128)[:, :, 16 * j:16 * j + 16]
                ins = vector.tensor_scalar(out=dst, in0=src, scalar1=4 * j, scalar2=15,
                                           op0=ALU.logical_shift_right, op1=ALU.bitwise_and)
                if j == 7:
                    ins.then_inc(sems.d_ch, 1)
            vector.wait_ge(sems.d_ch, 2 * (i + 1))
            vector.scalar_tensor_tensor(out=d.nbg[0:nb, 0:w], in0=d.zq[0:nb, 0:w],
                                        scalar=1.0, in1=d.nsc[0:nb, 0:w],
                                        op0=ALU.add, op1=ALU.mult).then_inc(sems.d_pr, 1)
        vector.wait_ge(sems.d_ix, 10)
        tsc = 0
        for it, (kind, G, c0, w, g) in enumerate(d_main):
            qrep = d.qrep[it % 2]
            nb = KB_GU if kind == "gu" else KB_DN
            if c0 == 0:
                if g >= 4:
                    vector.wait_ge(sems.d_mm, g_last_tt[g - 4])
                gbf = d.gbf_gu if kind == "gu" else d.gbf_dn
                vector.tensor_scalar(out=d.indT[0:nb, 128 * (g % 4):128 * (g % 4) + 128],
                                     in0=gbf[0:nb, 128 * G:128 * G + 128],
                                     scalar1=s.iota_col[0:nb, :], scalar2=None,
                                     op0=ALU.is_equal).then_inc(sems.d_ix, 1)
                if kind == "gu":
                    vector.tensor_scalar(out=s.ind_gu[:, 32 * G:32 * G + 32],
                                         in0=s.iota_row[:, 0:KB_GU],
                                         scalar1=s.gcolf_gu[:, G:G + 1], scalar2=None,
                                         op0=ALU.is_equal)
            vector.wait_ge([sems.d_q0, sems.d_q1][it % 2], (4 + 8 * (it // 2 + 1)) * DMA)
            d_w = [sems.d_w0, sems.d_w1, sems.d_w2, sems.d_w3]
            for u in range(w // 512):
                tsc += 1
                nib = d.nib[tsc % 2]
                nibf = d.nibf[tsc % 2]
                if tsc > 2:
                    vector.wait_ge(sems.d_ch, 2 * n_prep + tsc - 2)
                vector.tensor_scalar(out=nib[:], in0=qrep[:, 512 * u:512 * u + 512],
                                     scalar1=s.mask[:], scalar2=None,
                                     op0=ALU.bitwise_and).then_inc(sems.d_ve, 1)
                vector.wait_ge(sems.d_ve, tsc)
                if tsc > 2:
                    vector.wait_ge(sems.d_fr, tsc - 2)
                vector.tensor_scalar(out=nibf[:], in0=nib[:],
                                     scalar1=s.pscale[:], scalar2=None,
                                     op0=ALU.mult).then_inc(sems.d_ch, 1)
                vector.wait_ge(sems.d_mm, tsc)
                vector.wait_ge(sems.d_ch, 2 * n_prep + tsc)
                if tsc > 4:
                    vector.wait_ge(d_w[(tsc - 1) % 4], ((tsc - 1) // 4) * DMA)
                vector.tensor_tensor(out=d.wt[(tsc - 1) % 4][:], in0=nibf[:],
                                     in1=pq[tsc % 2][:, 0:512], op=ALU.mult).then_inc(sems.d_fr, 1)

    def d_tensor(tensor):
        tensor.wait_ge(sems.d_pr, n_prep)
        ttc = 0
        ixc = 0
        for it, (kind, G, c0, w, g) in enumerate(d_main):
            nb = KB_GU if kind == "gu" else KB_DN
            scbf = d.scbf_gu if kind == "gu" else d.scbf_dn
            if c0 == 0:
                tensor.wait_ge(sems.d_ix, 10 + (g + 1))
            for u in range(w // 512):
                ttc += 1
                if ttc > 2:
                    tensor.wait_ge(sems.d_fr, ttc - 2)
                tensor.matmul(out=pq[ttc % 2][:, 0:512],
                              lhsT=d.indT[0:nb, 128 * (g % 4):128 * (g % 4) + 128],
                              rhs=scbf[0:nb, c0 + 512 * u:c0 + 512 * (u + 1)],
                              start=True, stop=True).then_inc(sems.d_mm, 1)

    def d_scalar(scalar):
        scalar.wait_ge(sems.d_ix, 9)
        scalar.nop()
        scalar.sem_inc(sems.d_ix, 1)
        d_w = [sems.d_w0, sems.d_w1, sems.d_w2, sems.d_w3]
        ttc = 0
        for it, (kind, G, c0, w, g) in enumerate(d_main):
            cache = t.wgu_c if kind == "gu" else t.wdn_c
            for u in range(w // 512):
                ttc += 1
                scalar.wait_ge(sems.d_fr, ttc)
                mc = (c0 + 512 * u) // 512
                scalar.dma_start(out=cache[mc, G], in_=d.wt[(ttc - 1) % 4][:]).then_inc(d_w[(ttc - 1) % 4], DMA)

    p = NS()
    p.cosk = sbuf("p_cosk", CONST2, [64, 1024], F32)
    p.sink = sbuf("p_sink", CONST2 + 4096, [64, 1024], F32)
    p.cosq = sbuf("p_cosq", CONST2 + 8192, [64, 1024], F32)
    p.sinq = sbuf("p_sinq", CONST2 + 12288, [64, 1024], F32)
    p.rt0 = sbuf("p_rt0", CONST2 + 16384, [64, 1024], F32)
    p.rt1 = sbuf("p_rt1", CONST2 + 20480, [64, 1024], F32)
    p.rt2 = sbuf("p_rt2", R3 + 70656, [64, 1024], F32)
    p.rt3 = sbuf("p_rt3", R3 + 66560, [64, 1024], F32)
    p.xstage = [sbuf(f"p_xst{i}", STAGE + i * 4096, [128, 1024], F32) for i in range(2)]
    p.wstage = [sbuf(f"p_wst{i}", STAGE + 8192 + i * 2048, [128, 512], F32) for i in range(3)]
    p.wbf = [sbuf(f"p_wbf{i}", STAGE + 14336 + i * 1024, [128, 512], BF16) for i in range(3)]
    p.kbf = [sbuf(f"p_kbf{i}", STAGE + 17408 + i * 2048, [128, 1024], BF16) for i in range(2)]
    p.vbf = [sbuf(f"p_vbf{i}", STAGE + 17408 + i * 2048, [128, 512], BF16) for i in range(2)]
    p.xh = sbuf("p_xh", R3, [128, 32 * 1024], BF16)
    p.wvbf = [sbuf(f"p_wvbf{i}", R2 + 32768 * i, [128, 32 * 512], BF16) for i in range(2)]
    p.qt = sbuf("p_qt", R2, [128, 32 * 1024], BF16)

    p_x = [sems.p_x0, sems.p_x1]
    p_w = [sems.p_w0, sems.p_w1, sems.p_w2]
    p_v = [sems.p_v0, sems.p_v1]
    p_k = [sems.p_k0, sems.p_k1]
    PLD_H = 802
    PCV_H = 32
    PWL_H = 768
    PMM_H = 4096
    PVE_H = 96
    PST_H = 288
    P_TOT_LD = 2 * PLD_H + 546
    P_TOT_CV = 96
    P_TOT_WL = 2 * PWL_H + 512
    P_TOT_MM = 2 * PMM_H + 2048
    P_TOT_VE = 2 * PVE_H + 32
    P_TOT_ST = 2 * PST_H

    def p_xl_sync(sync, c, xsrc, cos_src, sin_src, cdst, sdst, cv_base):
        sync.dma_start(out=cdst[:], in_=cos_src).then_inc(sems.p_cs, DMA)
        sync.dma_start(out=sdst[:], in_=sin_src).then_inc(sems.p_cs, DMA)
        for k in range(32):
            if k >= 2:
                sync.wait_ge(sems.p_cv, cv_base + k - 1)
            sync.dma_start(out=p.xstage[k % 2][:],
                           in_=xsrc[128 * k:128 * k + 128, :]).then_inc(p_x[k % 2], DMA)

    def p_xl_vector(vector, xfill_base, cv_check=None):
        for k in range(32):
            vector.wait_ge(p_x[k % 2], (xfill_base + k // 2 + 1) * DMA)
            vector.tensor_copy(out=p.xh[:, 1024 * k:1024 * (k + 1)],
                               in_=p.xstage[k % 2][:]).then_inc(sems.p_cv, 1)

    IW_K0, IW_K1, IW_Q, IW_W = 0, 256, 512, 768
    IW_TOT = 1280

    def p_wload_queue(eng, c, w_src, iw_base, wid, par, ve_base, spill, sk_base):
        for mc4 in range(8):
            for k in range(32):
                if spill is not None and mc4 >= 1 and k in (0, 2, 4, 6):
                    j = 4 * (mc4 - 1) + k // 2
                    eng.wait_ge(sems.p_ve, ve_base + j + 1)
                    eng.dma_start(out=spill(j), in_=p.kbf[j % 2][:]).then_inc(p_k[j % 2], DMA)
                iw = iw_base + mc4 * 32 + k
                if iw >= 3:
                    eng.wait_ge(sems.p_wl, iw - 2)
                eng.dma_start(out=p.wstage[iw % 3][:, 0:wid],
                              in_=w_src[128 * k:128 * k + 128,
                                        wid * mc4:wid * (mc4 + 1)]).then_inc(p_w[iw % 3], DMA)
        if spill is not None:
            for m4 in range(4):
                j = 28 + m4
                eng.wait_ge(sems.p_ve, ve_base + j + 1)
                eng.dma_start(out=spill(j), in_=p.kbf[j % 2][:]).then_inc(p_k[j % 2], DMA)

    def p_proj_scalar(scalar, iw_base, wid, mm_base, mm_per):
        for i in range(256):
            iw = iw_base + i
            scalar.wait_ge(p_w[iw % 3], (iw // 3 + 1) * DMA)
            if i >= 3:
                scalar.wait_ge(sems.p_mm, mm_base + (i - 2) * mm_per)
            elif iw_base > 0:
                scalar.wait_ge(sems.p_mm, mm_base)
            scalar.copy(out=p.wbf[iw % 3][:, 0:wid],
                        in_=p.wstage[iw % 3][:, 0:wid]).then_inc(sems.p_wl, 1)

    def p_proj_tensor(tensor, iw_base, mm_base, cv_base, ve_base):
        for mc4 in range(8):
            for k in range(32):
                iw = iw_base + mc4 * 32 + k
                tensor.wait_ge(sems.p_wl, iw + 1)
                if k == 0:
                    tensor.wait_ge(sems.p_cv, cv_base + 32)
                    tensor.wait_ge(sems.p_ve, ve_base + 4 * mc4)
                for m4 in range(4):
                    for c5 in range(2):
                        tensor.matmul(out=pq[m4][:, 512 * c5:512 * (c5 + 1)],
                                      lhsT=p.wbf[iw % 3][:, 128 * m4:128 * (m4 + 1)],
                                      rhs=p.xh[:, 1024 * k + 512 * c5:1024 * k + 512 * (c5 + 1)],
                                      start=(k == 0), stop=(k == 31)).then_inc(sems.p_mm, 1)

    def p_proj_vector(vector, c, mm_base, sk_base, ve_base, cosg, sing, out_of_j, cs_val):
        for mc4 in range(8):
            for m4 in range(4):
                j = 4 * mc4 + m4
                if j == 0:
                    vector.wait_ge(sems.p_cs, cs_val * DMA)
                vector.wait_ge(sems.p_mm, mm_base + 256 * (mc4 + 1))
                if j > 0:
                    vector.wait_ge(sems.p_ve, ve_base + j)
                dst = out_of_j(j)
                if dst is None:
                    dstap = p.kbf[j % 2]
                    if j >= 2 or sk_base > 0:
                        vector.wait_ge(p_k[j % 2], (sk_base + j // 2) * DMA)
                    if j < 2 and sk_base > 0:
                        vector.wait_ge(p_v[j % 2], 4 * 32 * DMA)
                else:
                    dstap = dst
                pp = pq[m4]
                ch = c["ch"]
                vector.tensor_mul(out=p.rt0[:], in0=pp[0:64, :], in1=cosg[:]).then_inc(sems.p_ch, 1)
                vector.tensor_mul(out=p.rt1[:], in0=pp[64:128, :], in1=sing[:]).then_inc(sems.p_ch, 1)
                vector.wait_ge(sems.p_ch, ch + 2)
                vector.tensor_sub(out=dstap[0:64, :], in0=p.rt0[:], in1=p.rt1[:])
                vector.tensor_mul(out=p.rt2[:], in0=pp[0:64, :], in1=sing[:]).then_inc(sems.p_ch, 1)
                vector.tensor_mul(out=p.rt3[:], in0=pp[64:128, :], in1=cosg[:]).then_inc(sems.p_ch, 1)
                vector.wait_ge(sems.p_ch, ch + 4)
                c["ch"] += 4
                vector.tensor_add(out=dstap[64:128, :], in0=p.rt2[:],
                                  in1=p.rt3[:]).then_inc(sems.p_ve, 1)
                c["ve"] += 1

    def p_v_sync(sync, c, th, ve_base, st_base):
        XB = 288 * th
        for nc8 in range(8):
            if nc8 >= 1:
                for tokm in range(8):
                    e = (nc8 - 1) * 8 + tokm
                    sync.wait_ge(sems.p_ve, ve_base + e + 1)
                    for h4 in range(4):
                        sync.dma_start(out=t.v_sp[4 * (nc8 - 1) + h4, 8 * th + tokm],
                                       in_=p.vbf[e % 2][:, 128 * h4:128 * (h4 + 1)]).then_inc(p_v[e % 2], DMA)
            for k in range(32):
                i = nc8 * 32 + k
                if i >= 2:
                    sync.wait_ge(sems.p_cv, XB + 32 + i - 1)
                sync.dma_start(out=p.xstage[i % 2][:, 0:512],
                               in_=t.wv[128 * k:128 * k + 128, 512 * nc8:512 * nc8 + 512]).then_inc(p_x[i % 2], DMA)
        for tokm in range(8):
            e = 7 * 8 + tokm
            sync.wait_ge(sems.p_ve, ve_base + e + 1)
            for h4 in range(4):
                sync.dma_start(out=t.v_sp[4 * 7 + h4, 8 * th + tokm],
                               in_=p.vbf[e % 2][:, 128 * h4:128 * (h4 + 1)]).then_inc(p_v[e % 2], DMA)

    def p_v_scalar(scalar, c, th, xfill_base, mm_base):
        XB = 288 * th
        for nc8 in range(8):
            for k in range(32):
                i = nc8 * 32 + k
                scalar.wait_ge(p_x[i % 2], (xfill_base + i // 2 + 1) * DMA)
                if nc8 >= 2:
                    scalar.wait_ge(sems.p_mm, mm_base + 256 * (nc8 - 1))
                scalar.copy(out=p.wvbf[nc8 % 2][:, 512 * k:512 * (k + 1)],
                            in_=p.xstage[i % 2][:, 0:512]).then_inc(sems.p_cv, 1)

    def p_v_tensor(tensor, c, th, ve_base):
        XB = 288 * th
        for nc8 in range(8):
            for tokm in range(8):
                g = nc8 * 8 + tokm
                if g >= 2:
                    tensor.wait_ge(sems.p_ve, ve_base + g - 1)
                else:
                    tensor.wait_ge(sems.p_ve, ve_base)
                for k in range(32):
                    tensor.wait_ge(sems.p_cv, XB + 32 + nc8 * 32 + k + 1)
                    tensor.matmul(out=pq[0][:, 512 * (g % 2):512 * (g % 2) + 512],
                                  lhsT=p.xh[:, 1024 * k + 128 * tokm:1024 * k + 128 * (tokm + 1)],
                                  rhs=p.wvbf[nc8 % 2][:, 512 * k:512 * (k + 1)],
                                  start=(k == 0), stop=(k == 31)).then_inc(sems.p_mm, 1)

    def p_v_vector(vector, c, th, mm_base):
        for nc8 in range(8):
            for tokm in range(8):
                e = nc8 * 8 + tokm
                vector.wait_ge(sems.p_mm, mm_base + (e + 1) * 32)
                if e >= 2:
                    vector.wait_ge(p_v[e % 2], 4 * (32 * th + e // 2) * DMA)
                elif th == 1:
                    vector.wait_ge(p_v[e % 2], 4 * 32 * DMA)
                if e < 2:
                    vector.wait_ge(p_k[e % 2], (16 * (th + 1)) * DMA)
                vector.tensor_copy(out=p.vbf[e % 2][:],
                                   in_=pq[0][:, 512 * (e % 2):512 * (e % 2) + 512]).then_inc(sems.p_ve, 1)
                c["ve"] += 1

    PMM_H = 4096
    PVE_H = 96
    P_TOT_MM = 2 * PMM_H + 2048
    P_TOT_VE = 2 * PVE_H + 32
    P_TOT_CV = 608

    def p_sync(sync):
        c = {}
        for th in range(2):
            sl = slice(1024 * th, 1024 * (th + 1))
            p_xl_sync(sync, c, t.xT[:, sl], t.cos_k[:, sl], t.sin_k[:, sl],
                      p.cosk, p.sink, 288 * th)
            def kspill(j, th=th):
                return t.kT_sp.rearrange("(a b) c -> a b c", b=128)[j, :, sl]
            p_wload_queue(sync, c, t.wk, [IW_K0, IW_K1][th], 512, 0,
                          PVE_H * th, kspill, 16 * th)
            p_v_sync(sync, c, th, PVE_H * th + 32, 0)
            if th == 0:
                sync.wait_ge(sems.p_mm, PMM_H)
                sync.wait_ge(sems.p_ve, PVE_H)
                sync.wait_ge(sems.p_cv, 288)
        sync.wait_ge(sems.p_mm, 2 * PMM_H)
        sync.wait_ge(sems.p_cv, 576)
        sync.dma_start(out=p.cosq[:], in_=t.cos_q[:]).then_inc(sems.p_cs, DMA)
        sync.dma_start(out=p.sinq[:], in_=t.sin_q[:]).then_inc(sems.p_cs, DMA)
        for k in range(32):
            if k >= 2:
                sync.wait_ge(sems.p_cv, 576 + k - 1)
            sync.dma_start(out=p.xstage[k % 2][:],
                           in_=t.x_own[128 * k:128 * k + 128, :]).then_inc(p_x[k % 2], DMA)
        p_wload_queue(sync, c, t.wq, IW_Q, 512, 0, 2 * PVE_H, None, 0)
        sync.wait_ge(p_k[0], 32 * DMA)
        sync.wait_ge(p_k[1], 32 * DMA)
        sync.wait_ge(sems.p_ve, P_TOT_VE)
        sync.wait_ge(p_x[0], 304 * DMA)
        sync.wait_ge(p_x[1], 304 * DMA)
        sync.wait_ge(p_v[0], 2 * 32 * 4 * DMA)
        sync.wait_ge(p_v[1], 2 * 32 * 4 * DMA)
        sync.wait_ge(sems.p_cs, 6 * DMA)

    def p_gpsimd(gpsimd):
        pass

    def p_vector(vector):
        c = {"cv": 0, "ve": 0, "ch": 0}
        for th in range(2):
            p_xl_vector(vector, 144 * th)
            p_proj_vector(vector, c, PMM_H * th, 16 * th, PVE_H * th,
                          p.cosk, p.sink, lambda j: None, 2 * (th + 1))
            p_v_vector(vector, c, th, PMM_H * th + 2048)
        p_xl_vector(vector, 288)
        p_proj_vector(vector, c, 2 * PMM_H, 0, 2 * PVE_H,
                      p.cosq, p.sinq,
                      lambda j: p.qt[:, 1024 * j:1024 * (j + 1)], 6)

    def p_scalar(scalar):
        c = {}
        for th in range(2):
            p_proj_scalar(scalar, [IW_K0, IW_K1][th], 512, PMM_H * th, 8)
            p_v_scalar(scalar, c, th, 144 * th + 16, PMM_H * th + 2048)
        p_proj_scalar(scalar, IW_Q, 512, 2 * PMM_H, 8)

    def p_tensor(tensor):
        for th in range(2):
            p_proj_tensor(tensor, [IW_K0, IW_K1][th], PMM_H * th, 288 * th, PVE_H * th)
            p_v_tensor(tensor, None, th, PVE_H * th + 32)
        p_proj_tensor(tensor, IW_Q, 2 * PMM_H, 576, 2 * PVE_H)

    a = NS()
    a.kh = [sbuf(f"a_kh{i}", R3 + i * 4096, [128, 2048], BF16) for i in range(2)]
    a.vh = [sbuf(f"a_vh{i}", R3 + 8192 + i * 4096, [128, 2048], BF16) for i in range(2)]
    a.e = [sbuf(f"a_e{i}", R3 + 16384 + i * 1024, [128, 512], BF16) for i in range(4)]
    a.attn = sbuf("a_attn", R3 + 20480, [128, 32 * 1024], BF16)
    a.mask = sbuf("a_mask", STAGE, [128, 16 * 1024], U8)
    a.recb = [sbuf(f"a_recb{i}", STAGE + 16384 + i * 2048, [128, 512], F32) for i in range(2)]

    A_SEQ = ["S0", "S1", "S2", "S3"]
    for k in range(12):
        A_SEQ += [f"sum{k}", f"AV{k}", f"S{k + 4}"]
    for k in range(12, 16):
        A_SEQ += [f"sum{k}", f"AV{k}"]
    A_POS = {tag: i + 1 for i, tag in enumerate(A_SEQ)}
    A_MM_G = 48
    A_VE_G = 18
    A_EX_G = 16

    a_kv = [sems.a_k0, sems.a_k1]
    a_r = [sems.a_r0, sems.a_r1]
    a_b = [sems.a_b0, sems.a_b1]

    def a_sync(sync):
        sync.dma_start(out=a.mask[:], in_=t.mask[:]).then_inc(sems.a_sm, DMA)
        for h in range(H):
            if h >= 2:
                sync.wait_ge(sems.a_mm, 96 * (h - 1))
            sync.dma_start(out=a.kh[h % 2][:], in_=t.kT_sp[128 * h:128 * h + 128, :]).then_inc(a_kv[h % 2], DMA)
            sync.dma_start(out=a.vh[h % 2][:],
                           in_=t.v_sp[h].rearrange("t p d -> p t d")).then_inc(a_kv[h % 2], DMA)
            for qc in range(2):
                idx = 2 * h + qc
                sync.wait_ge(sems.a_ve, A_VE_G * idx + 17)
                if idx >= 2:
                    sync.wait_ge(sems.a_ve, A_VE_G * (idx - 1))
                sync.dma_start(out=t.rows_sp[idx % 4, 0:512],
                               in_=s.rows[0:1, 512 * (idx % 3):512 * (idx % 3) + 512]).then_inc(a_r[idx % 2], DMA)
                sync.wait_ge(a_r[idx % 2], (idx // 2 + 1) * DMA)
                sync.dma_start(out=a.recb[idx % 2][:],
                               in_=t.rows_sp[idx % 4, 0:512].unsqueeze(0).partition_broadcast(128)).then_inc(a_b[idx % 2], DMA)

    def a_tensor(tensor):
        for h in range(H):
            tensor.wait_ge(a_kv[h % 2], 2 * (h // 2 + 1) * DMA)
            for qc in range(2):
                idx = 2 * h + qc
                base = A_MM_G * idx
                par = idx % 2
                for tag in A_SEQ:
                    kind = "S" if tag[0] == "S" else ("sum" if tag[:3] == "sum" else "AV")
                    k = int(tag[len(kind):])
                    if kind == "S":
                        pS = [pq[0][:, 0:512], pq[0][:, 512:1024],
                              pq[1][:, 0:512], pq[1][:, 512:1024]][k % 4]
                        if k >= 4:
                            tensor.wait_ge(sems.a_ex, A_EX_G * idx + k - 3)
                        elif idx >= 1:
                            tensor.wait_ge(sems.a_ex, A_EX_G * idx)
                        tensor.matmul(out=pS, lhsT=a.kh[h % 2][:, 128 * k:128 * k + 128],
                                      rhs=p.qt[:, 1024 * h + 512 * qc:1024 * h + 512 * (qc + 1)],
                                      start=True, stop=True).then_inc(sems.a_mm, 1)
                    elif kind == "sum":
                        tensor.wait_ge(sems.a_ve, A_VE_G * idx + k + 1)
                        if k == 0 and idx >= 2:
                            tensor.wait_ge(sems.a_ve, A_VE_G * (idx - 1))
                        tensor.matmul(out=pq[3][0:1, 512 * par:512 * par + 512],
                                      lhsT=s.ones[:], rhs=a.e[k % 4][:],
                                      start=(k == 0), stop=(k == 15)).then_inc(sems.a_mm, 1)
                    else:
                        tensor.matmul(out=pq[2][:, 512 * par:512 * par + 512],
                                      lhsT=a.vh[h % 2][:, 128 * k:128 * k + 128],
                                      rhs=a.e[k % 4][:],
                                      start=(k == 0), stop=(k == 15)).then_inc(sems.a_mm, 1)

    def a_scalar(scalar):
        for h in range(H):
            for qc in range(2):
                idx = 2 * h + qc
                for k in range(16):
                    pS = [pq[0][:, 0:512], pq[0][:, 512:1024],
                          pq[1][:, 0:512], pq[1][:, 512:1024]][k % 4]
                    scalar.wait_ge(sems.a_mm, A_MM_G * idx + A_POS[f"S{k}"])
                    if k >= 4:
                        scalar.wait_ge(sems.a_mm, A_MM_G * idx + A_POS[f"AV{k - 4}"])
                    elif idx >= 1:
                        scalar.wait_ge(sems.a_mm, A_MM_G * (idx - 1) + A_POS[f"AV{k + 12}"])
                    scalar.activation(out=a.e[k % 4][:], in_=pS, func=AF.Exp,
                                      scale=SCALE).then_inc(sems.a_ex, 1)

    def a_vector(vector):
        vector.wait_ge(sems.a_sm, DMA)
        for h in range(H):
            for qc in range(2):
                idx = 2 * h + qc
                par = idx % 2
                for k in range(16):
                    vector.wait_ge(sems.a_ex, A_EX_G * idx + k + 1)
                    msl = a.mask[:, 1024 * k + 512 * qc:1024 * k + 512 * (qc + 1)]
                    vector.tensor_tensor(out=a.e[k % 4][:], in0=a.e[k % 4][:],
                                         in1=msl, op=ALU.mult).then_inc(sems.a_ve, 1)
                vector.wait_ge(sems.a_mm, A_MM_G * idx + A_POS["sum15"])
                if idx >= 3:
                    vector.wait_ge(a_r[(idx - 3) % 2], ((idx - 3) // 2 + 1) * DMA)
                vector.reciprocal(out=s.rows[0:1, 512 * (idx % 3):512 * (idx % 3) + 512],
                                  in_=pq[3][0:1, 512 * par:512 * par + 512]).then_inc(sems.a_ve, 1)
                vector.wait_ge(a_b[idx % 2], (idx // 2 + 1) * DMA)
                vector.wait_ge(sems.a_mm, A_MM_G * idx + A_POS["AV15"])
                vector.tensor_tensor(out=a.attn[:, 1024 * h + 512 * qc:1024 * h + 512 * (qc + 1)],
                                     in0=pq[2][:, 512 * par:512 * par + 512],
                                     in1=a.recb[idx % 2][:], op=ALU.mult).then_inc(sems.a_ve, 1)

    A_TOT_VE = A_VE_G * 64
    A_TOT_MM = A_MM_G * 64

    w = NS()
    w.tx = p.qt
    w.ab = sbuf("w_ab", CONST2, [128, 1024], F32)
    w.bb = sbuf("w_bb", CONST2 + 4096, [128, 1024], F32)
    w.u = sbuf("w_u", CONST2 + 8192, [128, 1024], F32)
    w.x2 = [sbuf(f"w_x2{i}", CONST2 + 12288 + i * 4096, [128, 1024], F32) for i in range(2)]
    w.rows = sbuf("w_rows", CONST2 + 20480, [128, 1024], F32)

    def w_sync(sync):
        for mc2 in range(16):
            for m2 in range(2):
                m = 2 * mc2 + m2
                if m >= 2:
                    sync.wait_ge(sems.w_ve, m - 1)
                sync.dma_start(out=p.xstage[m % 2][:],
                               in_=t.x_own[128 * m:128 * m + 128, :]).then_inc(p_x[m % 2], DMA)
            for k in range(32):
                i = mc2 * 32 + k
                iw = 768 + i
                if iw >= 3:
                    sync.wait_ge(sems.p_wl, iw - 2)
                sync.dma_start(out=p.wstage[iw % 3][:, 0:256],
                               in_=t.wo[128 * k:128 * k + 128, 256 * mc2:256 * mc2 + 256]).then_inc(p_w[iw % 3], DMA)
        sync.wait_ge(sems.w_ve, 37)
        sync.dma_start(out=t.rows_sp[4, :], in_=w.u[0:1, :]).then_inc(sems.w_rw, DMA)
        sync.dma_start(out=t.rows_sp[5, :], in_=w.rows[0:1, :]).then_inc(sems.w_rw, DMA)
        sync.wait_ge(sems.w_rw, 2 * DMA)
        sync.dma_start(out=w.ab[:], in_=t.rows_sp[4, :].unsqueeze(0).partition_broadcast(128)).then_inc(sems.w_rw, DMA)
        sync.dma_start(out=w.bb[:], in_=t.rows_sp[5, :].unsqueeze(0).partition_broadcast(128)).then_inc(sems.w_rw, DMA)
        for m in range(32):
            sync.wait_ge(sems.w_ve, 32 + 5 + 3 * m + 2)
            sync.dma_start(out=t.xmid[128 * m:128 * m + 128, :],
                           in_=w.x2[m % 2][:]).then_inc([sems.w_s0, sems.w_s1][m % 2], DMA)

    def w_scalar(scalar):
        for mc2 in range(16):
            for k in range(32):
                i = mc2 * 32 + k
                iw = 768 + i
                scalar.wait_ge(p_w[iw % 3], (iw // 3 + 1) * DMA)
                if i >= 3:
                    scalar.wait_ge(sems.w_mm, (i - 2) * 4)
                scalar.copy(out=p.wbf[iw % 3][:, 0:256],
                            in_=p.wstage[iw % 3][:, 0:256]).then_inc(sems.p_wl, 1)
            for m2 in range(2):
                m = (mc2 - 1) * 2 + m2
                if 0 <= m:
                    scalar.wait_ge(sems.w_ve, m + 1)
                    if m >= 2:
                        scalar.wait_ge(sems.w_st, 4 * (m - 1))
                    scalar.activation(out=p.kbf[m % 2][:], in_=w.tx[:, 1024 * m:1024 * (m + 1)],
                                      func=AF.Square).then_inc(sems.w_sc, 1)
        for m in (30, 31):
            scalar.wait_ge(sems.w_ve, m + 1)
            scalar.wait_ge(sems.w_st, 4 * (m - 1))
            scalar.activation(out=p.kbf[m % 2][:], in_=w.tx[:, 1024 * m:1024 * (m + 1)],
                              func=AF.Square).then_inc(sems.w_sc, 1)
        scalar.wait_ge(sems.w_ve, 34)
        scalar.activation(out=pq[2][0:1, 0:1024], in_=w.rows[0:1, :],
                          func=AF.Square).then_inc(sems.w_sc, 1)
        scalar.wait_ge(sems.w_ve, 36)
        scalar.activation(out=pq[2][0:1, 0:1024], in_=w.u[0:1, :],
                          func=AF.Sqrt).then_inc(sems.w_sc, 1)

    def w_tensor(tensor):
        for mc2 in range(16):
            for k in range(32):
                i = mc2 * 32 + k
                iw = 768 + i
                tensor.wait_ge(sems.p_wl, 768 + i + 1)
                if k == 0:
                    if mc2 >= 1:
                        tensor.wait_ge(sems.w_ve, 2 * mc2)
                for m2 in range(2):
                    for c5 in range(2):
                        tensor.matmul(out=pq[m2][:, 512 * c5:512 * (c5 + 1)],
                                      lhsT=p.wbf[iw % 3][:, 128 * m2:128 * (m2 + 1)],
                                      rhs=a.attn[:, 1024 * k + 512 * c5:1024 * k + 512 * (c5 + 1)],
                                      start=(k == 0), stop=(k == 31)).then_inc(sems.w_mm, 1)
            for m2 in range(2):
                m = (mc2 - 1) * 2 + m2
                if 0 <= m < 32:
                    tensor.wait_ge(sems.w_ve, m + 1)
                    tensor.wait_ge(sems.w_sc, m + 1)
                    for c5 in range(2):
                        tensor.matmul(out=pq[2][0:1, 512 * c5:512 * (c5 + 1)],
                                      lhsT=s.ones[:],
                                      rhs=w.tx[:, 1024 * m + 512 * c5:1024 * m + 512 * (c5 + 1)],
                                      start=(m == 0), stop=(m == 31)).then_inc(sems.w_st, 1)
                        tensor.matmul(out=pq[3][0:1, 512 * c5:512 * (c5 + 1)],
                                      lhsT=s.ones[:],
                                      rhs=p.kbf[m % 2][:, 512 * c5:512 * (c5 + 1)],
                                      start=(m == 0), stop=(m == 31)).then_inc(sems.w_st, 1)
        for m in (30, 31):
            tensor.wait_ge(sems.w_ve, m + 1)
            tensor.wait_ge(sems.w_sc, m + 1)
            for c5 in range(2):
                tensor.matmul(out=pq[2][0:1, 512 * c5:512 * (c5 + 1)],
                              lhsT=s.ones[:],
                              rhs=w.tx[:, 1024 * m + 512 * c5:1024 * m + 512 * (c5 + 1)],
                              start=(m == 0), stop=(m == 31)).then_inc(sems.w_st, 1)
                tensor.matmul(out=pq[3][0:1, 512 * c5:512 * (c5 + 1)],
                              lhsT=s.ones[:],
                              rhs=p.kbf[m % 2][:, 512 * c5:512 * (c5 + 1)],
                              start=(m == 0), stop=(m == 31)).then_inc(sems.w_st, 1)

    W_TOT_MM = 2048

    def w_vector(vector):
        for mc2 in range(16):
            for m2 in range(2):
                m = mc2 * 2 + m2
                vector.wait_ge(sems.w_mm, 128 * (mc2 + 1))
                vector.wait_ge(p_x[m % 2], (304 + m // 2 + 1) * DMA)
                vector.tensor_add(out=w.tx[:, 1024 * m:1024 * (m + 1)],
                                  in0=pq[m2][:, 0:1024],
                                  in1=p.xstage[m % 2][:]).then_inc(sems.w_ve, 1)
        vector.wait_ge(sems.w_mm, W_TOT_MM)
        vector.wait_ge(sems.w_st, 128)
        vector.tensor_scalar(out=w.rows[0:1, :], in0=pq[2][0:1, 0:1024],
                             scalar1=1.0 / D, scalar2=None, op0=ALU.mult).then_inc(sems.w_ve, 1)
        vector.tensor_scalar(out=w.u[0:1, :], in0=pq[3][0:1, 0:1024],
                             scalar1=1.0 / D, scalar2=None, op0=ALU.mult).then_inc(sems.w_ve, 1)
        vector.wait_ge(sems.w_sc, 33)
        vector.tensor_sub(out=w.u[0:1, :], in0=w.u[0:1, :],
                          in1=pq[2][0:1, 0:1024]).then_inc(sems.w_ve, 1)
        vector.wait_ge(sems.w_ve, 35)
        vector.tensor_scalar(out=w.u[0:1, :], in0=w.u[0:1, :],
                             scalar1=EPS, scalar2=None, op0=ALU.add).then_inc(sems.w_ve, 1)
        vector.wait_ge(sems.w_sc, 34)
        vector.reciprocal(out=w.u[0:1, :], in_=pq[2][0:1, 0:1024]).then_inc(sems.w_ve, 1)
        vector.wait_ge(sems.w_rw, 4 * DMA)
        for m in range(32):
            if m >= 2:
                vector.wait_ge([sems.w_s0, sems.w_s1][m % 2], (m // 2) * DMA)
            vector.tensor_sub(out=w.u[:], in0=w.tx[:, 1024 * m:1024 * (m + 1)],
                              in1=w.bb[:]).then_inc(sems.w_ve, 1)
            vector.wait_ge(sems.w_ve, 32 + 5 + 3 * m + 1)
            vector.tensor_mul(out=w.x2[m % 2][:], in0=w.u[:], in1=w.ab[:]).then_inc(sems.w_ve, 1)
            vector.wait_ge(sems.w_ve, 32 + 5 + 3 * m + 2)
            vector.tensor_copy(out=w.tx[:, 1024 * m:1024 * (m + 1)],
                               in_=w.x2[m % 2][:]).then_inc(sems.w_ve, 1)

    W_TOT_VE = 32 + 5 + 96

    f = NS()
    f.wring = [sbuf(f"f_wr{i}", CONST2 + i * 8192, [128, 8 * 512], BF16) for i in range(3)]
    f.hreg = sbuf("f_hreg", R3, [128, KB_DN * 512], BF16)
    f.xs = sbuf("f_xs", STAGE, [KB_GU, 512], BF16)
    f.hs = sbuf("f_hs", STAGE + 1024, [KB_DN, 512], BF16)
    f.nbg = [sbuf(f"f_nbg{i}", STAGE + 2048 + i * 1024, [KB_DN, 512], BF16) for i in range(2)]
    f.gsil = [sbuf(f"f_gsil{i}", STAGE + 4096 + i * 1024, [128, 512], BF16) for i in range(2)]
    f.ind = [sbuf(f"f_ind{i}", STAGE + 6144 + i * 256, [128, KB_DN], BF16) for i in range(2)]
    f.xm = [sbuf(f"f_xm{i}", STAGE + 6656 + i * 2048, [128, 512], F32) for i in range(2)]
    f.t2 = [sbuf(f"f_t2{i}", STAGE + 10752 + i * 2048, [128, 512], F32) for i in range(2)]
    f.t2bf = [sbuf(f"f_t2bf{i}", STAGE + 14848 + i * 1024, [128, 512], BF16) for i in range(2)]
    f.sq = [sbuf(f"f_sq{i}", STAGE + 16896 + i * 1024, [128, 512], BF16) for i in range(2)]
    f.rows2 = sbuf("f_rows2", STAGE + 18944, [128, 512], F32)
    f.ab2 = sbuf("f_ab2", STAGE, [128, 512], F32)
    f.bb2 = sbuf("f_bb2", STAGE + 2048, [128, 512], F32)
    f.t2r = [sbuf(f"f_t2r{i}", STAGE + 4096 + i * 2048, [128, 512], F32) for i in range(2)]
    f.ou = sbuf("f_ou", STAGE + 8704, [128, 512], F32)
    f.ob = [sbuf(f"f_ob{i}", STAGE + 10752 + i * 2048, [128, 512], F32) for i in range(2)]

    fq_sems = [sems.f_q0, sems.f_q1, sems.f_q2]
    fn_sems = [sems.f_n0, sems.f_n1]
    fm_sems = [sems.f_m0, sems.f_m1]
    ft_sems = [sems.f_t0, sems.f_t1]
    fr_sems = [sems.f_r0, sems.f_r1]
    fo_sems = [sems.f_o0, sems.f_o1]

    DN_KG = [8] * 10 + [6]
    F_FILLS_CH = 43 * 4 + 8 * 11
    def f_mm_after_gu_fill(mc, kq):
        return 32 + mc * 134 + 4 + 128 * 0 + (kq + 1) * 32
    def f_mm_after_dn_fill(mc8, kgi):
        base = 32 + 43 * 134
        return base + mc8 * 348 + 4 + 4 * (sum(DN_KG[:kgi]) + DN_KG[kgi])
    F_MM_CH = 32 + 43 * 134 + 8 * 348
    F_VE_XS = 1
    def f_ve_ind(mc, j2):
        return 1 + mc * 4 + j2 + 1
    def f_ve_h(mc, j2):
        return 1 + mc * 4 + 2 + j2 + 1
    F_VE_GU = 1 + 43 * 4
    F_VE_HS = F_VE_GU + 1
    def f_ve_t2(m):
        return F_VE_HS + 2 * m + 1
    def f_ve_t2bf(m):
        return F_VE_HS + 2 * m + 2
    F_VE_DN = F_VE_HS + 64
    F_VE_ROWS = F_VE_DN + 5
    def f_ve_ob(m):
        return F_VE_ROWS + 2 * m + 2
    F_VE_CH = F_VE_ROWS + 64
    F_SC_CH = 86 + 32 + 2
    F_S2_CH = 64

    def f_sync(sync, c):
        mmb = F_MM_CH * c
        veb = F_VE_CH * c
        fqb = F_FILLS_CH * c
        fill_done = []
        for mc in range(MC_GU):
            for kq in range(4):
                fill_done.append(mmb + 32 + mc * 134 + 4 + 32 * (kq + 1))
        for mc8 in range(MC_DN):
            k0 = 0
            for kg in DN_KG:
                k0 += kg
                fill_done.append(mmb + 32 + 43 * 134 + mc8 * 348 + 4 + 4 * k0)
        fq = fqb
        nfill_gu = [26 * c, 25 * c]
        for mc in range(MC_GU):
            if mc >= 2:
                sync.wait_ge(sems.f_mm, mmb + 32 + (mc - 2) * 134 + 4)
            elif c > 0:
                sync.wait_ge(sems.f_mm, mmb)
            sync.dma_start(out=f.nbg[mc % 2][0:KB_GU, :],
                           in_=t.negbg_gu[:, 512 * mc:512 * (mc + 1)]).then_inc(fn_sems[mc % 2], DMA)
            for kq in range(4):
                pf = fq - 3
                if pf >= 0 and pf >= fqb:
                    sync.wait_ge(sems.f_mm, fill_done[pf - fqb])
                elif pf >= 0:
                    sync.wait_ge(sems.f_mm, mmb)
                sync.dma_start(out=f.wring[fq % 3][:],
                               in_=t.wgu_c[mc, 8 * kq:8 * kq + 8].rearrange("k p n -> p k n")).then_inc(fq_sems[fq % 3], DMA)
                fq += 1
        for mc8 in range(MC_DN):
            if mc8 >= 2:
                sync.wait_ge(sems.f_mm, mmb + 32 + 43 * 134 + (mc8 - 2) * 348 + 4)
            else:
                sync.wait_ge(sems.f_mm, mmb + 32 + (42 - (mc8 % 2)) * 134 + 4)
            sync.dma_start(out=f.nbg[mc8 % 2][0:KB_DN, :],
                           in_=t.negbg_dn[:, 512 * mc8:512 * (mc8 + 1)]).then_inc(fn_sems[mc8 % 2], DMA)
            k0 = 0
            for kgi, kg in enumerate(DN_KG):
                pf = fq - 3
                if pf >= fqb:
                    sync.wait_ge(sems.f_mm, fill_done[pf - fqb])
                sync.dma_start(out=f.wring[fq % 3][:, 0:512 * kg],
                               in_=t.wdn_c[mc8, k0:k0 + kg].rearrange("k p n -> p k n")).then_inc(fq_sems[fq % 3], DMA)
                fq += 1
                k0 += kg
            for m2 in range(4):
                m = 4 * mc8 + m2
                if m >= 2:
                    sync.wait_ge(sems.f_ve, veb + f_ve_t2(m - 2))
                elif c > 0:
                    sync.wait_ge(sems.f_ve, veb)
                sync.dma_start(out=f.xm[m % 2][:],
                               in_=t.xmid[128 * m:128 * m + 128, 512 * c:512 * (c + 1)]).then_inc(fm_sems[m % 2], DMA)
            for m2 in range(4):
                m = 4 * mc8 + m2
                sync.wait_ge(sems.f_ve, veb + f_ve_t2(m))
                sync.dma_start(out=t.t2_sp[128 * m:128 * m + 128, :],
                               in_=f.t2[m % 2][:]).then_inc(ft_sems[m % 2], DMA)
        sync.wait_ge(sems.f_ve, veb + F_VE_ROWS)
        sync.dma_start(out=t.rows_sp[6, 0:512], in_=f.ou[0:1, :]).then_inc(sems.f_ab, DMA)
        sync.dma_start(out=t.rows_sp[7, 0:512], in_=f.rows2[0:1, :]).then_inc(sems.f_ab, DMA)
        sync.wait_ge(sems.f_ab, (4 * c + 2) * DMA)
        sync.dma_start(out=f.ab2[:], in_=t.rows_sp[6, 0:512].unsqueeze(0).partition_broadcast(128)).then_inc(sems.f_ab, DMA)
        sync.dma_start(out=f.bb2[:], in_=t.rows_sp[7, 0:512].unsqueeze(0).partition_broadcast(128)).then_inc(sems.f_ab, DMA)
        for m in range(32):
            if m >= 2:
                sync.wait_ge(sems.f_ve, veb + f_ve_ob(m - 2))
            sync.dma_start(out=f.t2r[m % 2][:],
                           in_=t.t2_sp[128 * m:128 * m + 128, :]).then_inc(fr_sems[m % 2], DMA)
            sync.wait_ge(sems.f_ve, veb + f_ve_ob(m) if False else veb + F_VE_ROWS + 2 * m + 2)
            sync.dma_start(out=t.outT[128 * m:128 * m + 128, 512 * c:512 * (c + 1)],
                           in_=f.ob[m % 2][:]).then_inc(fo_sems[m % 2], DMA)

    def f_tensor(tensor, c):
        mmb = F_MM_CH * c
        veb = F_VE_CH * c
        s2b = F_S2_CH * c
        fqb = F_FILLS_CH * c
        fq = fqb
        if c > 0:
            tensor.wait_ge(sems.f_ve, F_VE_CH * c)
        for k in range(32):
            tensor.matmul(out=pq[3][0:KB_GU, 0:512],
                          lhsT=s.ind_gu[:, 32 * k:32 * (k + 1)],
                          rhs=w.tx[:, 1024 * k + 512 * c:1024 * k + 512 * (c + 1)],
                          start=(k == 0), stop=(k == 31)).then_inc(sems.f_mm, 1)
        PQg = [pq[0], pq[1], pq[3]]
        pj = lambda j2, mc: PQg[(2 * mc + j2 // 2) % 3][:, 512 * (j2 % 2):512 * (j2 % 2) + 512]
        for mc in range(MC_GU):
            tensor.wait_ge(sems.f_ve, veb + F_VE_XS)
            tensor.wait_ge(fn_sems[mc % 2], ([26, 25][mc % 2] * c + mc // 2 + 1) * DMA)
            if mc >= 1:
                pprev = 2 * mc - 2
                tensor.wait_ge(sems.f_ve, veb + f_ve_h(pprev // 2, pprev % 2))
                tensor.wait_ge(sems.f_sc, F_SC_CH * c + max(2 * mc - 1, 0))
            for j2 in range(4):
                tensor.matmul(out=pj(j2, mc), lhsT=f.nbg[mc % 2][0:KB_GU, 128 * j2:128 * (j2 + 1)],
                              rhs=f.xs[:], start=True, stop=False).then_inc(sems.f_mm, 1)
            for kq in range(4):
                tensor.wait_ge(fq_sems[fq % 3], ((fq - fq % 3) // 3 + 1) * DMA)
                for k8 in range(8):
                    k = 8 * kq + k8
                    last = (k == 31)
                    for j2 in range(4):
                        tensor.matmul(out=pj(j2, mc),
                                      lhsT=f.wring[fq % 3][:, 512 * k8 + 128 * j2:512 * k8 + 128 * (j2 + 1)],
                                      rhs=w.tx[:, 1024 * k + 512 * c:1024 * k + 512 * (c + 1)],
                                      start=False, stop=last).then_inc(sems.f_mm, 1)
                fq += 1
            for j2 in range(2):
                jj = 2 * mc + j2
                tensor.wait_ge(sems.f_ve, veb + f_ve_h(mc, j2))
                tensor.matmul(out=pq[2][0:KB_DN, 0:512],
                              lhsT=f.ind[jj % 2][:],
                              rhs=f.hreg[:, 512 * jj:512 * (jj + 1)],
                              start=(jj == 0), stop=(jj == KB_DN - 1)).then_inc(sems.f_mm, 1)
        for mc8 in range(MC_DN):
            tensor.wait_ge(sems.f_ve, veb + F_VE_HS)
            tensor.wait_ge(sems.f_sc, F_SC_CH * c + 86)
            tensor.wait_ge(fn_sems[mc8 % 2], ([26, 25][mc8 % 2] * c + [22, 21][mc8 % 2] + mc8 // 2 + 1) * DMA)
            if mc8 >= 1:
                tensor.wait_ge(sems.f_ve, veb + f_ve_t2(4 * mc8 - 2))
            DSL = [pq[0][:, 0:512], pq[0][:, 512:1024], pq[1][:, 0:512],
                   pq[1][:, 512:1024], pq[2][:, 512:1024]]
            for j2 in range(4):
                m = 4 * mc8 + j2
                tensor.matmul(out=DSL[m % 5],
                              lhsT=f.nbg[mc8 % 2][0:KB_DN, 128 * j2:128 * (j2 + 1)],
                              rhs=f.hs[:], start=True, stop=False).then_inc(sems.f_mm, 1)
            k0 = 0
            for kgi, kg in enumerate(DN_KG):
                tensor.wait_ge(fq_sems[fq % 3], ((fq - fq % 3) // 3 + 1) * DMA)
                for kk in range(kg):
                    k = k0 + kk
                    last = (k == KB_DN - 1)
                    for j2 in range(4):
                        m = 4 * mc8 + j2
                        tensor.matmul(out=DSL[m % 5],
                                      lhsT=f.wring[fq % 3][:, 512 * kk + 128 * j2:512 * kk + 128 * (j2 + 1)],
                                      rhs=f.hreg[:, 512 * k:512 * (k + 1)],
                                      start=False, stop=last).then_inc(sems.f_mm, 1)
                fq += 1
                k0 += kg
            for m2 in range(4):
                m = 4 * mc8 + m2
                tensor.wait_ge(sems.f_ve, veb + f_ve_t2bf(m))
                tensor.wait_ge(sems.f_sc, F_SC_CH * c + 86 + m + 1)
                tensor.matmul(out=pq[3][0:1, 0:512], lhsT=s.ones[:],
                              rhs=f.t2bf[m % 2][:],
                              start=(m == 0), stop=(m == 31)).then_inc(sems.f_s2, 1)
                tensor.matmul(out=pq[3][0:1, 512:1024], lhsT=s.ones[:],
                              rhs=f.sq[m % 2][:],
                              start=(m == 0), stop=(m == 31)).then_inc(sems.f_s2, 1)

    def f_scalar(scalar, c):
        scb = F_SC_CH * c
        veb = F_VE_CH * c
        mmb = F_MM_CH * c
        PQg = [pq[0], pq[1], pq[3]]
        for mc in range(MC_GU):
            for j2 in range(2):
                jj = 2 * mc + j2
                gpos = mmb + 32 + mc * 134 + 4 + 128
                scalar.wait_ge(sems.f_mm, gpos)
                if jj >= 2:
                    scalar.wait_ge(sems.f_ve, veb + f_ve_h(*divmod(jj - 2, 2)))
                scalar.activation(out=f.gsil[jj % 2][:],
                                  in_=PQg[jj % 3][:, 0:512],
                                  func=AF.Silu).then_inc(sems.f_sc, 1)
        for m in range(32):
            scalar.wait_ge(sems.f_ve, veb + f_ve_t2bf(m))
            if m >= 2:
                scalar.wait_ge(sems.f_s2, F_S2_CH * c + 2 * (m - 1))
            scalar.activation(out=f.sq[m % 2][:], in_=f.t2bf[m % 2][:],
                              func=AF.Square).then_inc(sems.f_sc, 1)
        scalar.wait_ge(sems.f_ve, veb + F_VE_DN + 2)
        scalar.activation(out=pq[3][0:1, 0:512], in_=f.rows2[0:1, :],
                          func=AF.Square).then_inc(sems.f_sc, 1)
        scalar.wait_ge(sems.f_ve, veb + F_VE_DN + 4)
        scalar.activation(out=pq[3][0:1, 0:512], in_=f.ou[0:1, :],
                          func=AF.Sqrt).then_inc(sems.f_sc, 1)

    def f_vector(vector, c):
        veb = F_VE_CH * c
        mmb = F_MM_CH * c
        scb = F_SC_CH * c
        PQg2 = [pq[0], pq[1], pq[3]]
        DSL2 = [pq[0][:, 0:512], pq[0][:, 512:1024], pq[1][:, 0:512],
                pq[1][:, 512:1024], pq[2][:, 512:1024]]
        vector.wait_ge(sems.f_mm, mmb + 32)
        vector.tensor_copy(out=f.xs[:], in_=pq[3][0:KB_GU, 0:512]).then_inc(sems.f_ve, 1)
        for mc in range(MC_GU):
            for j2 in range(2):
                jj = 2 * mc + j2
                if jj >= 2:
                    vector.wait_ge(sems.f_mm, mmb + 32 + (mc - 1) * 134 + 133 + j2)
                vector.tensor_scalar(out=f.ind[jj % 2][:], in0=s.iota_row[:, 0:KB_DN],
                                     scalar1=s.gcolf_dn[:, jj:jj + 1], scalar2=None,
                                     op0=ALU.is_equal).then_inc(sems.f_ve, 1)
            for j2 in range(2):
                jj = 2 * mc + j2
                vector.wait_ge(sems.f_sc, scb + 2 * mc + j2 + 1)
                vector.tensor_tensor(out=f.hreg[:, 512 * jj:512 * (jj + 1)],
                                     in0=f.gsil[jj % 2][:],
                                     in1=PQg2[jj % 3][:, 512:1024],
                                     op=ALU.mult).then_inc(sems.f_ve, 1)
        vector.wait_ge(sems.f_mm, mmb + 32 + 43 * 134)
        vector.tensor_copy(out=f.hs[:], in_=pq[2][0:KB_DN, 0:512]).then_inc(sems.f_ve, 1)
        for mc8 in range(MC_DN):
            for m2 in range(4):
                m = 4 * mc8 + m2
                vector.wait_ge(sems.f_mm, mmb + 32 + 43 * 134 + mc8 * 348 + 4 + 4 * KB_DN)
                vector.wait_ge(fm_sems[m % 2], (16 * c + m // 2 + 1) * DMA)
                if m >= 2:
                    vector.wait_ge(ft_sems[m % 2], (16 * c + (m - 2) // 2 + 1) * DMA)
                elif c > 0:
                    vector.wait_ge(fo_sems[m % 2], 16 * c * DMA)
                vector.tensor_add(out=f.t2[m % 2][:], in0=DSL2[m % 5],
                                  in1=f.xm[m % 2][:]).then_inc(sems.f_ve, 1)
                vector.wait_ge(sems.f_ve, veb + f_ve_t2(m))
                if m >= 2:
                    vector.wait_ge(sems.f_sc, F_SC_CH * c + 85 + m)
                    vector.wait_ge(sems.f_s2, F_S2_CH * c + 2 * (m - 1))
                vector.tensor_copy(out=f.t2bf[m % 2][:], in_=f.t2[m % 2][:]).then_inc(sems.f_ve, 1)
        vector.wait_ge(sems.f_s2, F_S2_CH * (c + 1))
        vector.tensor_scalar(out=f.rows2[0:1, :], in0=pq[3][0:1, 0:512],
                             scalar1=1.0 / D, scalar2=None, op0=ALU.mult).then_inc(sems.f_ve, 1)
        vector.tensor_scalar(out=f.ou[0:1, :], in0=pq[3][0:1, 512:1024],
                             scalar1=1.0 / D, scalar2=None, op0=ALU.mult).then_inc(sems.f_ve, 1)
        vector.wait_ge(sems.f_sc, F_SC_CH * c + 119)
        vector.tensor_sub(out=f.ou[0:1, :], in0=f.ou[0:1, :],
                          in1=pq[3][0:1, 0:512]).then_inc(sems.f_ve, 1)
        vector.wait_ge(sems.f_ve, veb + F_VE_DN + 3)
        vector.tensor_scalar(out=f.ou[0:1, :], in0=f.ou[0:1, :],
                             scalar1=EPS, scalar2=None, op0=ALU.add).then_inc(sems.f_ve, 1)
        vector.wait_ge(sems.f_sc, F_SC_CH * (c + 1))
        vector.reciprocal(out=f.ou[0:1, :], in_=pq[3][0:1, 0:512]).then_inc(sems.f_ve, 1)
        vector.wait_ge(sems.f_ab, (4 * c + 4) * DMA)
        for m in range(32):
            vector.wait_ge(fr_sems[m % 2], (16 * c + m // 2 + 1) * DMA)
            if m >= 2:
                vector.wait_ge(fo_sems[m % 2], (16 * c + (m - 2) // 2 + 1) * DMA)
            vector.tensor_sub(out=f.ou[:], in0=f.t2r[m % 2][:],
                              in1=f.bb2[:]).then_inc(sems.f_ve, 1)
            vector.wait_ge(sems.f_ve, veb + F_VE_ROWS + 2 * m + 1)
            if m < 2:
                vector.wait_ge(ft_sems[m % 2], 16 * (c + 1) * DMA)
            vector.tensor_mul(out=f.ob[m % 2][:], in0=f.ou[:],
                              in1=f.ab2[:]).then_inc(sems.f_ve, 1)

    with nc.Block() as block:
        @block.sync
        def _(sync):
            nbar = 0
            if "D" in phases:
                d_sync(sync)
                sync.wait_ge(sems.d_st, d_tot_st * DMA)
                sync.wait_ge(sems.d_ld, d_tot_ld * DMA)
                for j, dwj in enumerate([sems.d_w0, sems.d_w1, sems.d_w2, sems.d_w3]):
                    sync.wait_ge(dwj, (d_tot_tt // 4) * DMA)
                if debug:
                    sync.dma_start(out=t.gbf_dump[:], in_=d.gbf_dn[:]).then_inc(sems.d_l0, DMA)
                    sync.dma_start(out=t.indT_dump[:], in_=d.indT[:]).then_inc(sems.d_l0, DMA)
                    sync.dma_start(out=t.indgu_dump[:], in_=s.ind_gu[:]).then_inc(sems.d_l0, DMA)
                    sync.wait_ge(sems.d_l0, (7 + 3) * DMA)
            nbar += 1
            sync.sem_inc(sems.bar, 1)
            if "P" in phases:
                sync.wait_ge(sems.bar, nbar)
                p_sync(sync)
                if debug:
                    sync.dma_start(out=t.qt_dump[:], in_=p.qt[:]).then_inc(sems.p_rp, DMA)
                    sync.wait_ge(sems.p_rp, DMA)
            nbar += 1
            sync.sem_inc(sems.bar, 1)
            if "A" in phases:
                sync.wait_ge(sems.bar, nbar)
                a_sync(sync)
                sync.wait_ge(sems.a_ve, A_TOT_VE)
                if debug:
                    sync.dma_start(out=t.attn_dump[:], in_=a.attn[:]).then_inc(sems.p_rp, DMA)
                    sync.wait_ge(sems.p_rp, 2 * DMA)
            nbar += 1
            sync.sem_inc(sems.bar, 1)
            if "W" in phases:
                sync.wait_ge(sems.bar, nbar)
                w_sync(sync)
                sync.wait_ge(sems.w_ve, W_TOT_VE)
                sync.wait_ge(sems.w_s0, 16 * DMA)
                sync.wait_ge(sems.w_s1, 16 * DMA)
            nbar += 1
            sync.sem_inc(sems.bar, 1)
            if "F" in phases:
                sync.wait_ge(sems.bar, nbar)
                for cch in range(2):
                    f_sync(sync, cch)
                sync.wait_ge(sems.f_o0, 32 * DMA)
                sync.wait_ge(sems.f_o1, 32 * DMA)

        @block.vector
        def _(vector):
            if "D" in phases:
                d_vector(vector)
            if "P" in phases:
                vector.wait_ge(sems.bar, 1)
                p_vector(vector)
            if "A" in phases:
                vector.wait_ge(sems.bar, 2)
                a_vector(vector)
            if "W" in phases:
                vector.wait_ge(sems.bar, 3)
                w_vector(vector)
            if "F" in phases:
                vector.wait_ge(sems.bar, 4)
                for cch in range(2):
                    f_vector(vector, cch)

        @block.tensor
        def _(tensor):
            if "D" in phases:
                d_tensor(tensor)
            if "P" in phases:
                tensor.wait_ge(sems.bar, 1)
                p_tensor(tensor)
            if "A" in phases:
                tensor.wait_ge(sems.bar, 2)
                a_tensor(tensor)
            if "W" in phases:
                tensor.wait_ge(sems.bar, 3)
                w_tensor(tensor)
            if "F" in phases:
                tensor.wait_ge(sems.bar, 4)
                for cch in range(2):
                    f_tensor(tensor, cch)

        @block.scalar
        def _(scalar):
            if "D" in phases:
                d_scalar(scalar)
            if "P" in phases:
                scalar.wait_ge(sems.bar, 1)
                p_scalar(scalar)
            if "A" in phases:
                scalar.wait_ge(sems.bar, 2)
                a_scalar(scalar)
            if "W" in phases:
                scalar.wait_ge(sems.bar, 3)
                w_scalar(scalar)
            if "F" in phases:
                scalar.wait_ge(sems.bar, 4)
                for cch in range(2):
                    f_scalar(scalar, cch)

        @block.gpsimd
        def _(gpsimd):
            if "P" in phases:
                gpsimd.wait_ge(sems.bar, 1)
                p_gpsimd(gpsimd)


    ctx.close()
    return nc


def prep_inputs(L0_attn_x, L0_attn_wq, L0_attn_wk, L0_attn_wv, L0_attn_wo,
                L0_ffn_qweight_gu, L0_ffn_qzeros_gu, L0_ffn_scales_gu, L0_ffn_g_idx_gu,
                L0_ffn_qweight_down, L0_ffn_qzeros_down, L0_ffn_scales_down, L0_ffn_g_idx_down):
    ph = full_perm(D)
    pi = full_perm(INTER)
    rp = rope_perm()
    blocks = []
    for j in range(KB_DN):
        blocks.append(pi[128 * j:128 * (j + 1)])
        blocks.append(INTER + pi[128 * j:128 * (j + 1)])
    pgu = np.concatenate(blocks)
    wp = []
    for j in range(KB_DN):
        wp.append(np.arange(16 * j, 16 * (j + 1)))
        wp.append(1376 + np.arange(16 * j, 16 * (j + 1)))
    word_perm = np.concatenate(wp)

    wq = np.ascontiguousarray(np.asarray(L0_attn_wq)[ph][:, rp])
    wk = np.ascontiguousarray(np.asarray(L0_attn_wk)[ph][:, rp])
    wv = np.ascontiguousarray(np.asarray(L0_attn_wv)[ph])
    wo = np.ascontiguousarray(np.asarray(L0_attn_wo)[:, ph])
    qw_gu = np.ascontiguousarray(np.asarray(L0_ffn_qweight_gu)[:, pgu])
    sc_gu = np.ascontiguousarray(np.asarray(L0_ffn_scales_gu)[:, pgu])
    gi_gu = np.ascontiguousarray(np.asarray(L0_ffn_g_idx_gu)[ph]).astype(np.int32)
    qw_dn = np.ascontiguousarray(np.asarray(L0_ffn_qweight_down)[:, ph])
    sc_dn = np.ascontiguousarray(np.asarray(L0_ffn_scales_down)[:, ph])
    gi_dn = np.ascontiguousarray(np.asarray(L0_ffn_g_idx_down)[pi]).astype(np.int32)
    qz_gu = np.ascontiguousarray(np.asarray(L0_ffn_qzeros_gu)[:, word_perm])
    qz_dn = np.ascontiguousarray(np.asarray(L0_ffn_qzeros_down))

    inv = 1.0 / (10000.0 ** (np.arange(0, HD, 2)[: HD // 2].astype(np.float32) / HD))
    tt = np.arange(S)
    f = np.outer(tt, inv).astype(np.float32)
    cos_k = np.ascontiguousarray(np.cos(f).T)
    sin_k = np.ascontiguousarray(np.sin(f).T)

    common = dict(
        wq=wq, wk=wk, wv=wv, wo=wo,
        qw_gu=qw_gu, qz_gu=qz_gu, sc_gu=sc_gu,
        qw_dn=qw_dn, qz_dn=qz_dn, sc_dn=sc_dn,
        gi_bc_gu=np.ascontiguousarray(np.broadcast_to(gi_gu[None, :], (KB_GU, D))),
        gi_col_gu=np.ascontiguousarray(gi_gu.reshape(KB_GU, 128).T),
        gi_bc_dn=np.ascontiguousarray(np.broadcast_to(gi_dn[None, :], (KB_DN, INTER))),
        gi_col_dn=np.ascontiguousarray(gi_dn.reshape(KB_DN, 128).T),
        cos_k=cos_k, sin_k=sin_k,
        ones_col=np.ones((128, 1), BF),
        iota_row=np.ascontiguousarray(np.broadcast_to(np.arange(KB_DN, dtype=np.float32), (128, KB_DN))),
        iota_col=np.arange(KB_DN, dtype=np.float32).reshape(KB_DN, 1),
        mask_col=np.ascontiguousarray(
            np.left_shift(np.uint32(15), 4 * (np.arange(128, dtype=np.uint32) // 16)).view(np.int32).reshape(128, 1)),
        pscale_col=(2.0 ** (-4.0 * (np.arange(128) // 16))).astype(np.float32).reshape(128, 1),
    )

    x = np.asarray(L0_attn_x)
    in_maps = []
    for c in range(N_CORES):
        b, half = c // 2, c % 2
        q0 = half * TOKQ
        xTb = np.ascontiguousarray(x[b].T[ph])
        m = dict(common)
        m["xT"] = xTb
        m["x_own"] = np.ascontiguousarray(xTb[:, q0:q0 + TOKQ])
        m["cos_q"] = np.ascontiguousarray(cos_k[:, q0:q0 + TOKQ])
        m["sin_q"] = np.ascontiguousarray(sin_k[:, q0:q0 + TOKQ])
        kv = (np.arange(16)[None, :, None] * 128 + np.arange(128)[:, None, None])
        qq = (q0 + np.arange(TOKQ))[None, None, :]
        mk = (kv <= qq).astype(np.uint8)
        m["mask"] = np.ascontiguousarray(mk.reshape(128, 16 * TOKQ))
        in_maps.append(m)
    return in_maps


def kernel(**inputs):
    in_maps = prep_inputs(**inputs)
    nc = build_program()
    res = run_bass_kernel_spmd(nc, in_maps, core_ids=list(range(N_CORES)))
    inv_ph = np.argsort(full_perm(D))
    out = np.empty((B, S, D), np.float32)
    for c in range(N_CORES):
        b, half = c // 2, c % 2
        out[b, half * TOKQ:(half + 1) * TOKQ, :] = res.results[c]["outT"][inv_ph].T
    return out
